# version 14
# baseline (speedup 1.0000x reference)
import sys
if "/opt/trn_rl_repo" not in sys.path:
    sys.path.insert(0, "/opt/trn_rl_repo")
from contextlib import ExitStack

import numpy as np
import ml_dtypes

import concourse.bass as bass
import concourse.mybir as mybir
from concourse.bass_utils import run_bass_kernel_spmd

F32 = mybir.dt.float32
BF16 = mybir.dt.bfloat16
AF = mybir.ActivationFunctionType
ALU = mybir.AluOpType

P = 128
T = 2048
NT = 4
TT = 512
NC16 = 16
D = 128
EPS = 1e-8
LNEPS = 1e-5
INTER = 5632
NI44 = 44
FFN_GROUPS = [(0, 15), (15, 15), (30, 14)]

TAPS = set()

ENGINES = ("sync", "tensor", "vector", "scalar", "gpsimd")


class Sem:
    def __init__(self, es, nc, name):
        self.h = es.enter_context(nc.semaphore(name))
        self.n = 0


def build(taps=()):
    taps = set(taps)
    nc = bass.Bass("TRN2", target_bir_lowering=False)

    d_x = nc.dram_tensor("x_l", [P, NC16, T], F32, kind="ExternalInput").ap()
    d_wq = nc.dram_tensor("wqt_l", [P, NC16, 2048], F32, kind="ExternalInput").ap()
    d_wk = nc.dram_tensor("wkt_l", [P, NC16, 512], F32, kind="ExternalInput").ap()
    d_wv = nc.dram_tensor("wvt_l", [P, NC16, 512], F32, kind="ExternalInput").ap()
    d_wg = nc.dram_tensor("wgt_l", [P, NC16, 2048], F32, kind="ExternalInput").ap()
    d_wmb = nc.dram_tensor("wmbt_l", [P, NC16, 2048], F32, kind="ExternalInput").ap()
    d_bq = nc.dram_tensor("bq_c", [P, 16], F32, kind="ExternalInput").ap()
    d_bk = nc.dram_tensor("bk_c", [P, 4], F32, kind="ExternalInput").ap()
    d_bv = nc.dram_tensor("bv_c", [P, 4], F32, kind="ExternalInput").ap()
    d_bg = nc.dram_tensor("bg_c", [P, 16], F32, kind="ExternalInput").ap()
    d_bmb = nc.dram_tensor("bmb_r", [1, 2048], F32, kind="ExternalInput").ap()
    d_mem = nc.dram_tensor("mem_l", [4, 6, P, 512], F32, kind="ExternalInput").ap()
    d_nrm = nc.dram_tensor("nrm_l", [P, 96], F32, kind="ExternalInput").ap()
    d_wgate = nc.dram_tensor("wgatet_l", [P, NC16, INTER], F32, kind="ExternalInput").ap()
    d_wup = nc.dram_tensor("wupt_l", [P, NC16, INTER], F32, kind="ExternalInput").ap()
    d_wdown = nc.dram_tensor("wdownt_l", [P, NI44, 2048], F32, kind="ExternalInput").ap()
    d_gs = nc.dram_tensor("gs_c", [P, NI44], F32, kind="ExternalInput").ap()
    d_us = nc.dram_tensor("us_c", [P, NI44], F32, kind="ExternalInput").ap()
    d_ds = nc.dram_tensor("ds_c", [P, 16], F32, kind="ExternalInput").ap()
    d_wmod = nc.dram_tensor("wmod_sh_l", [P, NC16, 1536], F32, kind="ExternalInput").ap()
    d_c = nc.dram_tensor("c_c", [P, 16], F32, kind="ExternalInput").ap()
    d_identb = nc.dram_tensor("ident_b", [P, P], BF16, kind="ExternalInput").ap()
    d_identf = nc.dram_tensor("ident_f", [P, P], F32, kind="ExternalInput").ap()
    d_onescol = nc.dram_tensor("ones_col_b", [P, 1], BF16, kind="ExternalInput").ap()
    d_onesrow = nc.dram_tensor("ones_row_b", [1, P], BF16, kind="ExternalInput").ap()
    d_onesrowf = nc.dram_tensor("ones_row_f", [1, P], F32, kind="ExternalInput").ap()

    d_out = nc.dram_tensor("out_l", [P, NC16, T], F32, kind="ExternalOutput").ap()

    d_xnew = nc.dram_tensor("xnew_d", [P, NC16, T], F32).ap()
    d_y = nc.dram_tensor("y_d", [P, NC16, T], F32).ap()
    d_arin = nc.dram_tensor("ar_in", [3096, 512], F32).ap()
    d_arout = nc.dram_tensor("ar_out", [3096, 512], F32).ap()
    d_magi = nc.dram_tensor("modag_in", [12, P], F32).ap()
    d_mago = nc.dram_tensor("modag_out", [96, P], F32).ap()

    d_tap = {}
    if "h" in taps:
        d_tap["h"] = nc.dram_tensor("tap_h", [P, NC16, T], F32, kind="ExternalOutput").ap()
    if "xnew" in taps:
        d_tap["xnew"] = nc.dram_tensor("tap_xnew", [P, NC16, T], F32, kind="ExternalOutput").ap()
    if "h2" in taps:
        d_tap["h2"] = nc.dram_tensor("tap_h2", [P, NC16, T], F32, kind="ExternalOutput").ap()
    if "newmem" in taps:
        d_tap["newmem"] = nc.dram_tensor("tap_newmem", [24, P, 512], F32, kind="ExternalOutput").ap()
        d_tap["newnrm"] = nc.dram_tensor("tap_newnrm", [P, 96], F32, kind="ExternalOutput").ap()
    if "kdbg" in taps:
        d_tap["kdbg"] = nc.dram_tensor("tap_kdbg", [P, 2048], F32, kind="ExternalOutput").ap()

    prog = {e: [] for e in ENGINES}

    with ExitStack() as es:
        def sb(name, shape, dt):
            return es.enter_context(nc.sbuf_tensor(name, shape, dt))

        h_b = sb("h_b", [P, NC16, T], BF16)
        memb = sb("memb", [P, 24, 512], BF16)
        wstg = sb("wstg", [P, 2048], F32)
        AR = sb("AR", [P, 23552], F32)
        fsc = sb("fsc", [P, 1536], F32)
        cols = sb("cols", [P, 288], F32)
        kcols = sb("kcols", [P, 256], F32)
        colsb = sb("colsb", [P, 192], BF16)
        kcolsb = sb("kcolsb", [P, 64], BF16)
        dnacc = sb("dnacc", [P, 96], F32)
        nrmf = sb("nrmf", [P, 96], F32)
        browb = sb("browb", [1, 512], BF16)
        identB = sb("identB", [P, P], BF16)
        identF = sb("identF", [P, P], F32)
        ones_col = sb("ones_col", [P, 1], BF16)
        ones_row = sb("ones_row", [1, P], BF16)
        ones_rowf = sb("ones_rowf", [1, P], F32)
        magr = sb("magr", [96, P], F32)
        csil = sb("csil", [P, 16], F32)
        csilb = sb("csilb", [P, 16], BF16)
        modsh = sb("modsh", [P, 12], F32)
        modrow = sb("modrow", [12, P], F32)
        gtmp = sb("gtmp", [P, 48], F32)

        ARf = AR[:, :]
        ARb = AR[:, :].bitcast(BF16)

        def vf(off, *dims):
            n = int(np.prod(dims))
            v = ARf[:, off:off + n]
            if len(dims) > 1:
                kw = {f"d{i}": d for i, d in enumerate(dims[:-1])}
                pat = "p (" + " ".join(f"d{i}" for i in range(len(dims))) + ") -> p " + \
                    " ".join(f"d{i}" for i in range(len(dims)))
                v = v.rearrange(pat, **kw)
            return v

        def vb(off, *dims):
            n = int(np.prod(dims))
            v = ARb[:, off:off + n]
            if len(dims) > 1:
                kw = {f"d{i}": d for i, d in enumerate(dims[:-1])}
                pat = "p (" + " ".join(f"d{i}" for i in range(len(dims))) + ") -> p " + \
                    " ".join(f"d{i}" for i in range(len(dims)))
                v = v.rearrange(pat, **kw)
            return v

        CO_SA, CO_ONECA, CO_GA = 0, 16, 32
        CO_SB, CO_ONECB, CO_GB = 48, 64, 80
        CO_BQ, CO_BK, CO_BG = 96, 112, 116
        CO_NBK, CO_NBQ, CO_BV = 132, 136, 152
        CO_GS, CO_US, CO_GBDS, CO_DS = 160, 204, 248, 264
        KC_SS, KC_RINV, KC_DN, KC_DEN, KC_NS1, KC_T1, KC_T2 = 0, 16, 32, 96, 160, 224, 240

        ps = [es.enter_context(nc.psum_tensor(f"ps{i}", [P, 512], F32)) for i in range(6)]
        psN = es.enter_context(nc.psum_tensor("psN", [P, 64], F32))
        psT = es.enter_context(nc.psum_tensor("psT", [P, 1024], BF16))

        s_ld = Sem(es, nc, "s_ld")
        s_w = Sem(es, nc, "s_w")
        s_x = Sem(es, nc, "s_x")
        s_v = Sem(es, nc, "s_v")
        s_sc = Sem(es, nc, "s_sc")
        s_mm = Sem(es, nc, "s_mm")
        s_gp = Sem(es, nc, "s_gp")
        s_cc = Sem(es, nc, "s_cc")
        s_st = Sem(es, nc, "s_st")
        s_std = Sem(es, nc, "s_std")
        s_stx = Sem(es, nc, "s_stx")
        s_sty = Sem(es, nc, "s_sty")
        s_stg = Sem(es, nc, "s_stg")
        s_x2 = Sem(es, nc, "s_x2")
        s_nb = Sem(es, nc, "s_nb")
        s_mb = Sem(es, nc, "s_mb")
        s_mg = Sem(es, nc, "s_mg")
        s_br = Sem(es, nc, "s_br")
        s_wk = Sem(es, nc, "s_wk")
        s_wmb = Sem(es, nc, "s_wmb")
        s_wq = Sem(es, nc, "s_wq")
        s_wg = Sem(es, nc, "s_wg")
        s_wga2 = [Sem(es, nc, "s_wga0"), Sem(es, nc, "s_wga1")]
        s_wup2 = [Sem(es, nc, "s_wup0"), Sem(es, nc, "s_wup1")]
        s_wdn2 = [Sem(es, nc, "s_wdn0"), Sem(es, nc, "s_wdn1")]
        s_yi2 = [Sem(es, nc, "s_yi0"), Sem(es, nc, "s_yi1")]
        s_xi2 = [Sem(es, nc, "s_xi0"), Sem(es, nc, "s_xi1")]
        s_rb = Sem(es, nc, "s_rb")
        s_qrA = Sem(es, nc, "s_qrA")
        s_qrB = Sem(es, nc, "s_qrB")
        s_xq2 = [Sem(es, nc, "s_xqA"), Sem(es, nc, "s_xqB")]
        s_qd = Sem(es, nc, "s_qd")
        s_qss = Sem(es, nc, "s_qss")
        s_qg = Sem(es, nc, "s_qg")
        s_bar = Sem(es, nc, "s_bar")

        def emit(eng, fn):
            prog[eng].append(fn)

        def _waits(e, waits):
            for (s, v) in waits:
                if v > 0:
                    e.wait_ge(s.h, v)

        def barrier():
            snap = tuple((s, s.n) for s in
                         (s_v, s_sc, s_mm, s_gp, s_ld, s_w, s_x, s_x2,
                          s_st, s_std, s_stx, s_sty, s_stg, s_cc))
            for eng in ENGINES:
                def f(e, w=snap):
                    _waits(e, w)
                emit(eng, f)

        def dma(eng, out, in_, sem, waits=(), accum=False):
            waits = tuple(waits)
            def f(e, o=out, i=in_, s=sem, w=waits, a=accum):
                _waits(e, w)
                if a:
                    return e.dma_start(out=o, in_=i, accum_op=ALU.add).then_inc(s.h, 16)
                return e.dma_start(out=o, in_=i).then_inc(s.h, 16)
            emit(eng, f)
            sem.n += 16
            return sem.n

        def mm(out, lhsT, rhs, start, stop, waits=(), inc=None):
            waits = tuple(waits)
            if inc is not None:
                inc.n += 1
            def f(e, o=out, l=lhsT, r=rhs, st=start, sp=stop, w=waits, s=inc):
                _waits(e, w)
                i = e.matmul(o, l, r, start=st, stop=sp)
                if s is not None:
                    i.then_inc(s.h, 1)
                return i
            emit("tensor", f)
            return inc.n if inc is not None else None

        def tmark(waits=()):
            assert not waits
            s_mm.n += 1
            last = prog["tensor"].pop()
            def f(e, g=last, t=s_mm.n):
                i = g(e)
                i.then_inc(s_mm.h, 1)
                return i
            emit("tensor", f)
            return s_mm.n

        def tr(out, in_, ident, waits=(), inc=None):
            waits = tuple(waits)
            if inc is not None:
                inc.n += 1
            def f(e, o=out, i=in_, idn=ident, w=waits, s=inc):
                _waits(e, w)
                ins = e.transpose(o, i, idn)
                if s is not None:
                    ins.then_inc(s.h, 1)
                return ins
            emit("tensor", f)
            return inc.n if inc is not None else None

        _eng_sem = {"vector": None, "scalar": None, "gpsimd": None}

        def vop(eng, meth, *args, waits=(), inc=None, kwargs=None):
            own = {"vector": s_v, "scalar": s_sc, "gpsimd": s_gp}.get(eng)
            if own is not None and own.n > 0:
                waits = tuple(waits) + ((own, own.n),)
            waits = tuple(waits)
            if inc is not None:
                inc.n += 1
            kw = dict(kwargs or {})
            def f(e, m=meth, a=args, w=waits, s=inc, kk=kw):
                _waits(e, w)
                i = getattr(e, m)(*a, **kk)
                if s is not None:
                    i.then_inc(s.h, 1)
                return i
            emit(eng, f)
            return inc.n if inc is not None else None

        bank_free = {i: (s_mm, 0) for i in range(8)}

        def col(t, i):
            return t[:, i:i + 1]

        def bcast(row, out, pbank, waits=()):
            n = out.shape[-1]
            mm(ps[pbank][:, 0:n], ones_rowf[:, :], row, start=True, stop=True,
               waits=list(waits) + [bank_free[pbank], (s_ld, s_ld.n)])
            t = tmark()
            r = vop("vector", "tensor_copy", out, ps[pbank][:, 0:n],
                    waits=[(s_mm, t)], inc=s_v)
            bank_free[pbank] = (s_v, r)
            return r

        dma("sync", identB[:, :], d_identb[:, :], s_ld)
        dma("sync", identF[:, :], d_identf[:, :], s_ld)
        dma("sync", ones_col[:, :], d_onescol[:, :], s_ld)
        dma("sync", ones_row[:, :], d_onesrow[:, :], s_ld)
        dma("sync", ones_rowf[:, :], d_onesrowf[:, :], s_ld)
        dma("sync", cols[:, CO_BQ:CO_BQ + 16], d_bq[:, :], s_ld)
        dma("sync", cols[:, CO_BK:CO_BK + 4], d_bk[:, :], s_ld)
        dma("sync", cols[:, CO_BV:CO_BV + 4], d_bv[:, :], s_ld)
        dma("sync", cols[:, CO_BG:CO_BG + 16], d_bg[:, :], s_ld)
        dma("sync", cols[:, CO_GS:CO_GS + 44], d_gs[:, :], s_ld)
        dma("sync", cols[:, CO_US:CO_US + 44], d_us[:, :], s_ld)
        dma("sync", cols[:, CO_DS:CO_DS + 16], d_ds[:, :], s_ld)
        dma("sync", nrmf[:, :], d_nrm[:, :], s_ld)
        n_colsb_ld = dma("gpsimd", colsb[:, 0:96], d_nrm[:, :], s_nb)
        n_ldc = dma("sync", csil[:, :], d_c[:, :], s_ld)
        n_ld0 = n_ldc

        n_silc = vop("scalar", "activation", csil[:, :], csil[:, :], AF.Silu,
                     waits=[(s_ld, n_ldc)], inc=s_sc)
        n_csb = vop("vector", "tensor_copy", csilb[:, :], csil[:, :],
                    waits=[(s_sc, n_silc)], inc=s_v)
        vop("vector", "tensor_scalar_mul", cols[:, CO_NBK:CO_NBK + 4],
            cols[:, CO_BK:CO_BK + 4], -1.0, waits=[(s_ld, n_ld0)], inc=s_v)
        vop("vector", "tensor_scalar_mul", cols[:, CO_NBQ:CO_NBQ + 16],
            cols[:, CO_BQ:CO_BQ + 16], -1.0, inc=s_v)

        n_memcast = 0
        for idx in range(24):
            j, c6 = idx // 6, idx % 6
            n_memcast = dma("gpsimd", memb[:, idx, :], d_mem[j, c6, :, :], s_mb)

        wm_f = vf(20480, 16, 128)
        n_mmprev = 0
        for oc in range(12):
            n_wm = dma("sync", wm_f[:, :, :], d_wmod[:, :, oc * 128:(oc + 1) * 128],
                       s_w, waits=[(s_mm, n_mmprev)])
            for fc in range(NC16):
                mm(psN[:, oc:oc + 1], wm_f[:, fc, :], csil[:, fc:fc + 1],
                   start=(fc == 0), stop=(fc == NC16 - 1),
                   waits=[(s_w, n_wm), (s_sc, n_silc)] if fc == 0 else ())
            n_mmprev = tmark()
        n_modev = vop("vector", "tensor_copy", modsh[:, :], psN[:, 0:12],
                      waits=[(s_mm, n_mmprev)], inc=s_v)
        bank_free[6] = (s_v, n_modev)
        tr(ps[0][0:12, 0:128], modsh[:, :], identF[:, :],
           waits=[(s_v, n_modev), (s_ld, n_ld0)])
        n_modtr = tmark()
        n_modrow = vop("vector", "tensor_copy", modrow[:, :], ps[0][0:12, 0:128],
                       waits=[(s_mm, n_modtr)], inc=s_v)
        bank_free[0] = (s_v, n_modrow)
        n_magst = dma("gpsimd", d_magi[:, :], modrow[:, :], s_stg,
                      waits=[(s_v, n_modrow)])

        def f_ag(e, w=n_magst):
            e.wait_ge(s_stg.h, w)
            e.collective_compute(
                "AllGather", ALU.bypass,
                replica_groups=[list(range(8))],
                ins=[d_magi[:, :]],
                outs=[d_mago[:, :]],
            ).then_inc(s_cc.h, 1)
        emit("gpsimd", f_ag)
        s_cc.n += 1
        n_magr = dma("gpsimd", magr[:, :], d_mago[:, :], s_mg, waits=[(s_cc, s_cc.n)])
        tr(ps[0][0:128, 0:96], magr[:, :], identF[0:96, 0:96],
           waits=[(s_mg, n_magr), bank_free[0]])
        n_modtr2 = tmark()
        n_modcols = vop("vector", "tensor_copy", cols[:, 0:96], ps[0][0:128, 0:96],
                        waits=[(s_mm, n_modtr2)], inc=s_v)
        bank_free[0] = (s_v, n_modcols)

        vop("vector", "tensor_scalar_add", cols[:, CO_ONECA:CO_ONECA + 16],
            cols[:, CO_ONECA:CO_ONECA + 16], 1.0, waits=[(s_v, n_modcols)], inc=s_v)
        vop("vector", "tensor_scalar_add", cols[:, CO_ONECB:CO_ONECB + 16],
            cols[:, CO_ONECB:CO_ONECB + 16], 1.0, inc=s_v)
        for co in (CO_GA, CO_GB):
            nab = vop("scalar", "activation", gtmp[:, 0:16], cols[:, co:co + 16],
                      AF.Abs, waits=[(s_v, s_v.n)], inc=s_sc)
            vop("vector", "tensor_scalar_max", gtmp[:, 0:16], gtmp[:, 0:16], 0.25,
                waits=[(s_sc, nab)], inc=s_v)
            vop("vector", "tensor_scalar", gtmp[:, 16:32], cols[:, co:co + 16], 0.0,
                None, ALU.is_ge, inc=s_v)
            vop("vector", "tensor_scalar", gtmp[:, 16:32], gtmp[:, 16:32], 2.0, -1.0,
                ALU.mult, ALU.add, inc=s_v)
            vop("vector", "tensor_tensor", cols[:, co:co + 16], gtmp[:, 0:16],
                gtmp[:, 16:32], ALU.mult, inc=s_v)
        vop("vector", "tensor_tensor", cols[:, CO_GBDS:CO_GBDS + 16],
            cols[:, CO_DS:CO_DS + 16], cols[:, CO_GB:CO_GB + 16], ALU.mult, inc=s_v)
        n_colmath = s_v.n

        def ln_phase(src, co_scale, co_shift, tapname, first_waits=()):
            x_t = vf(0, 16, TT)
            xb_t = vb(24576, 16, TT)
            sq_t = vb(32768, 16, TT)
            f2 = vf(8192, 2, TT)
            bc_m = vf(9216, TT)
            bc_r = vf(9728, TT)
            rrows = vf(10240, 4, TT)
            first = True
            for tt4 in range(NT):
                tsl = slice(tt4 * TT, (tt4 + 1) * TT)
                xw = [(s_sc, s_sc.n), (s_v, s_v.n)]
                if tt4 == 0:
                    xw += list(first_waits)
                nx = dma("sync", x_t[:, :, :], src[:, :, tsl], s_x, waits=xw)
                ncst = vop("vector", "tensor_copy", xb_t[:, :, :], x_t[:, :, :],
                           waits=[(s_x, nx)], inc=s_v)
                nsq = vop("scalar", "activation", sq_t[:, :, :], x_t[:, :, :], AF.Square,
                          waits=[(s_x, nx)], inc=s_sc)
                for fc in range(NC16):
                    mm(ps[2][0:1, :], ones_col[:, :], xb_t[:, fc, :],
                       start=(fc == 0), stop=(fc == NC16 - 1),
                       waits=([(s_v, ncst), bank_free[2], bank_free[3]] if fc == 0 else ()))
                for fc in range(NC16):
                    mm(ps[3][0:1, :], ones_col[:, :], sq_t[:, fc, :],
                       start=(fc == 0), stop=(fc == NC16 - 1),
                       waits=[(s_sc, nsq)] if fc == 0 else ())
                nstat = tmark()
                r_m, r_r = rrows[0:1, 0, :], rrows[0:1, 1, :]
                r_3, r_4 = rrows[0:1, 2, :], rrows[0:1, 3, :]
                vop("vector", "tensor_scalar_mul", r_m, ps[2][0:1, :], 1.0 / 2048.0,
                    waits=[(s_mm, nstat), (s_mm, s_mm.n)], inc=s_v)
                n23 = vop("vector", "tensor_scalar_mul", r_3, ps[3][0:1, :], 1.0 / 2048.0,
                          inc=s_v)
                bank_free[2] = (s_v, n23)
                bank_free[3] = (s_v, n23)
                vop("vector", "tensor_tensor", r_4, r_m, r_m, ALU.mult, inc=s_v)
                vop("vector", "tensor_tensor", r_3, r_3, r_4, ALU.subtract, inc=s_v)
                nvar = vop("vector", "tensor_scalar_add", r_3, r_3, LNEPS, inc=s_v)
                nln = vop("scalar", "activation", r_4, r_3, AF.Ln,
                          waits=[(s_v, nvar)], inc=s_sc)
                nr = vop("scalar", "activation", r_r, r_4, AF.Exp,
                         waits=[(s_sc, nln)], inc=s_sc, kwargs={"scale": -0.5})
                bcast(r_m, bc_m[:, :], 4, waits=[(s_sc, nr)])
                nbr = bcast(r_r, bc_r[:, :], 5)
                napply = 0
                for fc in range(NC16):
                    vop("vector", "tensor_tensor", f2[:, 0, :], x_t[:, fc, :],
                        bc_m[:, :], ALU.subtract,
                        waits=[(s_v, nbr)] if fc == 0 else (), inc=s_v)
                    v2 = vop("vector", "tensor_tensor", f2[:, 1, :], f2[:, 0, :],
                             bc_r[:, :], ALU.mult,
                             waits=[(s_sc, napply)] if napply else (), inc=s_v)
                    aw = [(s_v, v2)]
                    if first:
                        aw.append((s_v, n_colmath))
                        first = False
                    napply = vop("scalar", "activation", h_b[:, fc, tsl], f2[:, 1, :],
                                 AF.Identity, waits=aw, inc=s_sc,
                                 kwargs={"scale": col(cols, co_scale + fc),
                                         "bias": col(cols, co_shift + fc)})
            if tapname in taps:
                for tt4 in range(NT):
                    tsl = slice(tt4 * TT, (tt4 + 1) * TT)
                    for fc in range(NC16):
                        v = vop("vector", "tensor_copy", vf(8192, TT), h_b[:, fc, tsl],
                                waits=[(s_stg, s_stg.n), (s_sc, s_sc.n)], inc=s_v)
                        dma("gpsimd", d_tap[tapname][:, fc, tsl], vf(8192, TT),
                            s_stg, waits=[(s_v, v)])

        ln_phase(d_x, CO_ONECA, CO_SA, "h")
        barrier()

        U_tm = vb(0, 16, 768)
        U_fm = vb(12288, 6, 2048)
        wmv = vb(24576, 16, 512)
        a_tm = vb(32768, 16, 256)
        a_fm = vb(36864, 2, 2048)
        wmbrhs = vb(32768, 16, 512)
        kfm = vb(40960, 2048)
        roll = vb(40960, 2048)
        rollB = vb(43008, 2048)
        wkvb = vb(43008, 16, 128)
        mbt = vb(43008, 2, 512)
        v_tm = vb(45056, 16, 128)
        t1s = fsc[:, 0:512]
        stg_ev = [fsc[:, 512:1024], fsc[:, 1024:1536]]
        vstg = fsc[:, 1024:1536].bitcast(BF16)
        vtr_done = [0, 0]

        def roll_dmas(src2, dst, jr, half, sem, waits, queue="scalar"):
            dma(queue, dst[jr:128], src2[:, half, :][0:128 - jr, :], sem, waits=waits)
            return dma(queue, dst[0:jr], src2[:, 1 - half, :][128 - jr:128, :], sem,
                       waits=waits)

        arin_stores = 0
        n_ar_j = [0] * 4
        prev_cols_user = 0
        prev_mbt_users = [(s_mm, 0)]

        for j in range(4):
            n_wka = dma("gpsimd", wkvb[:, :, :], d_wk[:, :, j * 128:(j + 1) * 128],
                        s_wk, waits=prev_mbt_users)
            ev_k = []
            for t4 in range(NT):
                tsl = slice(t4 * TT, (t4 + 1) * TT)
                b = t4 % 2
                for fc in range(NC16):
                    mm(ps[b][:, :], wkvb[:, fc, :], h_b[:, fc, tsl],
                       start=(fc == 0), stop=(fc == NC16 - 1),
                       waits=[(s_wk, n_wka), bank_free[b]] if fc == 0 else ())
                nk = tmark()
                vop("scalar", "activation", kfm[:, tsl], ps[b][:, :], AF.Identity,
                    waits=[(s_mm, nk)], inc=s_sc,
                    kwargs={"bias": col(cols, CO_BK + j)})
                vop("scalar", "activation", a_fm[:, 0, tsl], ps[b][:, :], AF.Relu,
                    inc=s_sc, kwargs={"bias": col(cols, CO_BK + j)})
                nev = vop("scalar", "activation", a_fm[:, 1, tsl], ps[b][:, :], AF.Relu,
                          inc=s_sc, kwargs={"scale": -1.0, "bias": col(cols, CO_NBK + j)})
                bank_free[b] = (s_sc, nev)
                ev_k.append(nev)
            n_wkb = dma("gpsimd", wkvb[:, :, :], d_wv[:, :, j * 128:(j + 1) * 128],
                        s_wk, waits=[(s_mm, s_mm.n)])
            vev = []
            for t4 in range(NT):
                tsl = slice(t4 * TT, (t4 + 1) * TT)
                b = 2 + t4 % 2
                for fc in range(NC16):
                    mm(ps[b][:, :], wkvb[:, fc, :], h_b[:, fc, tsl],
                       start=(fc == 0), stop=(fc == NC16 - 1),
                       waits=[(s_wk, n_wkb), bank_free[b]] if fc == 0 else ())
                nv = tmark()
                vsl = vstg[:, (t4 % 2) * 512:(t4 % 2) * 512 + 512]
                ev = vop("vector", "tensor_scalar", vsl, ps[b][:, :],
                         col(cols, CO_BV + j), None, ALU.add,
                         waits=[(s_mm, nv), (s_mm, vtr_done[t4 % 2]),
                                (s_st, arin_stores)], inc=s_v)
                bank_free[b] = (s_v, ev)
                for tc4 in range(4):
                    tc = t4 * 4 + tc4
                    tr(psT[:, 0:128], vsl[:, tc4 * 128:(tc4 + 1) * 128], identB[:, :],
                       waits=[(s_v, ev), bank_free[7]])
                    ntr = tmark()
                    nevt = vop("scalar", "activation", v_tm[:, tc, :], psT[:, 0:128],
                               AF.Identity, waits=[(s_mm, ntr)], inc=s_sc)
                    bank_free[7] = (s_sc, nevt)
                vtr_done[t4 % 2] = s_mm.n
                vev.append(s_sc.n)
            for tc in range(16):
                t4 = tc // 4
                tr(psT[:, 128:256], kfm[:, tc * 128:(tc + 1) * 128], identB[:, :],
                   waits=[(s_sc, ev_k[t4]), bank_free[7]])
                ntr = tmark()
                vop("scalar", "activation", a_tm[:, tc, 0:128], psT[:, 128:256], AF.Relu,
                    waits=[(s_mm, ntr)], inc=s_sc)
                nevt = vop("scalar", "activation", a_tm[:, tc, 128:256], psT[:, 128:256],
                           AF.Relu, inc=s_sc, kwargs={"scale": -1.0})
                bank_free[7] = (s_sc, nevt)
            n_atm = s_sc.n
            for jr in (1, 2, 3):
                base = (jr - 1) * 256
                vop("vector", "tensor_tensor", U_tm[:, :, base + jr:base + 256],
                    a_tm[:, :, jr:256], a_tm[:, :, 0:256 - jr], ALU.mult,
                    waits=[(s_sc, n_atm)], inc=s_v)
                vop("vector", "tensor_tensor", U_tm[:, :, base:base + jr],
                    a_tm[:, :, 0:jr], a_tm[:, :, 256 - jr:256], ALU.mult, inc=s_v)
            n_utm = s_v.n
            for tc in range(16):
                vop("scalar", "activation", vb(24576, 768), U_tm[:, tc, :],
                    AF.Square,
                    waits=[(s_v, n_utm), (s_mm, prev_cols_user)] if tc == 0 else (),
                    inc=s_sc, kwargs={"accum_out": col(kcols, KC_SS + tc)})
            n_ss = s_sc.n
            ufm_done = []
            roll_free = [0, 0]
            pend = {}

            def issue_kroll(uc):
                jr, half = uc // 2 + 1, uc % 2
                slot = roll if uc % 2 == 0 else rollB
                sm = s_qrA if uc % 2 == 0 else s_qrB
                w = [(s_sc, ev_k[3]), (s_v, roll_free[uc % 2]), (s_mm, s_mm.n)]
                pend[uc] = (sm, roll_dmas(a_fm, slot, jr, half, sm, w,
                                          queue="gpsimd"))

            issue_kroll(0)
            issue_kroll(1)
            prev_tt = s_v.n
            for uchunk in range(6):
                slot = roll if uchunk % 2 == 0 else rollB
                sm, nr2 = pend[uchunk]
                prev_tt = vop("vector", "tensor_tensor", U_fm[:, uchunk, :],
                              a_fm[:, uchunk % 2, :], slot, ALU.mult,
                              waits=[(sm, nr2)], inc=s_v)
                roll_free[uchunk % 2] = prev_tt
                ufm_done.append(prev_tt)
                if uchunk + 2 < 6:
                    issue_kroll(uchunk + 2)
            for tc in range(16):
                for c6 in range(6):
                    w0 = [(s_v, ufm_done[c6])] if tc == 0 else []
                    if tc == 0 and c6 == 0:
                        w0.append(bank_free[6])
                        w0.append((s_nb, n_colsb_ld))
                    mm(psN[:, tc * 4:(tc + 1) * 4],
                       U_fm[:, c6, tc * 128:(tc + 1) * 128],
                       colsb[:, c6 * 16 + 4 * j:c6 * 16 + 4 * j + 4],
                       start=(c6 == 0), stop=(c6 == 5),
                       waits=w0)
            n_dn = tmark()
            ndn = vop("vector", "tensor_copy",
                      kcols[:, KC_DN:KC_DN + 64].rearrange("p (a b) -> p a b", a=4),
                      psN[:, 0:64].rearrange("p (b a) -> p a b", b=16),
                      waits=[(s_mm, n_dn)], inc=s_v)
            bank_free[6] = (s_v, ndn)
            vop("vector", "tensor_scalar_max", kcols[:, KC_T1:KC_T1 + 16],
                kcols[:, KC_SS:KC_SS + 16], 1e-24, waits=[(s_sc, n_ss)], inc=s_v)
            nv1 = s_v.n
            vop("scalar", "activation", kcols[:, KC_T1:KC_T1 + 16],
                kcols[:, KC_T1:KC_T1 + 16], AF.Ln, waits=[(s_v, nv1)], inc=s_sc)
            nsc1 = vop("scalar", "activation", kcols[:, KC_T1:KC_T1 + 16],
                       kcols[:, KC_T1:KC_T1 + 16], AF.Exp, inc=s_sc,
                       kwargs={"scale": -0.5})
            vop("vector", "tensor_scalar_min", kcols[:, KC_RINV:KC_RINV + 16],
                kcols[:, KC_T1:KC_T1 + 16], 1e12, waits=[(s_sc, nsc1)], inc=s_v)
            for i in range(4):
                vop("vector", "tensor_tensor",
                    kcols[:, KC_DEN + i * 16:KC_DEN + (i + 1) * 16],
                    kcols[:, KC_DN + i * 16:KC_DN + (i + 1) * 16],
                    kcols[:, KC_RINV:KC_RINV + 16], ALU.mult, inc=s_v)
            vop("vector", "tensor_scalar_add", kcols[:, KC_DEN:KC_DEN + 64],
                kcols[:, KC_DEN:KC_DEN + 64], EPS, inc=s_v)
            vop("vector", "tensor_tensor", kcols[:, KC_T2:KC_T2 + 16],
                kcols[:, KC_RINV:KC_RINV + 16], kcols[:, KC_RINV:KC_RINV + 16],
                ALU.mult, inc=s_v)
            vop("vector", "tensor_tensor", kcols[:, KC_T2:KC_T2 + 16],
                kcols[:, KC_T2:KC_T2 + 16], kcols[:, KC_SS:KC_SS + 16], ALU.mult,
                inc=s_v)
            vop("vector", "tensor_scalar_max", kcols[:, KC_T2:KC_T2 + 16],
                kcols[:, KC_T2:KC_T2 + 16], 1e-12, inc=s_v)
            vop("vector", "reciprocal", kcols[:, KC_T2:KC_T2 + 16],
                kcols[:, KC_T2:KC_T2 + 16], inc=s_v)
            kcb_v = kcolsb[:, :].rearrange("p (a b) -> p a b", a=16)
            for i in range(4):
                vop("vector", "tensor_tensor", kcols[:, KC_T1:KC_T1 + 16],
                    kcols[:, KC_DEN + i * 16:KC_DEN + (i + 1) * 16],
                    kcols[:, KC_T2:KC_T2 + 16], ALU.mult, inc=s_v)
                vop("vector", "tensor_scalar", kcols[:, KC_T1:KC_T1 + 16],
                    kcols[:, KC_T1:KC_T1 + 16], -1.0, 1.0, ALU.mult, ALU.add, inc=s_v)
                vop("vector", "tensor_scalar_max", kcols[:, KC_T1:KC_T1 + 16],
                    kcols[:, KC_T1:KC_T1 + 16], 0.0, inc=s_v)
                vop("vector", "tensor_scalar_min", kcols[:, KC_T1:KC_T1 + 16],
                    kcols[:, KC_T1:KC_T1 + 16], 1.0, inc=s_v)
                vop("vector", "tensor_tensor", kcols[:, KC_T1:KC_T1 + 16],
                    kcols[:, KC_T1:KC_T1 + 16], kcols[:, KC_RINV:KC_RINV + 16],
                    ALU.mult, inc=s_v)
                vop("vector", "tensor_copy", kcb_v[:, :, i], kcols[:, KC_T1:KC_T1 + 16],
                    inc=s_v)
            vop("vector", "reciprocal", kcols[:, KC_NS1:KC_NS1 + 64],
                kcols[:, KC_DEN:KC_DEN + 64], inc=s_v)
            for i in range(4):
                vop("vector", "tensor_tensor",
                    kcols[:, KC_NS1 + i * 16:KC_NS1 + (i + 1) * 16],
                    kcols[:, KC_NS1 + i * 16:KC_NS1 + (i + 1) * 16],
                    kcols[:, KC_RINV:KC_RINV + 16], ALU.mult, inc=s_v)
            n_cm = vop("vector", "tensor_scalar_mul", kcols[:, KC_NS1:KC_NS1 + 64],
                       kcols[:, KC_NS1:KC_NS1 + 64], -1.0, inc=s_v)

            n_mbrdy = dma("gpsimd", wmbrhs[:, :, :],
                          d_wmb[:, :, j * 512:(j + 1) * 512], s_wmb,
                          waits=[(s_v, ufm_done[5]), (s_sc, n_ss)])
            n_brow = dma("gpsimd", browb[:, :], d_bmb[0:1, j * 512:(j + 1) * 512], s_br,
                         waits=[(s_mm, s_mm.n)])

            wmv_marks = [0] * 16
            mbt_free = [roll_free[1], roll_free[1]]
            for tc in range(16):
                tsl128 = slice(tc * 128, (tc + 1) * 128)
                bmb_ = tc % 2
                bnm = 2 + tc % 2
                for fc in range(NC16):
                    mm(ps[bmb_][:, :], h_b[:, fc, tsl128], wmbrhs[:, fc, :],
                       start=(fc == 0), stop=False,
                       waits=[(s_wmb, n_mbrdy), bank_free[bmb_]] if fc == 0 else ())
                mm(ps[bmb_][:, :], ones_row[:, :], browb[0:1, :],
                   start=False, stop=True,
                   waits=[(s_br, n_brow)] if tc == 0 else ())
                nmb = tmark()
                nmbe = vop("scalar", "activation", mbt[:, tc % 2, :], ps[bmb_][:, :],
                           AF.Sigmoid,
                           waits=[(s_mm, nmb), (s_v, mbt_free[tc % 2])], inc=s_sc)
                bank_free[bmb_] = (s_sc, nmbe)
                for c6 in range(6):
                    mm(ps[bnm][:, :], U_fm[:, c6, tsl128], memb[:, j * 6 + c6, :],
                       start=(c6 == 0), stop=(c6 == 5),
                       waits=[bank_free[bnm], (s_mb, n_memcast)] if c6 == 0 else ())
                nnum = tmark()
                for i in range(4):
                    vop("vector", "scalar_tensor_tensor",
                        t1s[:, i * 128:(i + 1) * 128], ps[bnm][:, i * 128:(i + 1) * 128],
                        col(kcols, KC_NS1 + i * 16 + tc), v_tm[:, tc, :],
                        ALU.mult, ALU.add,
                        waits=[(s_mm, nnum), (s_v, n_cm), (s_sc, vev[-1])] if i == 0 else (),
                        inc=s_v)
                nt1 = s_v.n
                bank_free[bnm] = (s_v, nt1)
                for i in range(4):
                    vop("vector", "scalar_tensor_tensor",
                        wmv[:, tc, i * 128:(i + 1) * 128], t1s[:, i * 128:(i + 1) * 128],
                        col(kcols, KC_RINV + tc), mbt[:, tc % 2, i * 128:(i + 1) * 128],
                        ALU.mult, ALU.mult,
                        waits=[(s_sc, nmbe)] if i == 0 else (), inc=s_v)
                mbt_free[tc % 2] = s_v.n
                wmv_marks[tc] = s_v.n
            n_wmv = s_v.n
            prev_mbt_users = [(s_v, n_wmv), (s_sc, s_sc.n)]

            for c6 in range(6):
                for tc in range(16):
                    w0 = []
                    if c6 == 0:
                        w0.append((s_v, wmv_marks[tc]))
                    if tc == 0:
                        w0.append(bank_free[c6])
                    mm(ps[c6][:, :], U_tm[:, tc, c6 * 128:(c6 + 1) * 128],
                       wmv[:, tc, :], start=(tc == 0), stop=(tc == 15), waits=w0)
                    mm(psN[:, c6 * 4:(c6 + 1) * 4],
                       U_tm[:, tc, c6 * 128:(c6 + 1) * 128],
                       kcb_v[:, tc, :], start=(tc == 0), stop=(tc == 15),
                       waits=[bank_free[6]] if (tc == 0 and c6 == 0) else ())
            n_delta = tmark()
            prev_cols_user = n_delta
            for c6 in range(6):
                sl = stg_ev[c6 % 2]
                nev = vop("vector", "tensor_copy", sl, ps[c6][:, :],
                          waits=[(s_mm, n_delta), (s_st, arin_stores)], inc=s_v)
                bank_free[c6] = (s_v, nev)
                arin_stores = dma("sync",
                                  d_arin[(j * 6 + c6) * 128:(j * 6 + c6 + 1) * 128, :],
                                  sl, s_st, waits=[(s_v, nev)])
            ndne = vop("vector", "tensor_copy",
                       dnacc[:, :].rearrange("p (a b) -> p a b", a=6)[:, :, 4 * j:4 * j + 4],
                       psN[:, 0:24].rearrange("p (a b) -> p a b", a=6),
                       waits=[(s_mm, n_delta)], inc=s_v)
            bank_free[6] = (s_v, ndne)

            if "kdbg" in taps and j == 0:
                v = vop("vector", "tensor_copy", fsc[:, 0:256], kcols[:, :], inc=s_v)
                dma("gpsimd", d_tap["kdbg"][:, 0:256], fsc[:, 0:256], s_stg,
                    waits=[(s_v, v)])

            def f_arj(e, w=arin_stores, jj=j):
                e.wait_ge(s_st.h, w)
                e.collective_compute(
                    "AllReduce", ALU.add,
                    replica_groups=[list(range(8))],
                    ins=[d_arin[jj * 768:(jj + 1) * 768, :]],
                    outs=[d_arout[jj * 768:(jj + 1) * 768, :]],
                ).then_inc(s_cc.h, 1)
            emit("gpsimd", f_arj)
            s_cc.n += 1
            n_ar_j[j] = s_cc.n

        dma("sync", d_arin[3072:3096, :], dnacc[:, :], s_std, waits=[(s_v, s_v.n)])

        def f_ar(e, w=s_std.n):
            e.wait_ge(s_std.h, w)
            e.collective_compute(
                "AllReduce", ALU.add,
                replica_groups=[list(range(8))],
                ins=[d_arin[3072:3096, :]],
                outs=[d_arout[3072:3096, :]],
            ).then_inc(s_cc.h, 1)
        emit("gpsimd", f_ar)
        s_cc.n += 1
        n_ar = s_cc.n

        prevup = s_v.n
        for idx in range(24):
            sl = 1024 * (idx % 2)
            na = dma("sync", wstg[:, sl:sl + 512], d_arout[idx * 128:(idx + 1) * 128, :],
                     s_x, waits=[(s_cc, n_ar_j[idx // 6]), (s_v, prevup)])
            nb = dma("sync", wstg[:, sl + 512:sl + 1024],
                     d_mem[idx // 6, idx % 6, :, :], s_x)
            prevup = vop("vector", "tensor_tensor", memb[:, idx, :],
                         wstg[:, sl:sl + 512], wstg[:, sl + 512:sl + 1024], ALU.add,
                         waits=[(s_x, nb), (s_mm, s_mm.n)], inc=s_v)

        n_na = dma("sync", fsc[:, 0:96], d_arout[3072:3096, :], s_x,
                   waits=[(s_cc, n_ar), (s_v, s_v.n)])
        vop("vector", "tensor_tensor", nrmf[:, :], nrmf[:, :], fsc[:, 0:96], ALU.add,
            waits=[(s_x, n_na)], inc=s_v)
        n_newmem = vop("vector", "tensor_copy", colsb[:, 96:192], nrmf[:, :], inc=s_v)

        if "newmem" in taps:
            for idx in range(24):
                v = vop("vector", "tensor_copy", fsc[:, 512:1024], memb[:, idx, :],
                        waits=[(s_v, n_newmem), (s_stg, s_stg.n)], inc=s_v)
                dma("gpsimd", d_tap["newmem"][idx, :, :], fsc[:, 512:1024], s_stg,
                    waits=[(s_v, v)])
            v = vop("vector", "tensor_copy", fsc[:, 0:96], nrmf[:, :], inc=s_v)
            dma("gpsimd", d_tap["newnrm"][:, :], fsc[:, 0:96], s_stg, waits=[(s_v, v)])

        U_q = vb(0, 6, 2048)
        qa_fm = vb(12288, 2, 2, 2048)
        qroll = vb(20480, 2048)
        qrollB = vb(22528, 2048)
        qdust = fsc[:, 0:512].bitcast(BF16).rearrange("p (a b) -> p a b", a=2)
        wqb = vb(24576, 16, 128)
        wgb = vb(26624, 16, 128)
        gate_b = vb(28672, 2, 2048)
        xqB = vf(16384, 2048)
        bcD = vf(18432, 2048)
        xqA = vf(20480, 2048)
        tmpq = vf(22528, 512)
        ssT = vf(23040, 16)
        deT = vf(23056, 16)
        fT = vf(23072, 16)

        prev_projq = 0
        prev_projg = 0
        qa_free = [0, 0]
        gate_free = [0, 0]
        stA = {}

        qw_pend = {}

        def issue_qw(h):
            nwq = dma("gpsimd", wqb[:, :, :], d_wq[:, :, h * 128:(h + 1) * 128],
                      s_wq, waits=[(s_mm, stA.get(h - 1, (0,) * 9)[0])])
            nwg = dma("gpsimd", wgb[:, :, :], d_wg[:, :, h * 128:(h + 1) * 128],
                      s_wg, waits=[(s_mm, stA.get(h - 1, (0,) * 9)[1])])
            qw_pend[h] = (nwq, nwg)

        def q_stage_a(h):
            sl = h % 2
            nwq, nwg = qw_pend[h]
            nqa = 0
            ngev = 0
            lastq = 0
            lastg = 0
            for t4 in range(NT):
                tsl = slice(t4 * TT, (t4 + 1) * TT)
                b = t4 % 2
                for fc in range(NC16):
                    mm(ps[b][:, :], wqb[:, fc, :], h_b[:, fc, tsl],
                       start=(fc == 0), stop=(fc == NC16 - 1),
                       waits=[(s_wq, nwq), bank_free[b],
                              (s_v, qa_free[sl])] if fc == 0 else ())
                nq = tmark()
                lastq = nq
                vop("scalar", "activation", qa_fm[:, sl, 0, tsl], ps[b][:, :], AF.Relu,
                    waits=[(s_mm, nq)], inc=s_sc, kwargs={"bias": col(cols, CO_BQ + h)})
                nqa = vop("scalar", "activation", qa_fm[:, sl, 1, tsl], ps[b][:, :],
                          AF.Relu, inc=s_sc,
                          kwargs={"scale": -1.0, "bias": col(cols, CO_NBQ + h)})
                bank_free[b] = (s_sc, nqa)
                bg_ = 2 + t4 % 2
                for fc in range(NC16):
                    mm(ps[bg_][:, :], wgb[:, fc, :], h_b[:, fc, tsl],
                       start=(fc == 0), stop=(fc == NC16 - 1),
                       waits=[(s_wg, nwg), bank_free[bg_],
                              (s_v, gate_free[sl])] if fc == 0 else ())
                ng = tmark()
                lastg = ng
                ngev = vop("scalar", "activation", gate_b[:, sl, tsl], ps[bg_][:, :],
                           AF.Sigmoid, waits=[(s_mm, ng)], inc=s_sc,
                           kwargs={"bias": col(cols, CO_BG + h)})
                bank_free[bg_] = (s_sc, ngev)
            stA[h] = (lastq, lastg, nqa, ngev)

        qroll_free = [0, 0]
        xq_slot_free = [0, 0]
        xq_store_mark = [0, 0]
        dustA_mark = [0]
        dustB_mark = [0]
        qpend_g = {}

        def issue_qroll(h, uc):
            sl = h % 2
            nqa = stA[h][2]
            jr, half = uc // 2 + 1, uc % 2
            slot = qroll if uc % 2 == 0 else qrollB
            sm = s_qrA if uc % 2 == 0 else s_qrB
            w = [(s_sc, nqa), (s_v, qroll_free[uc % 2])]
            qpend_g[(h, uc)] = (sm, roll_dmas(qa_fm[:, sl], slot, jr, half, sm, w,
                                              queue="gpsimd"))

        def q_stage_b(h):
            sl = h % 2
            j, i = h // 4, h % 4
            lastq, lastg, nqa, ngev = stA[h]
            if (h, 0) not in qpend_g:
                issue_qroll(h, 0)
                issue_qroll(h, 1)
            uq_done = []
            for uchunk in range(6):
                slot = qroll if uchunk % 2 == 0 else qrollB
                sm2, nr2 = qpend_g[(h, uchunk)]
                ptt = vop("vector", "tensor_tensor", U_q[:, uchunk, :],
                          qa_fm[:, sl, uchunk % 2, :], slot, ALU.mult,
                          waits=[(sm2, nr2)], inc=s_v)
                qroll_free[uchunk % 2] = ptt
                uq_done.append(ptt)
                if uchunk + 2 < 6:
                    issue_qroll(h, uchunk + 2)
            qa_free[sl] = uq_done[-1]
            xqT = xqA if sl == 0 else xqB
            s_xqs = s_xq2[sl]
            s_stq = s_stx if sl == 0 else s_sty
            nxq = dma("sync", xqT[:, :], d_x[:, h, :], s_xqs,
                      waits=[(s_v, xq_slot_free[sl]), (s_stq, xq_store_mark[sl])])
            dustA = fsc[:, 0:1024].bitcast(BF16)
            dustB = wstg[:, 0:1024].bitcast(BF16)
            nsq = [None] * 6

            def issue_square(c6):
                d = dustA if c6 % 2 == 0 else dustB
                w = [(s_v, uq_done[c6])]
                if c6 % 2 == 0:
                    w.append((s_mm, dustA_mark[0]))
                    nsq[c6] = (s_sc, vop("scalar", "activation", d, U_q[:, c6, :],
                                         AF.Square, waits=w, inc=s_sc))
                else:
                    w.append((s_mm, dustB_mark[0]))
                    nsq[c6] = (s_v, vop("vector", "tensor_tensor", d, U_q[:, c6, :],
                                        U_q[:, c6, :], ALU.mult, waits=w, inc=s_v))
            for t4 in range(NT):
                tsl = slice(t4 * TT, (t4 + 1) * TT)
                bd = 4 + t4 % 2
                for c6 in range(6):
                    mm(ps[bd][0:1, :],
                       colsb[:, 96 + c6 * 16 + h:96 + c6 * 16 + h + 1],
                       U_q[:, c6, tsl], start=(c6 == 0), stop=(c6 == 5),
                       waits=([(s_v, n_newmem), bank_free[bd], (s_v, uq_done[0])]
                              if c6 == 0 else
                              [(s_v, uq_done[c6])] if t4 == 0 else ()))
                nsd = tmark()
                nev = vop("scalar", "activation", bcD[0:1, tsl], ps[bd][0:1, :],
                          AF.Identity, waits=[(s_mm, nsd)], inc=s_sc,
                          kwargs={"bias": 0.0})
                bank_free[bd] = (s_sc, nev)
            nscat_d = dma("gpsimd", deT[:, :], bcD[0:1, :], s_qd,
                          waits=[(s_sc, s_sc.n)])
            issue_square(0)
            issue_square(1)
            nss_t4 = [0] * NT
            for c6 in range(6):
                d = dustA if c6 % 2 == 0 else dustB
                for t4 in range(NT):
                    tsl = slice(t4 * TT, (t4 + 1) * TT)
                    nss_t4[t4] = mm(ps[t4][0:1, :], ones_col[:, :], d[:, tsl],
                                    start=(c6 == 0), stop=(c6 == 5),
                                    waits=([nsq[c6]] +
                                           ([bank_free[t4]] if c6 == 0 else [])),
                                    inc=(s_mm if (t4 == NT - 1 or c6 == 5)
                                         else None))
                if c6 % 2 == 0:
                    dustA_mark[0] = s_mm.n
                else:
                    dustB_mark[0] = s_mm.n
                if c6 + 2 < 6:
                    issue_square(c6 + 2)
            for t4 in range(NT):
                tsl = slice(t4 * TT, (t4 + 1) * TT)
                nev = vop("scalar", "activation", bcD[0:1, tsl], ps[t4][0:1, :],
                          AF.Identity,
                          waits=[(s_mm, nss_t4[t4]), (s_qd, nscat_d)], inc=s_sc,
                          kwargs={"bias": 0.0})
                bank_free[t4] = (s_sc, nev)
            nscat_s = dma("gpsimd", ssT[:, :], bcD[0:1, :], s_qss,
                          waits=[(s_sc, s_sc.n)])
            vop("vector", "tensor_scalar_max", ssT[:, :], ssT[:, :], 1e-24,
                waits=[(s_qss, nscat_s)], inc=s_v)
            nv1 = s_v.n
            vop("scalar", "activation", ssT[:, :], ssT[:, :], AF.Ln,
                waits=[(s_v, nv1)], inc=s_sc)
            nsc2 = vop("scalar", "activation", ssT[:, :], ssT[:, :], AF.Exp,
                       inc=s_sc, kwargs={"scale": -0.5})
            vop("vector", "tensor_scalar_min", ssT[:, :], ssT[:, :], 1e12,
                waits=[(s_sc, nsc2)], inc=s_v)
            vop("vector", "tensor_tensor", deT[:, :], deT[:, :], ssT[:, :],
                ALU.mult, inc=s_v)
            vop("vector", "tensor_scalar_add", deT[:, :], deT[:, :], EPS, inc=s_v)
            vop("vector", "reciprocal", deT[:, :], deT[:, :], inc=s_v)
            nf = vop("vector", "tensor_tensor", fT[:, :], ssT[:, :], deT[:, :],
                     ALU.mult, inc=s_v)
            ngat = dma("gpsimd", bcD[0:1, :], fT[:, :], s_qg, waits=[(s_v, nf)])
            nna4 = [0] * NT
            for t4 in range(NT):
                tsl = slice(t4 * TT, (t4 + 1) * TT)
                for c6 in range(6):
                    mm(ps[t4][:, :], memb[:, j * 6 + c6, i * 128:(i + 1) * 128],
                       U_q[:, c6, tsl], start=(c6 == 0), stop=(c6 == 5),
                       waits=[bank_free[t4], (s_v, n_newmem)] if c6 == 0 else ())
                nna4[t4] = tmark()
            nbcf = [0] * NT
            for t4 in range(NT):
                tsl = slice(t4 * TT, (t4 + 1) * TT)
                nbcf[t4] = bcast(bcD[0:1, tsl], bcD[:, tsl], 4 + t4 % 2,
                                 waits=[(s_qg, ngat)])
            ngm = 0
            for t4 in range(NT):
                tsl = slice(t4 * TT, (t4 + 1) * TT)
                nna = nna4[t4]
                vop("vector", "tensor_tensor", tmpq[:, :], ps[t4][:, :], bcD[:, tsl],
                    ALU.mult, waits=[(s_mm, nna), (s_v, nbcf[t4])], inc=s_v)
                nc1 = s_v.n
                bank_free[t4] = (s_v, nc1)
                ngm = vop("vector", "tensor_tensor", tmpq[:, :], tmpq[:, :],
                          gate_b[:, sl, tsl], ALU.mult, inc=s_v)
                vop("vector", "tensor_tensor", tmpq[:, :], tmpq[:, :], h_b[:, h, tsl],
                    ALU.add, inc=s_v)
                nxn = vop("vector", "scalar_tensor_tensor", xqT[:, tsl], tmpq[:, :],
                          col(cols, CO_GA + h), xqT[:, tsl], ALU.mult, ALU.add,
                          waits=[(s_xqs, nxq)], inc=s_v)
                dma("scalar", d_xnew[:, h, tsl], xqT[:, tsl], s_stq,
                    waits=[(s_v, nxn)])
            gate_free[sl] = ngm
            xq_slot_free[sl] = s_v.n
            xq_store_mark[sl] = s_stq.n

        issue_qw(0)
        q_stage_a(0)
        issue_qw(1)
        q_stage_a(1)
        issue_qroll(0, 0)
        issue_qroll(0, 1)
        for h in range(16):
            if h + 2 < 16:
                issue_qw(h + 2)
            q_stage_b(h)
            if h + 1 < 16:
                issue_qroll(h + 1, 0)
                issue_qroll(h + 1, 1)
            if h + 2 < 16:
                q_stage_a(h + 2)
        n_xnew_st = s_stx.n
        n_xnew_st2 = s_sty.n

        if "xnew" in taps:
            dma("gpsimd", d_tap["xnew"][:, :, :], d_xnew[:, :, :], s_stg,
                waits=[(s_stx, n_xnew_st), (s_sty, n_xnew_st2)])

        barrier()

        ln_phase(d_xnew, CO_ONECB, CO_SB, "h2",
                 first_waits=[(s_stx, n_xnew_st), (s_sty, n_xnew_st2)])
        barrier()

        act = vb(0, 15, 2048)
        wgateb = vb(30720, 2, 16, 128)
        wupb = vb(34816, 2, 16, 128)
        wdownb = vb(38912, 2, 16, 128)
        silu_t = vb(43008, 512)
        xnin = vf(21760, 2, 512)
        ystg = fsc[:, 0:512]
        yin = fsc[:, 512:1536].rearrange("p (a b) -> p a b", a=2)

        n_wga = [0, 0, 0]
        silu_free = 0
        gu_slot_mark = [[0, 0], [0, 0]]
        gu_loads = {}

        def issue_gu_load(o):
            sl = (o % 2)
            ng = dma("gpsimd", wgateb[:, sl, :, :],
                     d_wgate[:, :, o * 128:(o + 1) * 128], s_wga2[sl],
                     waits=[(s_mm, gu_slot_mark[0][sl])])
            nu = dma("gpsimd", wupb[:, sl, :, :],
                     d_wup[:, :, o * 128:(o + 1) * 128], s_wup2[sl],
                     waits=[(s_mm, gu_slot_mark[1][sl])])
            gu_loads[o] = (ng, nu)

        for gi, (gbase, gn) in enumerate(FFN_GROUPS):
            for ol in range(gn):
                o = gbase + ol
                if o not in gu_loads:
                    issue_gu_load(o)
                if o + 1 < INTER // 128 and (o + 1) not in gu_loads:
                    issue_gu_load(o + 1)
                nwg_, nwu_ = gu_loads[o]
                sl2 = o % 2
                for t4 in range(NT):
                    tsl = slice(t4 * TT, (t4 + 1) * TT)
                    bgt = t4 % 2
                    bup = 2 + t4 % 2
                    for fc in range(NC16):
                        mm(ps[bgt][:, :], wgateb[:, sl2, fc, :], h_b[:, fc, tsl],
                           start=(fc == 0), stop=(fc == NC16 - 1),
                           waits=[(s_wga2[sl2], nwg_), bank_free[bgt]] if fc == 0 else ())
                    ngm = tmark()
                    if t4 == NT - 1:
                        n_wga[0] = ngm
                        gu_slot_mark[0][sl2] = ngm
                    for fc in range(NC16):
                        mm(ps[bup][:, :], wupb[:, sl2, fc, :], h_b[:, fc, tsl],
                           start=(fc == 0), stop=(fc == NC16 - 1),
                           waits=[(s_wup2[sl2], nwu_), bank_free[bup]] if fc == 0 else ())
                    num_ = tmark()
                    if t4 == NT - 1:
                        n_wga[1] = num_
                        gu_slot_mark[1][sl2] = num_
                        if o + 2 < INTER // 128 and (o + 2) not in gu_loads:
                            issue_gu_load(o + 2)
                    nsl = vop("scalar", "activation", silu_t, ps[bgt][:, :], AF.Silu,
                              waits=[(s_mm, ngm), (s_v, silu_free)], inc=s_sc,
                              kwargs={"scale": col(cols, CO_GS + o)})
                    bank_free[bgt] = (s_sc, nsl)
                    nac = vop("vector", "scalar_tensor_tensor", act[:, ol, tsl],
                              ps[bup][:, :], col(cols, CO_US + o), silu_t,
                              ALU.mult, ALU.mult,
                              waits=[(s_mm, num_), (s_sc, nsl)], inc=s_v)
                    silu_free = nac
                    bank_free[bup] = (s_v, nac)
            n_act = s_v.n
            dn_slot_mark = [0, 0]
            dn_loads = {}

            def issue_dn_load(mc2):
                sl3 = mc2 % 2
                dn_loads[mc2] = dma(
                    "gpsimd", wdownb[:, sl3, 0:gn, :],
                    d_wdown[:, gbase:gbase + gn, mc2 * 128:(mc2 + 1) * 128],
                    s_wdn2[sl3], waits=[(s_mm, dn_slot_mark[sl3])])

            yin_mark = [0, 0]
            xnin_mark = [0, 0]
            yin_loads = {}

            def issue_yin(ti):
                mc2, t42 = ti // 4, ti % 4
                sl3 = ti % 2
                yin_loads[ti] = dma(
                    "scalar", yin[:, sl3, :],
                    d_y[:, mc2, t42 * TT:(t42 + 1) * TT], s_yi2[sl3],
                    waits=[(s_stg, n_ydone), (s_v, yin_mark[sl3])])

            n_ydone = s_stg.n if gi == 2 else 0
            for mc in range(NC16):
                if mc not in dn_loads:
                    issue_dn_load(mc)
                if mc + 1 < NC16 and (mc + 1) not in dn_loads:
                    issue_dn_load(mc + 1)
                nwd = dn_loads[mc]
                if gi == 2 and mc == 0:
                    issue_yin(0)
                    issue_yin(1)
                for t4 in range(NT):
                    tsl = slice(t4 * TT, (t4 + 1) * TT)
                    b = 4 + t4 % 2
                    for il in range(gn):
                        mm(ps[b][:, :], wdownb[:, mc % 2, il, :], act[:, il, tsl],
                           start=(il == 0), stop=(il == gn - 1),
                           waits=[(s_wdn2[mc % 2], nwd), (s_v, n_act), bank_free[b]] if il == 0 else ())
                    ndm = tmark()
                    if t4 == NT - 1:
                        n_wga[2] = ndm
                        dn_slot_mark[mc % 2] = ndm
                        if mc + 2 < NC16 and (mc + 2) not in dn_loads:
                            issue_dn_load(mc + 2)
                    if gi == 0:
                        nev = vop("vector", "tensor_copy", ystg, ps[b][:, :],
                                  waits=[(s_mm, ndm), (s_stg, s_stg.n)], inc=s_v)
                        bank_free[b] = (s_v, nev)
                        dma("gpsimd", d_y[:, mc, tsl], ystg, s_stg, waits=[(s_v, nev)])
                    elif gi == 1:
                        nev = vop("vector", "tensor_copy", ystg, ps[b][:, :],
                                  waits=[(s_mm, ndm), (s_stg, s_stg.n)], inc=s_v)
                        bank_free[b] = (s_v, nev)
                        dma("gpsimd", d_y[:, mc, tsl], ystg, s_stg,
                            waits=[(s_v, nev)], accum=True)
                    else:
                        nxn_ = dma("scalar", xnin[:, t4 % 2, :],
                                   d_xnew[:, mc, tsl], s_xi2[t4 % 2],
                                   waits=[(s_v, xnin_mark[t4 % 2]),
                                          (s_stx, n_xnew_st),
                                          (s_sty, n_xnew_st2)])
                        ti = mc * 4 + t4
                        nyi = yin_loads[ti]
                        nev = vop("vector", "tensor_tensor", ystg, ps[b][:, :],
                                  yin[:, ti % 2, :], ALU.add,
                                  waits=[(s_mm, ndm), (s_yi2[ti % 2], nyi),
                                         (s_stg, s_stg.n)], inc=s_v)
                        yin_mark[ti % 2] = nev
                        if ti + 2 < NC16 * 4:
                            issue_yin(ti + 2)
                        bank_free[b] = (s_v, nev)
                        nout = vop("vector", "scalar_tensor_tensor", ystg,
                                   ystg, col(cols, CO_GBDS + mc), xnin[:, t4 % 2, :],
                                   ALU.mult, ALU.add,
                                   waits=[(s_xi2[t4 % 2], nxn_)], inc=s_v)
                        xnin_mark[t4 % 2] = nout
                        dma("gpsimd", d_out[:, mc, tsl], ystg, s_stg,
                            waits=[(s_v, nout)])

        fin = s_stg.n
        emit("gpsimd", lambda e, t=fin: e.wait_ge(s_stg.h, t))
        barrier()

        with nc.Block() as block:
            @block.sync
            def _(e):
                for f in prog["sync"]:
                    f(e)

            @block.tensor
            def _(e):
                for f in prog["tensor"]:
                    f(e)

            @block.vector
            def _(e):
                for f in prog["vector"]:
                    f(e)

            @block.scalar
            def _(e):
                for f in prog["scalar"]:
                    f(e)

            @block.gpsimd
            def _(e):
                for f in prog["gpsimd"]:
                    f(e)

    return nc


def _layout_inputs(inputs):
    g = lambda k: np.asarray(inputs[k])
    x = g("L0_attn_hidden_states")[0]
    per_core = []
    shared = {}

    def chunk16(w):
        return np.ascontiguousarray(w.reshape(NC16, P, w.shape[1]).transpose(1, 0, 2))

    shared["wqt_l"] = chunk16(np.ascontiguousarray(g("L0_attn_Wq").T))
    shared["wkt_l"] = chunk16(np.ascontiguousarray(g("L0_attn_Wk").T))
    shared["wvt_l"] = chunk16(np.ascontiguousarray(g("L0_attn_Wv").T))
    shared["wgt_l"] = chunk16(np.ascontiguousarray(g("L0_attn_Wg").T))
    shared["wmbt_l"] = chunk16(np.ascontiguousarray(g("L0_attn_Wmb").T))
    shared["wgatet_l"] = chunk16(np.ascontiguousarray(g("L0_ffn_gate_w").T))
    shared["wupt_l"] = chunk16(np.ascontiguousarray(g("L0_ffn_up_w").T))
    wd = np.ascontiguousarray(g("L0_ffn_down_w").T)
    shared["wdownt_l"] = np.ascontiguousarray(wd.reshape(NI44, P, 2048).transpose(1, 0, 2))
    shared["bq_c"] = np.ascontiguousarray(g("L0_attn_bq").reshape(16, P).T)
    shared["bk_c"] = np.ascontiguousarray(g("L0_attn_bk").reshape(4, P).T)
    shared["bv_c"] = np.ascontiguousarray(g("L0_attn_bv").reshape(4, P).T)
    shared["bg_c"] = np.ascontiguousarray(g("L0_attn_bg").reshape(16, P).T)
    shared["bmb_r"] = np.ascontiguousarray(g("L0_attn_bmb").reshape(1, 2048))
    shared["mem_l"] = np.ascontiguousarray(
        g("L0_attn_memory").reshape(4, 4, 6, P, D).transpose(0, 2, 3, 1, 4).reshape(4, 6, P, 512))
    shared["nrm_l"] = np.ascontiguousarray(
        g("L0_attn_norm").reshape(16, 6, P).transpose(2, 1, 0).reshape(P, 96))
    shared["gs_c"] = np.ascontiguousarray(g("L0_ffn_gate_s").reshape(NI44, P).T)
    shared["us_c"] = np.ascontiguousarray(g("L0_ffn_up_s").reshape(NI44, P).T)
    shared["ds_c"] = np.ascontiguousarray(g("L0_ffn_down_s").reshape(16, P).T)
    shared["c_c"] = np.ascontiguousarray(g("L0_c").reshape(16, P).T)
    shared["ident_b"] = np.eye(P, dtype=ml_dtypes.bfloat16)
    shared["ident_f"] = np.eye(P, dtype=np.float32)
    shared["ones_col_b"] = np.ones((P, 1), dtype=ml_dtypes.bfloat16)
    shared["ones_row_b"] = np.ones((1, P), dtype=ml_dtypes.bfloat16)
    shared["ones_row_f"] = np.ones((1, P), dtype=np.float32)

    wmod = g("L0_wmod")
    for core in range(8):
        xs = np.ascontiguousarray(x[core * T:(core + 1) * T, :].T)
        m = dict(shared)
        m["x_l"] = np.ascontiguousarray(xs.reshape(NC16, P, T).transpose(1, 0, 2))
        m["wmod_sh_l"] = chunk16(
            np.ascontiguousarray(wmod[:, core * 1536:(core + 1) * 1536]))
        per_core.append(m)
    return per_core


def kernel(**inputs):
    in_maps = _layout_inputs(inputs)
    nc = build(taps=TAPS)
    res = run_bass_kernel_spmd(nc, in_maps, core_ids=list(range(8)))
    outs = []
    for r in res.results:
        o = r["out_l"]
        outs.append(o.transpose(1, 0, 2).reshape(2048, T).T)
    full = np.concatenate(outs, axis=0)[None]
    if TAPS:
        kernel.last_results = res
    return full.astype(np.float32)


# version 15
# speedup vs baseline: 1.0522x; 1.0522x over previous
import sys
if "/opt/trn_rl_repo" not in sys.path:
    sys.path.insert(0, "/opt/trn_rl_repo")
from contextlib import ExitStack

import numpy as np
import ml_dtypes

import concourse.bass as bass
import concourse.mybir as mybir
from concourse.bass_utils import run_bass_kernel_spmd

F32 = mybir.dt.float32
BF16 = mybir.dt.bfloat16
AF = mybir.ActivationFunctionType
ALU = mybir.AluOpType

P = 128
T = 2048
NT = 4
TT = 512
NC16 = 16
D = 128
EPS = 1e-8
LNEPS = 1e-5
INTER = 5632
NI44 = 44
FFN_GROUPS = [(0, 15), (15, 15), (30, 14)]

TAPS = set()

ENGINES = ("sync", "tensor", "vector", "scalar", "gpsimd")


class Sem:
    def __init__(self, es, nc, name):
        self.h = es.enter_context(nc.semaphore(name))
        self.n = 0


def build(taps=()):
    taps = set(taps)
    nc = bass.Bass("TRN2", target_bir_lowering=False)

    d_x = nc.dram_tensor("x_l", [P, NC16, T], F32, kind="ExternalInput").ap()
    d_wq = nc.dram_tensor("wqt_l", [P, NC16, 2048], F32, kind="ExternalInput").ap()
    d_wk = nc.dram_tensor("wkt_l", [P, NC16, 512], F32, kind="ExternalInput").ap()
    d_wv = nc.dram_tensor("wvt_l", [P, NC16, 512], F32, kind="ExternalInput").ap()
    d_wg = nc.dram_tensor("wgt_l", [P, NC16, 2048], F32, kind="ExternalInput").ap()
    d_wmb = nc.dram_tensor("wmbt_l", [P, NC16, 2048], F32, kind="ExternalInput").ap()
    d_bq = nc.dram_tensor("bq_c", [P, 16], F32, kind="ExternalInput").ap()
    d_bk = nc.dram_tensor("bk_c", [P, 4], F32, kind="ExternalInput").ap()
    d_bv = nc.dram_tensor("bv_c", [P, 4], F32, kind="ExternalInput").ap()
    d_bg = nc.dram_tensor("bg_c", [P, 16], F32, kind="ExternalInput").ap()
    d_bmb = nc.dram_tensor("bmb_r", [1, 2048], F32, kind="ExternalInput").ap()
    d_mem = nc.dram_tensor("mem_l", [4, 6, P, 512], F32, kind="ExternalInput").ap()
    d_nrm = nc.dram_tensor("nrm_l", [P, 96], F32, kind="ExternalInput").ap()
    d_wgate = nc.dram_tensor("wgatet_l", [P, NC16, INTER], F32, kind="ExternalInput").ap()
    d_wup = nc.dram_tensor("wupt_l", [P, NC16, INTER], F32, kind="ExternalInput").ap()
    d_wdown = nc.dram_tensor("wdownt_l", [P, NI44, 2048], F32, kind="ExternalInput").ap()
    d_gs = nc.dram_tensor("gs_c", [P, NI44], F32, kind="ExternalInput").ap()
    d_us = nc.dram_tensor("us_c", [P, NI44], F32, kind="ExternalInput").ap()
    d_ds = nc.dram_tensor("ds_c", [P, 16], F32, kind="ExternalInput").ap()
    d_wmod = nc.dram_tensor("wmod_sh_l", [P, NC16, 1536], F32, kind="ExternalInput").ap()
    d_c = nc.dram_tensor("c_c", [P, 16], F32, kind="ExternalInput").ap()
    d_identb = nc.dram_tensor("ident_b", [P, P], BF16, kind="ExternalInput").ap()
    d_identf = nc.dram_tensor("ident_f", [P, P], F32, kind="ExternalInput").ap()
    d_onescol = nc.dram_tensor("ones_col_b", [P, 1], BF16, kind="ExternalInput").ap()
    d_onesrow = nc.dram_tensor("ones_row_b", [1, P], BF16, kind="ExternalInput").ap()
    d_onesrowf = nc.dram_tensor("ones_row_f", [1, P], F32, kind="ExternalInput").ap()

    d_out = nc.dram_tensor("out_l", [P, NC16, T], F32, kind="ExternalOutput").ap()

    d_xnew = nc.dram_tensor("xnew_d", [P, NC16, T], F32).ap()
    d_y = nc.dram_tensor("y_d", [P, NC16, T], F32).ap()
    d_arin = nc.dram_tensor("ar_in", [3096, 512], F32).ap()
    d_arout = nc.dram_tensor("ar_out", [3096, 512], F32).ap()
    d_magi = nc.dram_tensor("modag_in", [12, P], F32).ap()
    d_mago = nc.dram_tensor("modag_out", [96, P], F32).ap()

    d_tap = {}
    if "h" in taps:
        d_tap["h"] = nc.dram_tensor("tap_h", [P, NC16, T], F32, kind="ExternalOutput").ap()
    if "xnew" in taps:
        d_tap["xnew"] = nc.dram_tensor("tap_xnew", [P, NC16, T], F32, kind="ExternalOutput").ap()
    if "h2" in taps:
        d_tap["h2"] = nc.dram_tensor("tap_h2", [P, NC16, T], F32, kind="ExternalOutput").ap()
    if "newmem" in taps:
        d_tap["newmem"] = nc.dram_tensor("tap_newmem", [24, P, 512], F32, kind="ExternalOutput").ap()
        d_tap["newnrm"] = nc.dram_tensor("tap_newnrm", [P, 96], F32, kind="ExternalOutput").ap()
    if "kdbg" in taps:
        d_tap["kdbg"] = nc.dram_tensor("tap_kdbg", [P, 2048], F32, kind="ExternalOutput").ap()

    prog = {e: [] for e in ENGINES}

    with ExitStack() as es:
        def sb(name, shape, dt):
            return es.enter_context(nc.sbuf_tensor(name, shape, dt))

        h_b = sb("h_b", [P, NC16, T], BF16)
        memb = sb("memb", [P, 24, 512], BF16)
        wstg = sb("wstg", [P, 2048], F32)
        AR = sb("AR", [P, 23552], F32)
        fsc = sb("fsc", [P, 1536], F32)
        cols = sb("cols", [P, 288], F32)
        kcols = sb("kcols", [P, 256], F32)
        colsb = sb("colsb", [P, 192], BF16)
        kcolsb = sb("kcolsb", [P, 64], BF16)
        dnacc = sb("dnacc", [P, 96], F32)
        nrmf = sb("nrmf", [P, 96], F32)
        browb = sb("browb", [1, 512], BF16)
        identB = sb("identB", [P, P], BF16)
        identF = sb("identF", [P, P], F32)
        ones_col = sb("ones_col", [P, 1], BF16)
        ones_row = sb("ones_row", [1, P], BF16)
        ones_rowf = sb("ones_rowf", [1, P], F32)
        magr = sb("magr", [96, P], F32)
        csil = sb("csil", [P, 16], F32)
        csilb = sb("csilb", [P, 16], BF16)
        modsh = sb("modsh", [P, 12], F32)
        modrow = sb("modrow", [12, P], F32)
        gtmp = sb("gtmp", [P, 48], F32)

        ARf = AR[:, :]
        ARb = AR[:, :].bitcast(BF16)

        def vf(off, *dims):
            n = int(np.prod(dims))
            v = ARf[:, off:off + n]
            if len(dims) > 1:
                kw = {f"d{i}": d for i, d in enumerate(dims[:-1])}
                pat = "p (" + " ".join(f"d{i}" for i in range(len(dims))) + ") -> p " + \
                    " ".join(f"d{i}" for i in range(len(dims)))
                v = v.rearrange(pat, **kw)
            return v

        def vb(off, *dims):
            n = int(np.prod(dims))
            v = ARb[:, off:off + n]
            if len(dims) > 1:
                kw = {f"d{i}": d for i, d in enumerate(dims[:-1])}
                pat = "p (" + " ".join(f"d{i}" for i in range(len(dims))) + ") -> p " + \
                    " ".join(f"d{i}" for i in range(len(dims)))
                v = v.rearrange(pat, **kw)
            return v

        CO_SA, CO_ONECA, CO_GA = 0, 16, 32
        CO_SB, CO_ONECB, CO_GB = 48, 64, 80
        CO_BQ, CO_BK, CO_BG = 96, 112, 116
        CO_NBK, CO_NBQ, CO_BV = 132, 136, 152
        CO_GS, CO_US, CO_GBDS, CO_DS = 160, 204, 248, 264
        KC_SS, KC_RINV, KC_DN, KC_DEN, KC_NS1, KC_T1, KC_T2 = 0, 16, 32, 96, 160, 224, 240

        ps = [es.enter_context(nc.psum_tensor(f"ps{i}", [P, 512], F32)) for i in range(6)]
        psN = es.enter_context(nc.psum_tensor("psN", [P, 64], F32))
        psT = es.enter_context(nc.psum_tensor("psT", [P, 1024], BF16))

        s_ld = Sem(es, nc, "s_ld")
        s_w = Sem(es, nc, "s_w")
        s_x = Sem(es, nc, "s_x")
        s_v = Sem(es, nc, "s_v")
        s_sc = Sem(es, nc, "s_sc")
        s_mm = Sem(es, nc, "s_mm")
        s_gp = Sem(es, nc, "s_gp")
        s_cc = Sem(es, nc, "s_cc")
        s_st = Sem(es, nc, "s_st")
        s_std = Sem(es, nc, "s_std")
        s_stx = Sem(es, nc, "s_stx")
        s_sty = Sem(es, nc, "s_sty")
        s_stg = Sem(es, nc, "s_stg")
        s_x2 = Sem(es, nc, "s_x2")
        s_nb = Sem(es, nc, "s_nb")
        s_mb = Sem(es, nc, "s_mb")
        s_mg = Sem(es, nc, "s_mg")
        s_br = Sem(es, nc, "s_br")
        s_wk = Sem(es, nc, "s_wk")
        s_wmb = Sem(es, nc, "s_wmb")
        s_wq = Sem(es, nc, "s_wq")
        s_wg = Sem(es, nc, "s_wg")
        s_wga2 = [Sem(es, nc, "s_wga0"), Sem(es, nc, "s_wga1")]
        s_wup2 = [Sem(es, nc, "s_wup0"), Sem(es, nc, "s_wup1")]
        s_wdn2 = [Sem(es, nc, "s_wdn0"), Sem(es, nc, "s_wdn1")]
        s_yi2 = [Sem(es, nc, "s_yi0"), Sem(es, nc, "s_yi1")]
        s_xi2 = [Sem(es, nc, "s_xi0"), Sem(es, nc, "s_xi1")]
        s_rb = Sem(es, nc, "s_rb")
        s_qrA = Sem(es, nc, "s_qrA")
        s_qrB = Sem(es, nc, "s_qrB")
        s_xq2 = [Sem(es, nc, "s_xqA"), Sem(es, nc, "s_xqB")]
        s_qd = Sem(es, nc, "s_qd")
        s_qss = Sem(es, nc, "s_qss")
        s_qg = Sem(es, nc, "s_qg")
        s_bar = Sem(es, nc, "s_bar")

        def emit(eng, fn):
            prog[eng].append(fn)

        def _waits(e, waits):
            for (s, v) in waits:
                if v > 0:
                    e.wait_ge(s.h, v)

        def barrier():
            snap = tuple((s, s.n) for s in
                         (s_v, s_sc, s_mm, s_gp, s_ld, s_w, s_x, s_x2,
                          s_st, s_std, s_stx, s_sty, s_stg, s_cc))
            for eng in ENGINES:
                def f(e, w=snap):
                    _waits(e, w)
                emit(eng, f)

        def dma(eng, out, in_, sem, waits=(), accum=False):
            waits = tuple(waits)
            def f(e, o=out, i=in_, s=sem, w=waits, a=accum):
                _waits(e, w)
                if a:
                    return e.dma_start(out=o, in_=i, accum_op=ALU.add).then_inc(s.h, 16)
                return e.dma_start(out=o, in_=i).then_inc(s.h, 16)
            emit(eng, f)
            sem.n += 16
            return sem.n

        def mm(out, lhsT, rhs, start, stop, waits=(), inc=None):
            waits = tuple(waits)
            if inc is not None:
                inc.n += 1
            def f(e, o=out, l=lhsT, r=rhs, st=start, sp=stop, w=waits, s=inc):
                _waits(e, w)
                i = e.matmul(o, l, r, start=st, stop=sp)
                if s is not None:
                    i.then_inc(s.h, 1)
                return i
            emit("tensor", f)
            return inc.n if inc is not None else None

        def tmark(waits=()):
            assert not waits
            s_mm.n += 1
            last = prog["tensor"].pop()
            def f(e, g=last, t=s_mm.n):
                i = g(e)
                i.then_inc(s_mm.h, 1)
                return i
            emit("tensor", f)
            return s_mm.n

        def tr(out, in_, ident, waits=(), inc=None):
            waits = tuple(waits)
            if inc is not None:
                inc.n += 1
            def f(e, o=out, i=in_, idn=ident, w=waits, s=inc):
                _waits(e, w)
                ins = e.transpose(o, i, idn)
                if s is not None:
                    ins.then_inc(s.h, 1)
                return ins
            emit("tensor", f)
            return inc.n if inc is not None else None

        _eng_sem = {"vector": None, "scalar": None, "gpsimd": None}

        def vop(eng, meth, *args, waits=(), inc=None, kwargs=None):
            own = {"vector": s_v, "scalar": s_sc, "gpsimd": s_gp}.get(eng)
            if own is not None and own.n > 0:
                waits = tuple(waits) + ((own, own.n),)
            waits = tuple(waits)
            if inc is not None:
                inc.n += 1
            kw = dict(kwargs or {})
            def f(e, m=meth, a=args, w=waits, s=inc, kk=kw):
                _waits(e, w)
                i = getattr(e, m)(*a, **kk)
                if s is not None:
                    i.then_inc(s.h, 1)
                return i
            emit(eng, f)
            return inc.n if inc is not None else None

        bank_free = {i: (s_mm, 0) for i in range(8)}

        def col(t, i):
            return t[:, i:i + 1]

        def bcast(row, out, pbank, waits=()):
            n = out.shape[-1]
            mm(ps[pbank][:, 0:n], ones_rowf[:, :], row, start=True, stop=True,
               waits=list(waits) + [bank_free[pbank], (s_ld, s_ld.n)])
            t = tmark()
            r = vop("vector", "tensor_copy", out, ps[pbank][:, 0:n],
                    waits=[(s_mm, t)], inc=s_v)
            bank_free[pbank] = (s_v, r)
            return r

        dma("sync", identB[:, :], d_identb[:, :], s_ld)
        dma("sync", identF[:, :], d_identf[:, :], s_ld)
        dma("sync", ones_col[:, :], d_onescol[:, :], s_ld)
        dma("sync", ones_row[:, :], d_onesrow[:, :], s_ld)
        dma("sync", ones_rowf[:, :], d_onesrowf[:, :], s_ld)
        dma("sync", cols[:, CO_BQ:CO_BQ + 16], d_bq[:, :], s_ld)
        dma("sync", cols[:, CO_BK:CO_BK + 4], d_bk[:, :], s_ld)
        dma("sync", cols[:, CO_BV:CO_BV + 4], d_bv[:, :], s_ld)
        dma("sync", cols[:, CO_BG:CO_BG + 16], d_bg[:, :], s_ld)
        dma("sync", cols[:, CO_GS:CO_GS + 44], d_gs[:, :], s_ld)
        dma("sync", cols[:, CO_US:CO_US + 44], d_us[:, :], s_ld)
        dma("sync", cols[:, CO_DS:CO_DS + 16], d_ds[:, :], s_ld)
        dma("sync", nrmf[:, :], d_nrm[:, :], s_ld)
        n_colsb_ld = dma("gpsimd", colsb[:, 0:96], d_nrm[:, :], s_nb)
        n_ldc = dma("sync", csil[:, :], d_c[:, :], s_ld)
        n_ld0 = n_ldc

        n_silc = vop("scalar", "activation", csil[:, :], csil[:, :], AF.Silu,
                     waits=[(s_ld, n_ldc)], inc=s_sc)
        n_csb = vop("vector", "tensor_copy", csilb[:, :], csil[:, :],
                    waits=[(s_sc, n_silc)], inc=s_v)
        vop("vector", "tensor_scalar_mul", cols[:, CO_NBK:CO_NBK + 4],
            cols[:, CO_BK:CO_BK + 4], -1.0, waits=[(s_ld, n_ld0)], inc=s_v)
        vop("vector", "tensor_scalar_mul", cols[:, CO_NBQ:CO_NBQ + 16],
            cols[:, CO_BQ:CO_BQ + 16], -1.0, inc=s_v)

        n_memcast = 0
        for idx in range(24):
            j, c6 = idx // 6, idx % 6
            n_memcast = dma("gpsimd", memb[:, idx, :], d_mem[j, c6, :, :], s_mb)

        wm_f = vf(20480, 16, 128)
        n_mmprev = 0
        for oc in range(12):
            n_wm = dma("sync", wm_f[:, :, :], d_wmod[:, :, oc * 128:(oc + 1) * 128],
                       s_w, waits=[(s_mm, n_mmprev)])
            for fc in range(NC16):
                mm(psN[:, oc:oc + 1], wm_f[:, fc, :], csil[:, fc:fc + 1],
                   start=(fc == 0), stop=(fc == NC16 - 1),
                   waits=[(s_w, n_wm), (s_sc, n_silc)] if fc == 0 else ())
            n_mmprev = tmark()
        n_modev = vop("vector", "tensor_copy", modsh[:, :], psN[:, 0:12],
                      waits=[(s_mm, n_mmprev)], inc=s_v)
        bank_free[6] = (s_v, n_modev)
        tr(ps[0][0:12, 0:128], modsh[:, :], identF[:, :],
           waits=[(s_v, n_modev), (s_ld, n_ld0)])
        n_modtr = tmark()
        n_modrow = vop("vector", "tensor_copy", modrow[:, :], ps[0][0:12, 0:128],
                       waits=[(s_mm, n_modtr)], inc=s_v)
        bank_free[0] = (s_v, n_modrow)
        n_magst = dma("gpsimd", d_magi[:, :], modrow[:, :], s_stg,
                      waits=[(s_v, n_modrow)])

        def f_ag(e, w=n_magst):
            e.wait_ge(s_stg.h, w)
            e.collective_compute(
                "AllGather", ALU.bypass,
                replica_groups=[list(range(8))],
                ins=[d_magi[:, :]],
                outs=[d_mago[:, :]],
            ).then_inc(s_cc.h, 1)
        emit("gpsimd", f_ag)
        s_cc.n += 1
        n_magr = dma("gpsimd", magr[:, :], d_mago[:, :], s_mg, waits=[(s_cc, s_cc.n)])
        tr(ps[0][0:128, 0:96], magr[:, :], identF[0:96, 0:96],
           waits=[(s_mg, n_magr), bank_free[0]])
        n_modtr2 = tmark()
        n_modcols = vop("vector", "tensor_copy", cols[:, 0:96], ps[0][0:128, 0:96],
                        waits=[(s_mm, n_modtr2)], inc=s_v)
        bank_free[0] = (s_v, n_modcols)

        vop("vector", "tensor_scalar_add", cols[:, CO_ONECA:CO_ONECA + 16],
            cols[:, CO_ONECA:CO_ONECA + 16], 1.0, waits=[(s_v, n_modcols)], inc=s_v)
        vop("vector", "tensor_scalar_add", cols[:, CO_ONECB:CO_ONECB + 16],
            cols[:, CO_ONECB:CO_ONECB + 16], 1.0, inc=s_v)
        for co in (CO_GA, CO_GB):
            nab = vop("scalar", "activation", gtmp[:, 0:16], cols[:, co:co + 16],
                      AF.Abs, waits=[(s_v, s_v.n)], inc=s_sc)
            vop("vector", "tensor_scalar_max", gtmp[:, 0:16], gtmp[:, 0:16], 0.25,
                waits=[(s_sc, nab)], inc=s_v)
            vop("vector", "tensor_scalar", gtmp[:, 16:32], cols[:, co:co + 16], 0.0,
                None, ALU.is_ge, inc=s_v)
            vop("vector", "tensor_scalar", gtmp[:, 16:32], gtmp[:, 16:32], 2.0, -1.0,
                ALU.mult, ALU.add, inc=s_v)
            vop("vector", "tensor_tensor", cols[:, co:co + 16], gtmp[:, 0:16],
                gtmp[:, 16:32], ALU.mult, inc=s_v)
        vop("vector", "tensor_tensor", cols[:, CO_GBDS:CO_GBDS + 16],
            cols[:, CO_DS:CO_DS + 16], cols[:, CO_GB:CO_GB + 16], ALU.mult, inc=s_v)
        n_colmath = s_v.n

        def ln_phase(src, co_scale, co_shift, tapname, first_waits=()):
            x_t = vf(0, 16, TT)
            xb_t = vb(24576, 16, TT)
            sq_t = vb(32768, 16, TT)
            f2 = vf(8192, 2, TT)
            bc_m = vf(9216, TT)
            bc_r = vf(9728, TT)
            rrows = vf(10240, 4, TT)
            first = True
            for tt4 in range(NT):
                tsl = slice(tt4 * TT, (tt4 + 1) * TT)
                xw = [(s_sc, s_sc.n), (s_v, s_v.n)]
                if tt4 == 0:
                    xw += list(first_waits)
                nx = dma("sync", x_t[:, :, :], src[:, :, tsl], s_x, waits=xw)
                ncst = vop("vector", "tensor_copy", xb_t[:, :, :], x_t[:, :, :],
                           waits=[(s_x, nx)], inc=s_v)
                nsq = vop("scalar", "activation", sq_t[:, :, :], x_t[:, :, :], AF.Square,
                          waits=[(s_x, nx)], inc=s_sc)
                for fc in range(NC16):
                    mm(ps[2][0:1, :], ones_col[:, :], xb_t[:, fc, :],
                       start=(fc == 0), stop=(fc == NC16 - 1),
                       waits=([(s_v, ncst), bank_free[2], bank_free[3]] if fc == 0 else ()))
                for fc in range(NC16):
                    mm(ps[3][0:1, :], ones_col[:, :], sq_t[:, fc, :],
                       start=(fc == 0), stop=(fc == NC16 - 1),
                       waits=[(s_sc, nsq)] if fc == 0 else ())
                nstat = tmark()
                r_m, r_r = rrows[0:1, 0, :], rrows[0:1, 1, :]
                r_3, r_4 = rrows[0:1, 2, :], rrows[0:1, 3, :]
                vop("vector", "tensor_scalar_mul", r_m, ps[2][0:1, :], 1.0 / 2048.0,
                    waits=[(s_mm, nstat), (s_mm, s_mm.n)], inc=s_v)
                n23 = vop("vector", "tensor_scalar_mul", r_3, ps[3][0:1, :], 1.0 / 2048.0,
                          inc=s_v)
                bank_free[2] = (s_v, n23)
                bank_free[3] = (s_v, n23)
                vop("vector", "tensor_tensor", r_4, r_m, r_m, ALU.mult, inc=s_v)
                vop("vector", "tensor_tensor", r_3, r_3, r_4, ALU.subtract, inc=s_v)
                nvar = vop("vector", "tensor_scalar_add", r_3, r_3, LNEPS, inc=s_v)
                nln = vop("scalar", "activation", r_4, r_3, AF.Ln,
                          waits=[(s_v, nvar)], inc=s_sc)
                nr = vop("scalar", "activation", r_r, r_4, AF.Exp,
                         waits=[(s_sc, nln)], inc=s_sc, kwargs={"scale": -0.5})
                bcast(r_m, bc_m[:, :], 4, waits=[(s_sc, nr)])
                nbr = bcast(r_r, bc_r[:, :], 5)
                napply = 0
                for fc in range(NC16):
                    vop("vector", "tensor_tensor", f2[:, 0, :], x_t[:, fc, :],
                        bc_m[:, :], ALU.subtract,
                        waits=[(s_v, nbr)] if fc == 0 else (), inc=s_v)
                    v2 = vop("vector", "tensor_tensor", f2[:, 1, :], f2[:, 0, :],
                             bc_r[:, :], ALU.mult,
                             waits=[(s_sc, napply)] if napply else (), inc=s_v)
                    aw = [(s_v, v2)]
                    if first:
                        aw.append((s_v, n_colmath))
                        first = False
                    napply = vop("scalar", "activation", h_b[:, fc, tsl], f2[:, 1, :],
                                 AF.Identity, waits=aw, inc=s_sc,
                                 kwargs={"scale": col(cols, co_scale + fc),
                                         "bias": col(cols, co_shift + fc)})
            if tapname in taps:
                for tt4 in range(NT):
                    tsl = slice(tt4 * TT, (tt4 + 1) * TT)
                    for fc in range(NC16):
                        v = vop("vector", "tensor_copy", vf(8192, TT), h_b[:, fc, tsl],
                                waits=[(s_stg, s_stg.n), (s_sc, s_sc.n)], inc=s_v)
                        dma("gpsimd", d_tap[tapname][:, fc, tsl], vf(8192, TT),
                            s_stg, waits=[(s_v, v)])

        ln_phase(d_x, CO_ONECA, CO_SA, "h")
        barrier()

        U_tm = vb(0, 16, 768)
        U_fm = vb(12288, 6, 2048)
        wmv = vb(24576, 16, 512)
        a_tm = vb(32768, 16, 256)
        a_fm = vb(36864, 2, 2048)
        wmbrhs = vb(32768, 16, 512)
        kfm = vb(40960, 2048)
        roll = vb(40960, 2048)
        rollB = vb(43008, 2048)
        wkvb = vb(43008, 16, 128)
        mbt = vb(43008, 2, 512)
        v_tm = vb(45056, 16, 128)
        t1s = fsc[:, 0:512]
        stg_ev = [fsc[:, 512:1024], fsc[:, 1024:1536]]
        vstg = fsc[:, 1024:1536].bitcast(BF16)
        vtr_done = [0, 0]

        def roll_dmas(src2, dst, jr, half, sem, waits, queue="scalar"):
            dma(queue, dst[jr:128], src2[:, half, :][0:128 - jr, :], sem, waits=waits)
            return dma(queue, dst[0:jr], src2[:, 1 - half, :][128 - jr:128, :], sem,
                       waits=waits)

        arin_stores = 0
        n_ar_j = [0] * 4
        prev_cols_user = 0
        prev_mbt_users = [(s_mm, 0)]

        for j in range(4):
            n_wka = dma("gpsimd", wkvb[:, :, :], d_wk[:, :, j * 128:(j + 1) * 128],
                        s_wk, waits=prev_mbt_users)
            ev_k = []
            for t4 in range(NT):
                tsl = slice(t4 * TT, (t4 + 1) * TT)
                b = t4 % 2
                for fc in range(NC16):
                    mm(ps[b][:, :], wkvb[:, fc, :], h_b[:, fc, tsl],
                       start=(fc == 0), stop=(fc == NC16 - 1),
                       waits=[(s_wk, n_wka), bank_free[b]] if fc == 0 else ())
                nk = tmark()
                vop("scalar", "activation", kfm[:, tsl], ps[b][:, :], AF.Identity,
                    waits=[(s_mm, nk)], inc=s_sc,
                    kwargs={"bias": col(cols, CO_BK + j)})
                vop("scalar", "activation", a_fm[:, 0, tsl], ps[b][:, :], AF.Relu,
                    inc=s_sc, kwargs={"bias": col(cols, CO_BK + j)})
                nev = vop("scalar", "activation", a_fm[:, 1, tsl], ps[b][:, :], AF.Relu,
                          inc=s_sc, kwargs={"scale": -1.0, "bias": col(cols, CO_NBK + j)})
                bank_free[b] = (s_sc, nev)
                ev_k.append(nev)
            n_wkb = dma("gpsimd", wkvb[:, :, :], d_wv[:, :, j * 128:(j + 1) * 128],
                        s_wk, waits=[(s_mm, s_mm.n)])
            vev = []
            for t4 in range(NT):
                tsl = slice(t4 * TT, (t4 + 1) * TT)
                b = 2 + t4 % 2
                for fc in range(NC16):
                    mm(ps[b][:, :], wkvb[:, fc, :], h_b[:, fc, tsl],
                       start=(fc == 0), stop=(fc == NC16 - 1),
                       waits=[(s_wk, n_wkb), bank_free[b]] if fc == 0 else ())
                nv = tmark()
                vsl = vstg[:, (t4 % 2) * 512:(t4 % 2) * 512 + 512]
                ev = vop("vector", "tensor_scalar", vsl, ps[b][:, :],
                         col(cols, CO_BV + j), None, ALU.add,
                         waits=[(s_mm, nv), (s_mm, vtr_done[t4 % 2]),
                                (s_st, arin_stores)], inc=s_v)
                bank_free[b] = (s_v, ev)
                for tc4 in range(4):
                    tc = t4 * 4 + tc4
                    tr(psT[:, 0:128], vsl[:, tc4 * 128:(tc4 + 1) * 128], identB[:, :],
                       waits=[(s_v, ev), bank_free[7]])
                    ntr = tmark()
                    nevt = vop("scalar", "activation", v_tm[:, tc, :], psT[:, 0:128],
                               AF.Identity, waits=[(s_mm, ntr)], inc=s_sc)
                    bank_free[7] = (s_sc, nevt)
                vtr_done[t4 % 2] = s_mm.n
                vev.append(s_sc.n)
            for tc in range(16):
                t4 = tc // 4
                tr(psT[:, 128:256], kfm[:, tc * 128:(tc + 1) * 128], identB[:, :],
                   waits=[(s_sc, ev_k[t4]), bank_free[7]])
                ntr = tmark()
                vop("scalar", "activation", a_tm[:, tc, 0:128], psT[:, 128:256], AF.Relu,
                    waits=[(s_mm, ntr)], inc=s_sc)
                nevt = vop("scalar", "activation", a_tm[:, tc, 128:256], psT[:, 128:256],
                           AF.Relu, inc=s_sc, kwargs={"scale": -1.0})
                bank_free[7] = (s_sc, nevt)
            n_atm = s_sc.n
            for jr in (1, 2, 3):
                base = (jr - 1) * 256
                vop("vector", "tensor_tensor", U_tm[:, :, base + jr:base + 256],
                    a_tm[:, :, jr:256], a_tm[:, :, 0:256 - jr], ALU.mult,
                    waits=[(s_sc, n_atm)], inc=s_v)
                vop("vector", "tensor_tensor", U_tm[:, :, base:base + jr],
                    a_tm[:, :, 0:jr], a_tm[:, :, 256 - jr:256], ALU.mult, inc=s_v)
            n_utm = s_v.n
            for tc in range(16):
                vop("scalar", "activation", vb(24576, 768), U_tm[:, tc, :],
                    AF.Square,
                    waits=[(s_v, n_utm), (s_mm, prev_cols_user)] if tc == 0 else (),
                    inc=s_sc, kwargs={"accum_out": col(kcols, KC_SS + tc)})
            n_ss = s_sc.n
            ufm_done = []
            roll_free = [0, 0]
            pend = {}

            def issue_kroll(uc):
                jr, half = uc // 2 + 1, uc % 2
                slot = roll if uc % 2 == 0 else rollB
                sm = s_qrA if uc % 2 == 0 else s_qrB
                w = [(s_sc, ev_k[3]), (s_v, roll_free[uc % 2]), (s_mm, s_mm.n)]
                pend[uc] = (sm, roll_dmas(a_fm, slot, jr, half, sm, w,
                                          queue="gpsimd"))

            issue_kroll(0)
            issue_kroll(1)
            prev_tt = s_v.n
            for uchunk in range(6):
                slot = roll if uchunk % 2 == 0 else rollB
                sm, nr2 = pend[uchunk]
                prev_tt = vop("vector", "tensor_tensor", U_fm[:, uchunk, :],
                              a_fm[:, uchunk % 2, :], slot, ALU.mult,
                              waits=[(sm, nr2)], inc=s_v)
                roll_free[uchunk % 2] = prev_tt
                ufm_done.append(prev_tt)
                if uchunk + 2 < 6:
                    issue_kroll(uchunk + 2)
            for tc in range(16):
                for c6 in range(6):
                    w0 = [(s_v, ufm_done[c6])] if tc == 0 else []
                    if tc == 0 and c6 == 0:
                        w0.append(bank_free[6])
                        w0.append((s_nb, n_colsb_ld))
                    mm(psN[:, tc * 4:(tc + 1) * 4],
                       U_fm[:, c6, tc * 128:(tc + 1) * 128],
                       colsb[:, c6 * 16 + 4 * j:c6 * 16 + 4 * j + 4],
                       start=(c6 == 0), stop=(c6 == 5),
                       waits=w0)
            n_dn = tmark()
            ndn = vop("vector", "tensor_copy",
                      kcols[:, KC_DN:KC_DN + 64].rearrange("p (a b) -> p a b", a=4),
                      psN[:, 0:64].rearrange("p (b a) -> p a b", b=16),
                      waits=[(s_mm, n_dn)], inc=s_v)
            bank_free[6] = (s_v, ndn)
            vop("vector", "tensor_scalar_max", kcols[:, KC_T1:KC_T1 + 16],
                kcols[:, KC_SS:KC_SS + 16], 1e-24, waits=[(s_sc, n_ss)], inc=s_v)
            nv1 = s_v.n
            vop("scalar", "activation", kcols[:, KC_T1:KC_T1 + 16],
                kcols[:, KC_T1:KC_T1 + 16], AF.Ln, waits=[(s_v, nv1)], inc=s_sc)
            nsc1 = vop("scalar", "activation", kcols[:, KC_T1:KC_T1 + 16],
                       kcols[:, KC_T1:KC_T1 + 16], AF.Exp, inc=s_sc,
                       kwargs={"scale": -0.5})
            vop("vector", "tensor_scalar_min", kcols[:, KC_RINV:KC_RINV + 16],
                kcols[:, KC_T1:KC_T1 + 16], 1e12, waits=[(s_sc, nsc1)], inc=s_v)
            for i in range(4):
                vop("vector", "tensor_tensor",
                    kcols[:, KC_DEN + i * 16:KC_DEN + (i + 1) * 16],
                    kcols[:, KC_DN + i * 16:KC_DN + (i + 1) * 16],
                    kcols[:, KC_RINV:KC_RINV + 16], ALU.mult, inc=s_v)
            vop("vector", "tensor_scalar_add", kcols[:, KC_DEN:KC_DEN + 64],
                kcols[:, KC_DEN:KC_DEN + 64], EPS, inc=s_v)
            vop("vector", "tensor_tensor", kcols[:, KC_T2:KC_T2 + 16],
                kcols[:, KC_RINV:KC_RINV + 16], kcols[:, KC_RINV:KC_RINV + 16],
                ALU.mult, inc=s_v)
            vop("vector", "tensor_tensor", kcols[:, KC_T2:KC_T2 + 16],
                kcols[:, KC_T2:KC_T2 + 16], kcols[:, KC_SS:KC_SS + 16], ALU.mult,
                inc=s_v)
            vop("vector", "tensor_scalar_max", kcols[:, KC_T2:KC_T2 + 16],
                kcols[:, KC_T2:KC_T2 + 16], 1e-12, inc=s_v)
            vop("vector", "reciprocal", kcols[:, KC_T2:KC_T2 + 16],
                kcols[:, KC_T2:KC_T2 + 16], inc=s_v)
            kcb_v = kcolsb[:, :].rearrange("p (a b) -> p a b", a=16)
            for i in range(4):
                vop("vector", "tensor_tensor", kcols[:, KC_T1:KC_T1 + 16],
                    kcols[:, KC_DEN + i * 16:KC_DEN + (i + 1) * 16],
                    kcols[:, KC_T2:KC_T2 + 16], ALU.mult, inc=s_v)
                vop("vector", "tensor_scalar", kcols[:, KC_T1:KC_T1 + 16],
                    kcols[:, KC_T1:KC_T1 + 16], -1.0, 1.0, ALU.mult, ALU.add, inc=s_v)
                vop("vector", "tensor_scalar_max", kcols[:, KC_T1:KC_T1 + 16],
                    kcols[:, KC_T1:KC_T1 + 16], 0.0, inc=s_v)
                vop("vector", "tensor_scalar_min", kcols[:, KC_T1:KC_T1 + 16],
                    kcols[:, KC_T1:KC_T1 + 16], 1.0, inc=s_v)
                vop("vector", "tensor_tensor", kcols[:, KC_T1:KC_T1 + 16],
                    kcols[:, KC_T1:KC_T1 + 16], kcols[:, KC_RINV:KC_RINV + 16],
                    ALU.mult, inc=s_v)
                vop("vector", "tensor_copy", kcb_v[:, :, i], kcols[:, KC_T1:KC_T1 + 16],
                    inc=s_v)
            vop("vector", "reciprocal", kcols[:, KC_NS1:KC_NS1 + 64],
                kcols[:, KC_DEN:KC_DEN + 64], inc=s_v)
            for i in range(4):
                vop("vector", "tensor_tensor",
                    kcols[:, KC_NS1 + i * 16:KC_NS1 + (i + 1) * 16],
                    kcols[:, KC_NS1 + i * 16:KC_NS1 + (i + 1) * 16],
                    kcols[:, KC_RINV:KC_RINV + 16], ALU.mult, inc=s_v)
            n_cm = vop("vector", "tensor_scalar_mul", kcols[:, KC_NS1:KC_NS1 + 64],
                       kcols[:, KC_NS1:KC_NS1 + 64], -1.0, inc=s_v)

            n_mbrdy = dma("gpsimd", wmbrhs[:, :, :],
                          d_wmb[:, :, j * 512:(j + 1) * 512], s_wmb,
                          waits=[(s_v, ufm_done[5]), (s_sc, n_ss)])
            n_brow = dma("gpsimd", browb[:, :], d_bmb[0:1, j * 512:(j + 1) * 512], s_br,
                         waits=[(s_mm, s_mm.n)])

            wmv_marks = [0] * 16
            mbt_free = [roll_free[1], roll_free[1]]
            for tc in range(16):
                tsl128 = slice(tc * 128, (tc + 1) * 128)
                bmb_ = tc % 2
                bnm = 2 + tc % 2
                for fc in range(NC16):
                    mm(ps[bmb_][:, :], h_b[:, fc, tsl128], wmbrhs[:, fc, :],
                       start=(fc == 0), stop=False,
                       waits=[(s_wmb, n_mbrdy), bank_free[bmb_]] if fc == 0 else ())
                mm(ps[bmb_][:, :], ones_row[:, :], browb[0:1, :],
                   start=False, stop=True,
                   waits=[(s_br, n_brow)] if tc == 0 else ())
                nmb = tmark()
                nmbe = vop("scalar", "activation", mbt[:, tc % 2, :], ps[bmb_][:, :],
                           AF.Sigmoid,
                           waits=[(s_mm, nmb), (s_v, mbt_free[tc % 2])], inc=s_sc)
                bank_free[bmb_] = (s_sc, nmbe)
                for c6 in range(6):
                    mm(ps[bnm][:, :], U_fm[:, c6, tsl128], memb[:, j * 6 + c6, :],
                       start=(c6 == 0), stop=(c6 == 5),
                       waits=[bank_free[bnm], (s_mb, n_memcast)] if c6 == 0 else ())
                nnum = tmark()
                for i in range(4):
                    vop("vector", "scalar_tensor_tensor",
                        t1s[:, i * 128:(i + 1) * 128], ps[bnm][:, i * 128:(i + 1) * 128],
                        col(kcols, KC_NS1 + i * 16 + tc), v_tm[:, tc, :],
                        ALU.mult, ALU.add,
                        waits=[(s_mm, nnum), (s_v, n_cm), (s_sc, vev[-1])] if i == 0 else (),
                        inc=s_v)
                nt1 = s_v.n
                bank_free[bnm] = (s_v, nt1)
                for i in range(4):
                    vop("vector", "scalar_tensor_tensor",
                        wmv[:, tc, i * 128:(i + 1) * 128], t1s[:, i * 128:(i + 1) * 128],
                        col(kcols, KC_RINV + tc), mbt[:, tc % 2, i * 128:(i + 1) * 128],
                        ALU.mult, ALU.mult,
                        waits=[(s_sc, nmbe)] if i == 0 else (), inc=s_v)
                mbt_free[tc % 2] = s_v.n
                wmv_marks[tc] = s_v.n
            n_wmv = s_v.n
            prev_mbt_users = [(s_v, n_wmv), (s_sc, s_sc.n)]

            for c6 in range(6):
                for tc in range(16):
                    w0 = []
                    if c6 == 0:
                        w0.append((s_v, wmv_marks[tc]))
                    if tc == 0:
                        w0.append(bank_free[c6])
                    mm(ps[c6][:, :], U_tm[:, tc, c6 * 128:(c6 + 1) * 128],
                       wmv[:, tc, :], start=(tc == 0), stop=(tc == 15), waits=w0)
                    mm(psN[:, c6 * 4:(c6 + 1) * 4],
                       U_tm[:, tc, c6 * 128:(c6 + 1) * 128],
                       kcb_v[:, tc, :], start=(tc == 0), stop=(tc == 15),
                       waits=[bank_free[6]] if (tc == 0 and c6 == 0) else ())
            n_delta = tmark()
            prev_cols_user = n_delta
            for c6 in range(6):
                sl = stg_ev[c6 % 2]
                nev = vop("vector", "tensor_copy", sl, ps[c6][:, :],
                          waits=[(s_mm, n_delta), (s_st, arin_stores)], inc=s_v)
                bank_free[c6] = (s_v, nev)
                arin_stores = dma("sync",
                                  d_arin[(j * 6 + c6) * 128:(j * 6 + c6 + 1) * 128, :],
                                  sl, s_st, waits=[(s_v, nev)])
            ndne = vop("vector", "tensor_copy",
                       dnacc[:, :].rearrange("p (a b) -> p a b", a=6)[:, :, 4 * j:4 * j + 4],
                       psN[:, 0:24].rearrange("p (a b) -> p a b", a=6),
                       waits=[(s_mm, n_delta)], inc=s_v)
            bank_free[6] = (s_v, ndne)

            if "kdbg" in taps and j == 0:
                v = vop("vector", "tensor_copy", fsc[:, 0:256], kcols[:, :], inc=s_v)
                dma("gpsimd", d_tap["kdbg"][:, 0:256], fsc[:, 0:256], s_stg,
                    waits=[(s_v, v)])

            def f_arj(e, w=arin_stores, jj=j):
                e.wait_ge(s_st.h, w)
                e.collective_compute(
                    "AllReduce", ALU.add,
                    replica_groups=[list(range(8))],
                    ins=[d_arin[jj * 768:(jj + 1) * 768, :]],
                    outs=[d_arout[jj * 768:(jj + 1) * 768, :]],
                ).then_inc(s_cc.h, 1)
            emit("gpsimd", f_arj)
            s_cc.n += 1
            n_ar_j[j] = s_cc.n

        dma("sync", d_arin[3072:3096, :], dnacc[:, :], s_std, waits=[(s_v, s_v.n)])

        def f_ar(e, w=s_std.n):
            e.wait_ge(s_std.h, w)
            e.collective_compute(
                "AllReduce", ALU.add,
                replica_groups=[list(range(8))],
                ins=[d_arin[3072:3096, :]],
                outs=[d_arout[3072:3096, :]],
            ).then_inc(s_cc.h, 1)
        emit("gpsimd", f_ar)
        s_cc.n += 1
        n_ar = s_cc.n

        prevup = s_v.n
        for idx in range(24):
            sl = 1024 * (idx % 2)
            na = dma("sync", wstg[:, sl:sl + 512], d_arout[idx * 128:(idx + 1) * 128, :],
                     s_x, waits=[(s_cc, n_ar_j[idx // 6]), (s_v, prevup)])
            nb = dma("sync", wstg[:, sl + 512:sl + 1024],
                     d_mem[idx // 6, idx % 6, :, :], s_x)
            prevup = vop("vector", "tensor_tensor", memb[:, idx, :],
                         wstg[:, sl:sl + 512], wstg[:, sl + 512:sl + 1024], ALU.add,
                         waits=[(s_x, nb), (s_mm, s_mm.n)], inc=s_v)

        n_na = dma("sync", fsc[:, 0:96], d_arout[3072:3096, :], s_x,
                   waits=[(s_cc, n_ar), (s_v, s_v.n)])
        vop("vector", "tensor_tensor", nrmf[:, :], nrmf[:, :], fsc[:, 0:96], ALU.add,
            waits=[(s_x, n_na)], inc=s_v)
        n_newmem = vop("vector", "tensor_copy", colsb[:, 96:192], nrmf[:, :], inc=s_v)

        if "newmem" in taps:
            for idx in range(24):
                v = vop("vector", "tensor_copy", fsc[:, 512:1024], memb[:, idx, :],
                        waits=[(s_v, n_newmem), (s_stg, s_stg.n)], inc=s_v)
                dma("gpsimd", d_tap["newmem"][idx, :, :], fsc[:, 512:1024], s_stg,
                    waits=[(s_v, v)])
            v = vop("vector", "tensor_copy", fsc[:, 0:96], nrmf[:, :], inc=s_v)
            dma("gpsimd", d_tap["newnrm"][:, :], fsc[:, 0:96], s_stg, waits=[(s_v, v)])

        U_q = vb(0, 6, 2048)
        qa_fm = vb(12288, 2, 2, 2048)
        qroll = vb(20480, 2048)
        qrollB = vb(22528, 2048)
        qdust = fsc[:, 0:512].bitcast(BF16).rearrange("p (a b) -> p a b", a=2)
        wqb = vb(24576, 16, 128)
        wgb = vb(26624, 16, 128)
        gate_b = vb(28672, 2, 2048)
        xqB = vf(16384, 2048)
        bcD = vf(18432, 2048)
        xqA = vf(20480, 2048)
        tmpq = vf(22528, 512)
        ssT = vf(23040, 16)
        deT = vf(23056, 16)
        fT = vf(23072, 16)

        prev_projq = 0
        prev_projg = 0
        qa_free = [0, 0]
        gate_free = [0, 0]
        stA = {}

        qw_pend = {}

        def issue_qw(h):
            nwq = dma("gpsimd", wqb[:, :, :], d_wq[:, :, h * 128:(h + 1) * 128],
                      s_wq, waits=[(s_mm, stA.get(h - 1, (0,) * 9)[0])])
            nwg = dma("gpsimd", wgb[:, :, :], d_wg[:, :, h * 128:(h + 1) * 128],
                      s_wg, waits=[(s_mm, stA.get(h - 1, (0,) * 9)[1])])
            qw_pend[h] = (nwq, nwg)

        def q_stage_a(h):
            sl = h % 2
            nwq, nwg = qw_pend[h]
            nqa = 0
            ngev = 0
            lastq = 0
            lastg = 0
            for t4 in range(NT):
                tsl = slice(t4 * TT, (t4 + 1) * TT)
                b = t4 % 2
                for fc in range(NC16):
                    mm(ps[b][:, :], wqb[:, fc, :], h_b[:, fc, tsl],
                       start=(fc == 0), stop=(fc == NC16 - 1),
                       waits=[(s_wq, nwq), bank_free[b],
                              (s_v, qa_free[sl])] if fc == 0 else ())
                nq = tmark()
                lastq = nq
                vop("scalar", "activation", qa_fm[:, sl, 0, tsl], ps[b][:, :], AF.Relu,
                    waits=[(s_mm, nq)], inc=s_sc, kwargs={"bias": col(cols, CO_BQ + h)})
                nqa = vop("scalar", "activation", qa_fm[:, sl, 1, tsl], ps[b][:, :],
                          AF.Relu, inc=s_sc,
                          kwargs={"scale": -1.0, "bias": col(cols, CO_NBQ + h)})
                bank_free[b] = (s_sc, nqa)
                bg_ = 2 + t4 % 2
                for fc in range(NC16):
                    mm(ps[bg_][:, :], wgb[:, fc, :], h_b[:, fc, tsl],
                       start=(fc == 0), stop=(fc == NC16 - 1),
                       waits=[(s_wg, nwg), bank_free[bg_],
                              (s_v, gate_free[sl])] if fc == 0 else ())
                ng = tmark()
                lastg = ng
                ngev = vop("scalar", "activation", gate_b[:, sl, tsl], ps[bg_][:, :],
                           AF.Sigmoid, waits=[(s_mm, ng)], inc=s_sc,
                           kwargs={"bias": col(cols, CO_BG + h)})
                bank_free[bg_] = (s_sc, ngev)
            stA[h] = (lastq, lastg, nqa, ngev)

        qroll_free = [0, 0]
        xq_slot_free = [0, 0]
        xq_store_mark = [0, 0]
        dustA_mark = [0]
        dustB_mark = [0]
        qpend_g = {}

        def issue_qroll(h, uc):
            sl = h % 2
            nqa = stA[h][2]
            jr, half = uc // 2 + 1, uc % 2
            slot = qroll if uc % 2 == 0 else qrollB
            sm = s_qrA if uc % 2 == 0 else s_qrB
            w = [(s_sc, nqa), (s_v, qroll_free[uc % 2])]
            qpend_g[(h, uc)] = (sm, roll_dmas(qa_fm[:, sl], slot, jr, half, sm, w,
                                              queue="gpsimd"))

        def q_stage_b(h):
            sl = h % 2
            j, i = h // 4, h % 4
            lastq, lastg, nqa, ngev = stA[h]
            if (h, 0) not in qpend_g:
                issue_qroll(h, 0)
                issue_qroll(h, 1)
            uq_done = []
            for uchunk in range(6):
                slot = qroll if uchunk % 2 == 0 else qrollB
                sm2, nr2 = qpend_g[(h, uchunk)]
                ptt = vop("vector", "tensor_tensor", U_q[:, uchunk, :],
                          qa_fm[:, sl, uchunk % 2, :], slot, ALU.mult,
                          waits=[(sm2, nr2)], inc=s_v)
                qroll_free[uchunk % 2] = ptt
                uq_done.append(ptt)
                if uchunk + 2 < 6:
                    issue_qroll(h, uchunk + 2)
            qa_free[sl] = uq_done[-1]
            if h + 2 < 16:
                issue_qw(h + 2)
            xqT = xqA if sl == 0 else xqB
            s_xqs = s_xq2[sl]
            s_stq = s_stx if sl == 0 else s_sty
            nxq = dma("sync", xqT[:, :], d_x[:, h, :], s_xqs,
                      waits=[(s_v, xq_slot_free[sl]), (s_stq, xq_store_mark[sl])])
            dustA = fsc[:, 0:1024].bitcast(BF16)
            dustB = wstg[:, 0:1024].bitcast(BF16)
            nsq = [None] * 6

            def issue_square(c6):
                d = dustA if c6 % 2 == 0 else dustB
                w = [(s_v, uq_done[c6])]
                if c6 % 2 == 0:
                    w.append((s_mm, dustA_mark[0]))
                    nsq[c6] = (s_sc, vop("scalar", "activation", d, U_q[:, c6, :],
                                         AF.Square, waits=w, inc=s_sc))
                else:
                    w.append((s_mm, dustB_mark[0]))
                    nsq[c6] = (s_v, vop("vector", "tensor_tensor", d, U_q[:, c6, :],
                                        U_q[:, c6, :], ALU.mult, waits=w, inc=s_v))
            for t4 in range(NT):
                tsl = slice(t4 * TT, (t4 + 1) * TT)
                bd = 4 + t4 % 2
                for c6 in range(6):
                    mm(ps[bd][0:1, :],
                       colsb[:, 96 + c6 * 16 + h:96 + c6 * 16 + h + 1],
                       U_q[:, c6, tsl], start=(c6 == 0), stop=(c6 == 5),
                       waits=([(s_v, n_newmem), bank_free[bd], (s_v, uq_done[0])]
                              if c6 == 0 else
                              [(s_v, uq_done[c6])] if t4 == 0 else ()))
                nsd = tmark()
                nev = vop("scalar", "activation", bcD[0:1, tsl], ps[bd][0:1, :],
                          AF.Identity, waits=[(s_mm, nsd)], inc=s_sc,
                          kwargs={"bias": 0.0})
                bank_free[bd] = (s_sc, nev)
            nscat_d = dma("gpsimd", deT[:, :], bcD[0:1, :], s_qd,
                          waits=[(s_sc, s_sc.n)])
            issue_square(0)
            issue_square(1)
            nss_t4 = [0] * NT
            for c6 in range(6):
                d = dustA if c6 % 2 == 0 else dustB
                for t4 in range(NT):
                    tsl = slice(t4 * TT, (t4 + 1) * TT)
                    nss_t4[t4] = mm(ps[t4][0:1, :], ones_col[:, :], d[:, tsl],
                                    start=(c6 == 0), stop=(c6 == 5),
                                    waits=([nsq[c6]] +
                                           ([bank_free[t4]] if c6 == 0 else [])),
                                    inc=(s_mm if (t4 == NT - 1 or c6 == 5)
                                         else None))
                if c6 % 2 == 0:
                    dustA_mark[0] = s_mm.n
                else:
                    dustB_mark[0] = s_mm.n
                if c6 + 2 < 6:
                    issue_square(c6 + 2)
            for t4 in range(NT):
                tsl = slice(t4 * TT, (t4 + 1) * TT)
                nev = vop("scalar", "activation", bcD[0:1, tsl], ps[t4][0:1, :],
                          AF.Identity,
                          waits=[(s_mm, nss_t4[t4]), (s_qd, nscat_d)], inc=s_sc,
                          kwargs={"bias": 0.0})
                bank_free[t4] = (s_sc, nev)
            nscat_s = dma("gpsimd", ssT[:, :], bcD[0:1, :], s_qss,
                          waits=[(s_sc, s_sc.n)])
            vop("vector", "tensor_scalar_max", ssT[:, :], ssT[:, :], 1e-24,
                waits=[(s_qss, nscat_s)], inc=s_v)
            nv1 = s_v.n
            vop("scalar", "activation", ssT[:, :], ssT[:, :], AF.Ln,
                waits=[(s_v, nv1)], inc=s_sc)
            nsc2 = vop("scalar", "activation", ssT[:, :], ssT[:, :], AF.Exp,
                       inc=s_sc, kwargs={"scale": -0.5})
            vop("vector", "tensor_scalar_min", ssT[:, :], ssT[:, :], 1e12,
                waits=[(s_sc, nsc2)], inc=s_v)
            vop("vector", "tensor_tensor", deT[:, :], deT[:, :], ssT[:, :],
                ALU.mult, inc=s_v)
            vop("vector", "tensor_scalar_add", deT[:, :], deT[:, :], EPS, inc=s_v)
            vop("vector", "reciprocal", deT[:, :], deT[:, :], inc=s_v)
            nf = vop("vector", "tensor_tensor", fT[:, :], ssT[:, :], deT[:, :],
                     ALU.mult, inc=s_v)
            ngat = dma("gpsimd", bcD[0:1, :], fT[:, :], s_qg, waits=[(s_v, nf)])
            nna4 = [0] * NT
            for t4 in range(NT):
                tsl = slice(t4 * TT, (t4 + 1) * TT)
                for c6 in range(6):
                    mm(ps[t4][:, :], memb[:, j * 6 + c6, i * 128:(i + 1) * 128],
                       U_q[:, c6, tsl], start=(c6 == 0), stop=(c6 == 5),
                       waits=[bank_free[t4], (s_v, n_newmem)] if c6 == 0 else ())
                nna4[t4] = tmark()
            nbcf = [0] * NT
            for t4 in range(NT):
                tsl = slice(t4 * TT, (t4 + 1) * TT)
                nbcf[t4] = bcast(bcD[0:1, tsl], bcD[:, tsl], 4 + t4 % 2,
                                 waits=[(s_qg, ngat)])
            ngm = 0
            for t4 in range(NT):
                tsl = slice(t4 * TT, (t4 + 1) * TT)
                nna = nna4[t4]
                vop("vector", "tensor_tensor", tmpq[:, :], ps[t4][:, :], bcD[:, tsl],
                    ALU.mult, waits=[(s_mm, nna), (s_v, nbcf[t4])], inc=s_v)
                nc1 = s_v.n
                bank_free[t4] = (s_v, nc1)
                ngm = vop("vector", "tensor_tensor", tmpq[:, :], tmpq[:, :],
                          gate_b[:, sl, tsl], ALU.mult, inc=s_v)
                vop("vector", "tensor_tensor", tmpq[:, :], tmpq[:, :], h_b[:, h, tsl],
                    ALU.add, inc=s_v)
                nxn = vop("vector", "scalar_tensor_tensor", xqT[:, tsl], tmpq[:, :],
                          col(cols, CO_GA + h), xqT[:, tsl], ALU.mult, ALU.add,
                          waits=[(s_xqs, nxq)], inc=s_v)
                dma("scalar", d_xnew[:, h, tsl], xqT[:, tsl], s_stq,
                    waits=[(s_v, nxn)])
            gate_free[sl] = ngm
            xq_slot_free[sl] = s_v.n
            xq_store_mark[sl] = s_stq.n

        issue_qw(0)
        q_stage_a(0)
        issue_qw(1)
        q_stage_a(1)
        issue_qroll(0, 0)
        issue_qroll(0, 1)
        for h in range(16):
            q_stage_b(h)
            if h + 1 < 16:
                issue_qroll(h + 1, 0)
                issue_qroll(h + 1, 1)
            if h + 2 < 16:
                q_stage_a(h + 2)
        n_xnew_st = s_stx.n
        n_xnew_st2 = s_sty.n

        if "xnew" in taps:
            dma("gpsimd", d_tap["xnew"][:, :, :], d_xnew[:, :, :], s_stg,
                waits=[(s_stx, n_xnew_st), (s_sty, n_xnew_st2)])

        barrier()

        ln_phase(d_xnew, CO_ONECB, CO_SB, "h2",
                 first_waits=[(s_stx, n_xnew_st), (s_sty, n_xnew_st2)])
        barrier()

        act = vb(0, 15, 2048)
        wgateb = vb(30720, 2, 16, 128)
        wupb = vb(34816, 2, 16, 128)
        wdownb = vb(38912, 2, 16, 128)
        silu_t = vb(43008, 512)
        xnin = vf(21760, 2, 512)
        ystg = fsc[:, 0:512]
        yin = fsc[:, 512:1536].rearrange("p (a b) -> p a b", a=2)

        n_wga = [0, 0, 0]
        silu_free = 0
        gu_slot_mark = [[0, 0], [0, 0]]
        gu_loads = {}

        def issue_gu_load(o):
            sl = (o % 2)
            ng = dma("gpsimd", wgateb[:, sl, :, :],
                     d_wgate[:, :, o * 128:(o + 1) * 128], s_wga2[sl],
                     waits=[(s_mm, gu_slot_mark[0][sl])])
            nu = dma("gpsimd", wupb[:, sl, :, :],
                     d_wup[:, :, o * 128:(o + 1) * 128], s_wup2[sl],
                     waits=[(s_mm, gu_slot_mark[1][sl])])
            gu_loads[o] = (ng, nu)

        for gi, (gbase, gn) in enumerate(FFN_GROUPS):
            for ol in range(gn):
                o = gbase + ol
                if o not in gu_loads:
                    issue_gu_load(o)
                if o + 1 < INTER // 128 and (o + 1) not in gu_loads:
                    issue_gu_load(o + 1)
                nwg_, nwu_ = gu_loads[o]
                sl2 = o % 2
                for t4 in range(NT):
                    tsl = slice(t4 * TT, (t4 + 1) * TT)
                    bgt = t4 % 2
                    bup = 2 + t4 % 2
                    for fc in range(NC16):
                        mm(ps[bgt][:, :], wgateb[:, sl2, fc, :], h_b[:, fc, tsl],
                           start=(fc == 0), stop=(fc == NC16 - 1),
                           waits=[(s_wga2[sl2], nwg_), bank_free[bgt]] if fc == 0 else ())
                    ngm = tmark()
                    if t4 == NT - 1:
                        n_wga[0] = ngm
                        gu_slot_mark[0][sl2] = ngm
                    for fc in range(NC16):
                        mm(ps[bup][:, :], wupb[:, sl2, fc, :], h_b[:, fc, tsl],
                           start=(fc == 0), stop=(fc == NC16 - 1),
                           waits=[(s_wup2[sl2], nwu_), bank_free[bup]] if fc == 0 else ())
                    num_ = tmark()
                    if t4 == NT - 1:
                        n_wga[1] = num_
                        gu_slot_mark[1][sl2] = num_
                        if o + 2 < INTER // 128 and (o + 2) not in gu_loads:
                            issue_gu_load(o + 2)
                    nsl = vop("scalar", "activation", silu_t, ps[bgt][:, :], AF.Silu,
                              waits=[(s_mm, ngm), (s_v, silu_free)], inc=s_sc,
                              kwargs={"scale": col(cols, CO_GS + o)})
                    bank_free[bgt] = (s_sc, nsl)
                    nac = vop("vector", "scalar_tensor_tensor", act[:, ol, tsl],
                              ps[bup][:, :], col(cols, CO_US + o), silu_t,
                              ALU.mult, ALU.mult,
                              waits=[(s_mm, num_), (s_sc, nsl)], inc=s_v)
                    silu_free = nac
                    bank_free[bup] = (s_v, nac)
            n_act = s_v.n
            dn_slot_mark = [0, 0]
            dn_loads = {}

            def issue_dn_load(mc2):
                sl3 = mc2 % 2
                dn_loads[mc2] = dma(
                    "gpsimd", wdownb[:, sl3, 0:gn, :],
                    d_wdown[:, gbase:gbase + gn, mc2 * 128:(mc2 + 1) * 128],
                    s_wdn2[sl3], waits=[(s_mm, dn_slot_mark[sl3])])

            yin_mark = [0, 0]
            xnin_mark = [0, 0]
            yin_loads = {}

            def issue_yin(ti):
                mc2, t42 = ti // 4, ti % 4
                sl3 = ti % 2
                yin_loads[ti] = dma(
                    "scalar", yin[:, sl3, :],
                    d_y[:, mc2, t42 * TT:(t42 + 1) * TT], s_yi2[sl3],
                    waits=[(s_stg, n_ydone), (s_v, yin_mark[sl3])])

            n_ydone = s_stg.n if gi == 2 else 0
            for mc in range(NC16):
                if mc not in dn_loads:
                    issue_dn_load(mc)
                if mc + 1 < NC16 and (mc + 1) not in dn_loads:
                    issue_dn_load(mc + 1)
                nwd = dn_loads[mc]
                if gi == 2 and mc == 0:
                    issue_yin(0)
                    issue_yin(1)
                for t4 in range(NT):
                    tsl = slice(t4 * TT, (t4 + 1) * TT)
                    b = 4 + t4 % 2
                    for il in range(gn):
                        mm(ps[b][:, :], wdownb[:, mc % 2, il, :], act[:, il, tsl],
                           start=(il == 0), stop=(il == gn - 1),
                           waits=[(s_wdn2[mc % 2], nwd), (s_v, n_act), bank_free[b]] if il == 0 else ())
                    ndm = tmark()
                    if t4 == NT - 1:
                        n_wga[2] = ndm
                        dn_slot_mark[mc % 2] = ndm
                        if mc + 2 < NC16 and (mc + 2) not in dn_loads:
                            issue_dn_load(mc + 2)
                    if gi == 0:
                        nev = vop("vector", "tensor_copy", ystg, ps[b][:, :],
                                  waits=[(s_mm, ndm), (s_stg, s_stg.n)], inc=s_v)
                        bank_free[b] = (s_v, nev)
                        dma("gpsimd", d_y[:, mc, tsl], ystg, s_stg, waits=[(s_v, nev)])
                    elif gi == 1:
                        nev = vop("vector", "tensor_copy", ystg, ps[b][:, :],
                                  waits=[(s_mm, ndm), (s_stg, s_stg.n)], inc=s_v)
                        bank_free[b] = (s_v, nev)
                        dma("gpsimd", d_y[:, mc, tsl], ystg, s_stg,
                            waits=[(s_v, nev)], accum=True)
                    else:
                        nxn_ = dma("scalar", xnin[:, t4 % 2, :],
                                   d_xnew[:, mc, tsl], s_xi2[t4 % 2],
                                   waits=[(s_v, xnin_mark[t4 % 2]),
                                          (s_stx, n_xnew_st),
                                          (s_sty, n_xnew_st2)])
                        ti = mc * 4 + t4
                        nyi = yin_loads[ti]
                        nev = vop("vector", "tensor_tensor", ystg, ps[b][:, :],
                                  yin[:, ti % 2, :], ALU.add,
                                  waits=[(s_mm, ndm), (s_yi2[ti % 2], nyi),
                                         (s_stg, s_stg.n)], inc=s_v)
                        yin_mark[ti % 2] = nev
                        if ti + 2 < NC16 * 4:
                            issue_yin(ti + 2)
                        bank_free[b] = (s_v, nev)
                        nout = vop("vector", "scalar_tensor_tensor", ystg,
                                   ystg, col(cols, CO_GBDS + mc), xnin[:, t4 % 2, :],
                                   ALU.mult, ALU.add,
                                   waits=[(s_xi2[t4 % 2], nxn_)], inc=s_v)
                        xnin_mark[t4 % 2] = nout
                        dma("gpsimd", d_out[:, mc, tsl], ystg, s_stg,
                            waits=[(s_v, nout)])

        fin = s_stg.n
        emit("gpsimd", lambda e, t=fin: e.wait_ge(s_stg.h, t))
        barrier()

        with nc.Block() as block:
            @block.sync
            def _(e):
                for f in prog["sync"]:
                    f(e)

            @block.tensor
            def _(e):
                for f in prog["tensor"]:
                    f(e)

            @block.vector
            def _(e):
                for f in prog["vector"]:
                    f(e)

            @block.scalar
            def _(e):
                for f in prog["scalar"]:
                    f(e)

            @block.gpsimd
            def _(e):
                for f in prog["gpsimd"]:
                    f(e)

    return nc


def _layout_inputs(inputs):
    g = lambda k: np.asarray(inputs[k])
    x = g("L0_attn_hidden_states")[0]
    per_core = []
    shared = {}

    def chunk16(w):
        return np.ascontiguousarray(w.reshape(NC16, P, w.shape[1]).transpose(1, 0, 2))

    shared["wqt_l"] = chunk16(np.ascontiguousarray(g("L0_attn_Wq").T))
    shared["wkt_l"] = chunk16(np.ascontiguousarray(g("L0_attn_Wk").T))
    shared["wvt_l"] = chunk16(np.ascontiguousarray(g("L0_attn_Wv").T))
    shared["wgt_l"] = chunk16(np.ascontiguousarray(g("L0_attn_Wg").T))
    shared["wmbt_l"] = chunk16(np.ascontiguousarray(g("L0_attn_Wmb").T))
    shared["wgatet_l"] = chunk16(np.ascontiguousarray(g("L0_ffn_gate_w").T))
    shared["wupt_l"] = chunk16(np.ascontiguousarray(g("L0_ffn_up_w").T))
    wd = np.ascontiguousarray(g("L0_ffn_down_w").T)
    shared["wdownt_l"] = np.ascontiguousarray(wd.reshape(NI44, P, 2048).transpose(1, 0, 2))
    shared["bq_c"] = np.ascontiguousarray(g("L0_attn_bq").reshape(16, P).T)
    shared["bk_c"] = np.ascontiguousarray(g("L0_attn_bk").reshape(4, P).T)
    shared["bv_c"] = np.ascontiguousarray(g("L0_attn_bv").reshape(4, P).T)
    shared["bg_c"] = np.ascontiguousarray(g("L0_attn_bg").reshape(16, P).T)
    shared["bmb_r"] = np.ascontiguousarray(g("L0_attn_bmb").reshape(1, 2048))
    shared["mem_l"] = np.ascontiguousarray(
        g("L0_attn_memory").reshape(4, 4, 6, P, D).transpose(0, 2, 3, 1, 4).reshape(4, 6, P, 512))
    shared["nrm_l"] = np.ascontiguousarray(
        g("L0_attn_norm").reshape(16, 6, P).transpose(2, 1, 0).reshape(P, 96))
    shared["gs_c"] = np.ascontiguousarray(g("L0_ffn_gate_s").reshape(NI44, P).T)
    shared["us_c"] = np.ascontiguousarray(g("L0_ffn_up_s").reshape(NI44, P).T)
    shared["ds_c"] = np.ascontiguousarray(g("L0_ffn_down_s").reshape(16, P).T)
    shared["c_c"] = np.ascontiguousarray(g("L0_c").reshape(16, P).T)
    shared["ident_b"] = np.eye(P, dtype=ml_dtypes.bfloat16)
    shared["ident_f"] = np.eye(P, dtype=np.float32)
    shared["ones_col_b"] = np.ones((P, 1), dtype=ml_dtypes.bfloat16)
    shared["ones_row_b"] = np.ones((1, P), dtype=ml_dtypes.bfloat16)
    shared["ones_row_f"] = np.ones((1, P), dtype=np.float32)

    wmod = g("L0_wmod")
    for core in range(8):
        xs = np.ascontiguousarray(x[core * T:(core + 1) * T, :].T)
        m = dict(shared)
        m["x_l"] = np.ascontiguousarray(xs.reshape(NC16, P, T).transpose(1, 0, 2))
        m["wmod_sh_l"] = chunk16(
            np.ascontiguousarray(wmod[:, core * 1536:(core + 1) * 1536]))
        per_core.append(m)
    return per_core


def kernel(**inputs):
    in_maps = _layout_inputs(inputs)
    nc = build(taps=TAPS)
    res = run_bass_kernel_spmd(nc, in_maps, core_ids=list(range(8)))
    outs = []
    for r in res.results:
        o = r["out_l"]
        outs.append(o.transpose(1, 0, 2).reshape(2048, T).T)
    full = np.concatenate(outs, axis=0)[None]
    if TAPS:
        kernel.last_results = res
    return full.astype(np.float32)


# version 16 (final)
# speedup vs baseline: 1.0529x; 1.0007x over previous
import sys
if "/opt/trn_rl_repo" not in sys.path:
    sys.path.insert(0, "/opt/trn_rl_repo")
from contextlib import ExitStack

import numpy as np
import ml_dtypes

import concourse.bass as bass
import concourse.mybir as mybir
from concourse.bass_utils import run_bass_kernel_spmd

F32 = mybir.dt.float32
BF16 = mybir.dt.bfloat16
AF = mybir.ActivationFunctionType
ALU = mybir.AluOpType

P = 128
T = 2048
NT = 4
TT = 512
NC16 = 16
D = 128
EPS = 1e-8
LNEPS = 1e-5
INTER = 5632
NI44 = 44
FFN_GROUPS = [(0, 15), (15, 15), (30, 14)]

TAPS = set()

ENGINES = ("sync", "tensor", "vector", "scalar", "gpsimd")


class Sem:
    def __init__(self, es, nc, name):
        self.h = es.enter_context(nc.semaphore(name))
        self.n = 0


def build(taps=()):
    taps = set(taps)
    nc = bass.Bass("TRN2", target_bir_lowering=False)

    d_x = nc.dram_tensor("x_l", [P, NC16, T], F32, kind="ExternalInput").ap()
    d_wq = nc.dram_tensor("wqt_l", [P, NC16, 2048], F32, kind="ExternalInput").ap()
    d_wk = nc.dram_tensor("wkt_l", [P, NC16, 512], F32, kind="ExternalInput").ap()
    d_wv = nc.dram_tensor("wvt_l", [P, NC16, 512], F32, kind="ExternalInput").ap()
    d_wg = nc.dram_tensor("wgt_l", [P, NC16, 2048], F32, kind="ExternalInput").ap()
    d_wmb = nc.dram_tensor("wmbt_l", [P, NC16, 2048], F32, kind="ExternalInput").ap()
    d_bq = nc.dram_tensor("bq_c", [P, 16], F32, kind="ExternalInput").ap()
    d_bk = nc.dram_tensor("bk_c", [P, 4], F32, kind="ExternalInput").ap()
    d_bv = nc.dram_tensor("bv_c", [P, 4], F32, kind="ExternalInput").ap()
    d_bg = nc.dram_tensor("bg_c", [P, 16], F32, kind="ExternalInput").ap()
    d_bmb = nc.dram_tensor("bmb_r", [1, 2048], F32, kind="ExternalInput").ap()
    d_mem = nc.dram_tensor("mem_l", [4, 6, P, 512], F32, kind="ExternalInput").ap()
    d_nrm = nc.dram_tensor("nrm_l", [P, 96], F32, kind="ExternalInput").ap()
    d_wgate = nc.dram_tensor("wgatet_l", [P, NC16, INTER], F32, kind="ExternalInput").ap()
    d_wup = nc.dram_tensor("wupt_l", [P, NC16, INTER], F32, kind="ExternalInput").ap()
    d_wdown = nc.dram_tensor("wdownt_l", [P, NI44, 2048], F32, kind="ExternalInput").ap()
    d_gs = nc.dram_tensor("gs_c", [P, NI44], F32, kind="ExternalInput").ap()
    d_us = nc.dram_tensor("us_c", [P, NI44], F32, kind="ExternalInput").ap()
    d_ds = nc.dram_tensor("ds_c", [P, 16], F32, kind="ExternalInput").ap()
    d_wmod = nc.dram_tensor("wmod_sh_l", [P, NC16, 1536], F32, kind="ExternalInput").ap()
    d_c = nc.dram_tensor("c_c", [P, 16], F32, kind="ExternalInput").ap()
    d_identb = nc.dram_tensor("ident_b", [P, P], BF16, kind="ExternalInput").ap()
    d_identf = nc.dram_tensor("ident_f", [P, P], F32, kind="ExternalInput").ap()
    d_onescol = nc.dram_tensor("ones_col_b", [P, 1], BF16, kind="ExternalInput").ap()
    d_onesrow = nc.dram_tensor("ones_row_b", [1, P], BF16, kind="ExternalInput").ap()
    d_onesrowf = nc.dram_tensor("ones_row_f", [1, P], F32, kind="ExternalInput").ap()

    d_out = nc.dram_tensor("out_l", [P, NC16, T], F32, kind="ExternalOutput").ap()

    d_xnew = nc.dram_tensor("xnew_d", [P, NC16, T], F32).ap()
    d_y = nc.dram_tensor("y_d", [P, NC16, T], F32).ap()
    d_arin = nc.dram_tensor("ar_in", [3096, 512], F32).ap()
    d_arout = nc.dram_tensor("ar_out", [3096, 512], F32).ap()
    d_magi = nc.dram_tensor("modag_in", [12, P], F32).ap()
    d_mago = nc.dram_tensor("modag_out", [96, P], F32).ap()

    d_tap = {}
    if "h" in taps:
        d_tap["h"] = nc.dram_tensor("tap_h", [P, NC16, T], F32, kind="ExternalOutput").ap()
    if "xnew" in taps:
        d_tap["xnew"] = nc.dram_tensor("tap_xnew", [P, NC16, T], F32, kind="ExternalOutput").ap()
    if "h2" in taps:
        d_tap["h2"] = nc.dram_tensor("tap_h2", [P, NC16, T], F32, kind="ExternalOutput").ap()
    if "newmem" in taps:
        d_tap["newmem"] = nc.dram_tensor("tap_newmem", [24, P, 512], F32, kind="ExternalOutput").ap()
        d_tap["newnrm"] = nc.dram_tensor("tap_newnrm", [P, 96], F32, kind="ExternalOutput").ap()
    if "kdbg" in taps:
        d_tap["kdbg"] = nc.dram_tensor("tap_kdbg", [P, 2048], F32, kind="ExternalOutput").ap()

    prog = {e: [] for e in ENGINES}

    with ExitStack() as es:
        def sb(name, shape, dt):
            return es.enter_context(nc.sbuf_tensor(name, shape, dt))

        h_b = sb("h_b", [P, NC16, T], BF16)
        memb = sb("memb", [P, 24, 512], BF16)
        wstg = sb("wstg", [P, 2048], F32)
        AR = sb("AR", [P, 23552], F32)
        fsc = sb("fsc", [P, 1536], F32)
        cols = sb("cols", [P, 288], F32)
        kcols = sb("kcols", [P, 256], F32)
        colsb = sb("colsb", [P, 192], BF16)
        kcolsb = sb("kcolsb", [P, 64], BF16)
        dnacc = sb("dnacc", [P, 96], F32)
        nrmf = sb("nrmf", [P, 96], F32)
        browb = sb("browb", [1, 512], BF16)
        identB = sb("identB", [P, P], BF16)
        identF = sb("identF", [P, P], F32)
        ones_col = sb("ones_col", [P, 1], BF16)
        ones_row = sb("ones_row", [1, P], BF16)
        ones_rowf = sb("ones_rowf", [1, P], F32)
        magr = sb("magr", [96, P], F32)
        csil = sb("csil", [P, 16], F32)
        csilb = sb("csilb", [P, 16], BF16)
        modsh = sb("modsh", [P, 12], F32)
        modrow = sb("modrow", [12, P], F32)
        gtmp = sb("gtmp", [P, 48], F32)

        ARf = AR[:, :]
        ARb = AR[:, :].bitcast(BF16)

        def vf(off, *dims):
            n = int(np.prod(dims))
            v = ARf[:, off:off + n]
            if len(dims) > 1:
                kw = {f"d{i}": d for i, d in enumerate(dims[:-1])}
                pat = "p (" + " ".join(f"d{i}" for i in range(len(dims))) + ") -> p " + \
                    " ".join(f"d{i}" for i in range(len(dims)))
                v = v.rearrange(pat, **kw)
            return v

        def vb(off, *dims):
            n = int(np.prod(dims))
            v = ARb[:, off:off + n]
            if len(dims) > 1:
                kw = {f"d{i}": d for i, d in enumerate(dims[:-1])}
                pat = "p (" + " ".join(f"d{i}" for i in range(len(dims))) + ") -> p " + \
                    " ".join(f"d{i}" for i in range(len(dims)))
                v = v.rearrange(pat, **kw)
            return v

        CO_SA, CO_ONECA, CO_GA = 0, 16, 32
        CO_SB, CO_ONECB, CO_GB = 48, 64, 80
        CO_BQ, CO_BK, CO_BG = 96, 112, 116
        CO_NBK, CO_NBQ, CO_BV = 132, 136, 152
        CO_GS, CO_US, CO_GBDS, CO_DS = 160, 204, 248, 264
        KC_SS, KC_RINV, KC_DN, KC_DEN, KC_NS1, KC_T1, KC_T2 = 0, 16, 32, 96, 160, 224, 240

        ps = [es.enter_context(nc.psum_tensor(f"ps{i}", [P, 512], F32)) for i in range(6)]
        psN = es.enter_context(nc.psum_tensor("psN", [P, 64], F32))
        psT = es.enter_context(nc.psum_tensor("psT", [P, 1024], BF16))

        s_ld = Sem(es, nc, "s_ld")
        s_w = Sem(es, nc, "s_w")
        s_x = Sem(es, nc, "s_x")
        s_v = Sem(es, nc, "s_v")
        s_sc = Sem(es, nc, "s_sc")
        s_mm = Sem(es, nc, "s_mm")
        s_gp = Sem(es, nc, "s_gp")
        s_cc = Sem(es, nc, "s_cc")
        s_st = Sem(es, nc, "s_st")
        s_std = Sem(es, nc, "s_std")
        s_stx = Sem(es, nc, "s_stx")
        s_sty = Sem(es, nc, "s_sty")
        s_stg = Sem(es, nc, "s_stg")
        s_x2 = Sem(es, nc, "s_x2")
        s_nb = Sem(es, nc, "s_nb")
        s_mb = Sem(es, nc, "s_mb")
        s_mg = Sem(es, nc, "s_mg")
        s_br = Sem(es, nc, "s_br")
        s_wk = Sem(es, nc, "s_wk")
        s_wmb = Sem(es, nc, "s_wmb")
        s_wq = Sem(es, nc, "s_wq")
        s_wg = Sem(es, nc, "s_wg")
        s_wga2 = [Sem(es, nc, "s_wga0"), Sem(es, nc, "s_wga1")]
        s_wup2 = [Sem(es, nc, "s_wup0"), Sem(es, nc, "s_wup1")]
        s_wdn2 = [Sem(es, nc, "s_wdn0"), Sem(es, nc, "s_wdn1")]
        s_yi2 = [Sem(es, nc, "s_yi0"), Sem(es, nc, "s_yi1")]
        s_xi2 = [Sem(es, nc, "s_xi0"), Sem(es, nc, "s_xi1")]
        s_rb = Sem(es, nc, "s_rb")
        s_qrA = Sem(es, nc, "s_qrA")
        s_qrB = Sem(es, nc, "s_qrB")
        s_xq2 = [Sem(es, nc, "s_xqA"), Sem(es, nc, "s_xqB")]
        s_qd = Sem(es, nc, "s_qd")
        s_qss = Sem(es, nc, "s_qss")
        s_qg = Sem(es, nc, "s_qg")
        s_bar = Sem(es, nc, "s_bar")

        def emit(eng, fn):
            prog[eng].append(fn)

        def _waits(e, waits):
            for (s, v) in waits:
                if v > 0:
                    e.wait_ge(s.h, v)

        def barrier():
            snap = tuple((s, s.n) for s in
                         (s_v, s_sc, s_mm, s_gp, s_ld, s_w, s_x, s_x2,
                          s_st, s_std, s_stx, s_sty, s_stg, s_cc))
            for eng in ENGINES:
                def f(e, w=snap):
                    _waits(e, w)
                emit(eng, f)

        def dma(eng, out, in_, sem, waits=(), accum=False):
            waits = tuple(waits)
            def f(e, o=out, i=in_, s=sem, w=waits, a=accum):
                _waits(e, w)
                if a:
                    return e.dma_start(out=o, in_=i, accum_op=ALU.add).then_inc(s.h, 16)
                return e.dma_start(out=o, in_=i).then_inc(s.h, 16)
            emit(eng, f)
            sem.n += 16
            return sem.n

        def mm(out, lhsT, rhs, start, stop, waits=(), inc=None):
            waits = tuple(waits)
            if inc is not None:
                inc.n += 1
            def f(e, o=out, l=lhsT, r=rhs, st=start, sp=stop, w=waits, s=inc):
                _waits(e, w)
                i = e.matmul(o, l, r, start=st, stop=sp)
                if s is not None:
                    i.then_inc(s.h, 1)
                return i
            emit("tensor", f)
            return inc.n if inc is not None else None

        def tmark(waits=()):
            assert not waits
            s_mm.n += 1
            last = prog["tensor"].pop()
            def f(e, g=last, t=s_mm.n):
                i = g(e)
                i.then_inc(s_mm.h, 1)
                return i
            emit("tensor", f)
            return s_mm.n

        def tr(out, in_, ident, waits=(), inc=None):
            waits = tuple(waits)
            if inc is not None:
                inc.n += 1
            def f(e, o=out, i=in_, idn=ident, w=waits, s=inc):
                _waits(e, w)
                ins = e.transpose(o, i, idn)
                if s is not None:
                    ins.then_inc(s.h, 1)
                return ins
            emit("tensor", f)
            return inc.n if inc is not None else None

        _eng_sem = {"vector": None, "scalar": None, "gpsimd": None}

        def vop(eng, meth, *args, waits=(), inc=None, kwargs=None):
            own = {"vector": s_v, "scalar": s_sc, "gpsimd": s_gp}.get(eng)
            if own is not None and own.n > 0:
                waits = tuple(waits) + ((own, own.n),)
            waits = tuple(waits)
            if inc is not None:
                inc.n += 1
            kw = dict(kwargs or {})
            def f(e, m=meth, a=args, w=waits, s=inc, kk=kw):
                _waits(e, w)
                i = getattr(e, m)(*a, **kk)
                if s is not None:
                    i.then_inc(s.h, 1)
                return i
            emit(eng, f)
            return inc.n if inc is not None else None

        bank_free = {i: (s_mm, 0) for i in range(8)}

        def col(t, i):
            return t[:, i:i + 1]

        def bcast(row, out, pbank, waits=()):
            n = out.shape[-1]
            mm(ps[pbank][:, 0:n], ones_rowf[:, :], row, start=True, stop=True,
               waits=list(waits) + [bank_free[pbank], (s_ld, s_ld.n)])
            t = tmark()
            r = vop("vector", "tensor_copy", out, ps[pbank][:, 0:n],
                    waits=[(s_mm, t)], inc=s_v)
            bank_free[pbank] = (s_v, r)
            return r

        dma("sync", identB[:, :], d_identb[:, :], s_ld)
        dma("sync", identF[:, :], d_identf[:, :], s_ld)
        dma("sync", ones_col[:, :], d_onescol[:, :], s_ld)
        dma("sync", ones_row[:, :], d_onesrow[:, :], s_ld)
        dma("sync", ones_rowf[:, :], d_onesrowf[:, :], s_ld)
        dma("sync", cols[:, CO_BQ:CO_BQ + 16], d_bq[:, :], s_ld)
        dma("sync", cols[:, CO_BK:CO_BK + 4], d_bk[:, :], s_ld)
        dma("sync", cols[:, CO_BV:CO_BV + 4], d_bv[:, :], s_ld)
        dma("sync", cols[:, CO_BG:CO_BG + 16], d_bg[:, :], s_ld)
        dma("sync", cols[:, CO_GS:CO_GS + 44], d_gs[:, :], s_ld)
        dma("sync", cols[:, CO_US:CO_US + 44], d_us[:, :], s_ld)
        dma("sync", cols[:, CO_DS:CO_DS + 16], d_ds[:, :], s_ld)
        dma("sync", nrmf[:, :], d_nrm[:, :], s_ld)
        n_colsb_ld = dma("gpsimd", colsb[:, 0:96], d_nrm[:, :], s_nb)
        n_ldc = dma("sync", csil[:, :], d_c[:, :], s_ld)
        n_ld0 = n_ldc

        n_silc = vop("scalar", "activation", csil[:, :], csil[:, :], AF.Silu,
                     waits=[(s_ld, n_ldc)], inc=s_sc)
        n_csb = vop("vector", "tensor_copy", csilb[:, :], csil[:, :],
                    waits=[(s_sc, n_silc)], inc=s_v)
        vop("vector", "tensor_scalar_mul", cols[:, CO_NBK:CO_NBK + 4],
            cols[:, CO_BK:CO_BK + 4], -1.0, waits=[(s_ld, n_ld0)], inc=s_v)
        vop("vector", "tensor_scalar_mul", cols[:, CO_NBQ:CO_NBQ + 16],
            cols[:, CO_BQ:CO_BQ + 16], -1.0, inc=s_v)

        n_memcast = 0
        for idx in range(24):
            j, c6 = idx // 6, idx % 6
            n_memcast = dma("gpsimd", memb[:, idx, :], d_mem[j, c6, :, :], s_mb)

        wm_f = vf(20480, 16, 128)
        n_mmprev = 0
        for oc in range(12):
            n_wm = dma("sync", wm_f[:, :, :], d_wmod[:, :, oc * 128:(oc + 1) * 128],
                       s_w, waits=[(s_mm, n_mmprev)])
            for fc in range(NC16):
                mm(psN[:, oc:oc + 1], wm_f[:, fc, :], csil[:, fc:fc + 1],
                   start=(fc == 0), stop=(fc == NC16 - 1),
                   waits=[(s_w, n_wm), (s_sc, n_silc)] if fc == 0 else ())
            n_mmprev = tmark()
        n_modev = vop("vector", "tensor_copy", modsh[:, :], psN[:, 0:12],
                      waits=[(s_mm, n_mmprev)], inc=s_v)
        bank_free[6] = (s_v, n_modev)
        tr(ps[0][0:12, 0:128], modsh[:, :], identF[:, :],
           waits=[(s_v, n_modev), (s_ld, n_ld0)])
        n_modtr = tmark()
        n_modrow = vop("vector", "tensor_copy", modrow[:, :], ps[0][0:12, 0:128],
                       waits=[(s_mm, n_modtr)], inc=s_v)
        bank_free[0] = (s_v, n_modrow)
        n_magst = dma("gpsimd", d_magi[:, :], modrow[:, :], s_stg,
                      waits=[(s_v, n_modrow)])

        def f_ag(e, w=n_magst):
            e.wait_ge(s_stg.h, w)
            e.collective_compute(
                "AllGather", ALU.bypass,
                replica_groups=[list(range(8))],
                ins=[d_magi[:, :]],
                outs=[d_mago[:, :]],
            ).then_inc(s_cc.h, 1)
        emit("gpsimd", f_ag)
        s_cc.n += 1
        n_magr = dma("gpsimd", magr[:, :], d_mago[:, :], s_mg, waits=[(s_cc, s_cc.n)])
        tr(ps[0][0:128, 0:96], magr[:, :], identF[0:96, 0:96],
           waits=[(s_mg, n_magr), bank_free[0]])
        n_modtr2 = tmark()
        n_modcols = vop("vector", "tensor_copy", cols[:, 0:96], ps[0][0:128, 0:96],
                        waits=[(s_mm, n_modtr2)], inc=s_v)
        bank_free[0] = (s_v, n_modcols)

        vop("vector", "tensor_scalar_add", cols[:, CO_ONECA:CO_ONECA + 16],
            cols[:, CO_ONECA:CO_ONECA + 16], 1.0, waits=[(s_v, n_modcols)], inc=s_v)
        vop("vector", "tensor_scalar_add", cols[:, CO_ONECB:CO_ONECB + 16],
            cols[:, CO_ONECB:CO_ONECB + 16], 1.0, inc=s_v)
        for co in (CO_GA, CO_GB):
            nab = vop("scalar", "activation", gtmp[:, 0:16], cols[:, co:co + 16],
                      AF.Abs, waits=[(s_v, s_v.n)], inc=s_sc)
            vop("vector", "tensor_scalar_max", gtmp[:, 0:16], gtmp[:, 0:16], 0.25,
                waits=[(s_sc, nab)], inc=s_v)
            vop("vector", "tensor_scalar", gtmp[:, 16:32], cols[:, co:co + 16], 0.0,
                None, ALU.is_ge, inc=s_v)
            vop("vector", "tensor_scalar", gtmp[:, 16:32], gtmp[:, 16:32], 2.0, -1.0,
                ALU.mult, ALU.add, inc=s_v)
            vop("vector", "tensor_tensor", cols[:, co:co + 16], gtmp[:, 0:16],
                gtmp[:, 16:32], ALU.mult, inc=s_v)
        vop("vector", "tensor_tensor", cols[:, CO_GBDS:CO_GBDS + 16],
            cols[:, CO_DS:CO_DS + 16], cols[:, CO_GB:CO_GB + 16], ALU.mult, inc=s_v)
        n_colmath = s_v.n

        def ln_phase(src, co_scale, co_shift, tapname, first_waits=(),
                     skip_first_guard=False):
            x_t = vf(0, 16, TT)
            xb_t = vb(24576, 16, TT)
            sq_t = vb(32768, 16, TT)
            f2 = vf(8192, 2, TT)
            bc_m = vf(9216, TT)
            bc_r = vf(9728, TT)
            rrows = vf(10240, 4, TT)
            apply_marks = []
            first = True
            for tt4 in range(NT):
                tsl = slice(tt4 * TT, (tt4 + 1) * TT)
                if tt4 == 0 and skip_first_guard:
                    xw = []
                else:
                    xw = [(s_sc, s_sc.n), (s_v, s_v.n)]
                if tt4 == 0:
                    xw += list(first_waits)
                nx = dma("sync", x_t[:, :, :], src[:, :, tsl], s_x, waits=xw)
                ncst = vop("vector", "tensor_copy", xb_t[:, :, :], x_t[:, :, :],
                           waits=[(s_x, nx)], inc=s_v)
                nsq = vop("scalar", "activation", sq_t[:, :, :], x_t[:, :, :], AF.Square,
                          waits=[(s_x, nx)], inc=s_sc)
                for fc in range(NC16):
                    mm(ps[2][0:1, :], ones_col[:, :], xb_t[:, fc, :],
                       start=(fc == 0), stop=(fc == NC16 - 1),
                       waits=([(s_v, ncst), bank_free[2], bank_free[3]] if fc == 0 else ()))
                for fc in range(NC16):
                    mm(ps[3][0:1, :], ones_col[:, :], sq_t[:, fc, :],
                       start=(fc == 0), stop=(fc == NC16 - 1),
                       waits=[(s_sc, nsq)] if fc == 0 else ())
                nstat = tmark()
                r_m, r_r = rrows[0:1, 0, :], rrows[0:1, 1, :]
                r_3, r_4 = rrows[0:1, 2, :], rrows[0:1, 3, :]
                vop("vector", "tensor_scalar_mul", r_m, ps[2][0:1, :], 1.0 / 2048.0,
                    waits=[(s_mm, nstat), (s_mm, s_mm.n)], inc=s_v)
                n23 = vop("vector", "tensor_scalar_mul", r_3, ps[3][0:1, :], 1.0 / 2048.0,
                          inc=s_v)
                bank_free[2] = (s_v, n23)
                bank_free[3] = (s_v, n23)
                vop("vector", "tensor_tensor", r_4, r_m, r_m, ALU.mult, inc=s_v)
                vop("vector", "tensor_tensor", r_3, r_3, r_4, ALU.subtract, inc=s_v)
                nvar = vop("vector", "tensor_scalar_add", r_3, r_3, LNEPS, inc=s_v)
                nln = vop("scalar", "activation", r_4, r_3, AF.Ln,
                          waits=[(s_v, nvar)], inc=s_sc)
                nr = vop("scalar", "activation", r_r, r_4, AF.Exp,
                         waits=[(s_sc, nln)], inc=s_sc, kwargs={"scale": -0.5})
                bcast(r_m, bc_m[:, :], 4, waits=[(s_sc, nr)])
                nbr = bcast(r_r, bc_r[:, :], 5)
                napply = 0
                for fc in range(NC16):
                    vop("vector", "tensor_tensor", f2[:, 0, :], x_t[:, fc, :],
                        bc_m[:, :], ALU.subtract,
                        waits=[(s_v, nbr)] if fc == 0 else (), inc=s_v)
                    v2 = vop("vector", "tensor_tensor", f2[:, 1, :], f2[:, 0, :],
                             bc_r[:, :], ALU.mult,
                             waits=[(s_sc, napply)] if napply else (), inc=s_v)
                    aw = [(s_v, v2)]
                    if first:
                        aw.append((s_v, n_colmath))
                        first = False
                    napply = vop("scalar", "activation", h_b[:, fc, tsl], f2[:, 1, :],
                                 AF.Identity, waits=aw, inc=s_sc,
                                 kwargs={"scale": col(cols, co_scale + fc),
                                         "bias": col(cols, co_shift + fc)})
                apply_marks.append(napply)
            if tapname in taps:
                for tt4 in range(NT):
                    tsl = slice(tt4 * TT, (tt4 + 1) * TT)
                    for fc in range(NC16):
                        v = vop("vector", "tensor_copy", vf(8192, TT), h_b[:, fc, tsl],
                                waits=[(s_stg, s_stg.n), (s_sc, s_sc.n)], inc=s_v)
                        dma("gpsimd", d_tap[tapname][:, fc, tsl], vf(8192, TT),
                            s_stg, waits=[(s_v, v)])
            return apply_marks

        ln1_marks = ln_phase(d_x, CO_ONECA, CO_SA, "h", skip_first_guard=True)
        if taps:
            barrier()

        U_tm = vb(0, 16, 768)
        U_fm = vb(12288, 6, 2048)
        wmv = vb(24576, 16, 512)
        a_tm = vb(32768, 16, 256)
        a_fm = vb(36864, 2, 2048)
        wmbrhs = vb(32768, 16, 512)
        kfm = vb(40960, 2048)
        roll = vb(40960, 2048)
        rollB = vb(43008, 2048)
        wkvb = vb(43008, 16, 128)
        mbt = vb(43008, 2, 512)
        v_tm = vb(45056, 16, 128)
        t1s = fsc[:, 0:512]
        stg_ev = [fsc[:, 512:1024], fsc[:, 1024:1536]]
        vstg = fsc[:, 1024:1536].bitcast(BF16)
        vtr_done = [0, 0]

        def roll_dmas(src2, dst, jr, half, sem, waits, queue="scalar"):
            dma(queue, dst[jr:128], src2[:, half, :][0:128 - jr, :], sem, waits=waits)
            return dma(queue, dst[0:jr], src2[:, 1 - half, :][128 - jr:128, :], sem,
                       waits=waits)

        arin_stores = 0
        n_ar_j = [0] * 4
        prev_cols_user = 0
        prev_mbt_users = [(s_mm, 0)]

        for j in range(4):
            n_wka = dma("gpsimd", wkvb[:, :, :], d_wk[:, :, j * 128:(j + 1) * 128],
                        s_wk, waits=prev_mbt_users)
            ev_k = []
            for t4 in range(NT):
                tsl = slice(t4 * TT, (t4 + 1) * TT)
                b = t4 % 2
                for fc in range(NC16):
                    mm(ps[b][:, :], wkvb[:, fc, :], h_b[:, fc, tsl],
                       start=(fc == 0), stop=(fc == NC16 - 1),
                       waits=[(s_wk, n_wka), bank_free[b],
                              (s_sc, ln1_marks[t4])] if fc == 0 else ())
                nk = tmark()
                vop("scalar", "activation", kfm[:, tsl], ps[b][:, :], AF.Identity,
                    waits=[(s_mm, nk)], inc=s_sc,
                    kwargs={"bias": col(cols, CO_BK + j)})
                vop("scalar", "activation", a_fm[:, 0, tsl], ps[b][:, :], AF.Relu,
                    inc=s_sc, kwargs={"bias": col(cols, CO_BK + j)})
                nev = vop("scalar", "activation", a_fm[:, 1, tsl], ps[b][:, :], AF.Relu,
                          inc=s_sc, kwargs={"scale": -1.0, "bias": col(cols, CO_NBK + j)})
                bank_free[b] = (s_sc, nev)
                ev_k.append(nev)
            n_wkb = dma("gpsimd", wkvb[:, :, :], d_wv[:, :, j * 128:(j + 1) * 128],
                        s_wk, waits=[(s_mm, s_mm.n)])
            vev = []
            for t4 in range(NT):
                tsl = slice(t4 * TT, (t4 + 1) * TT)
                b = 2 + t4 % 2
                for fc in range(NC16):
                    mm(ps[b][:, :], wkvb[:, fc, :], h_b[:, fc, tsl],
                       start=(fc == 0), stop=(fc == NC16 - 1),
                       waits=[(s_wk, n_wkb), bank_free[b]] if fc == 0 else ())
                nv = tmark()
                vsl = vstg[:, (t4 % 2) * 512:(t4 % 2) * 512 + 512]
                ev = vop("vector", "tensor_scalar", vsl, ps[b][:, :],
                         col(cols, CO_BV + j), None, ALU.add,
                         waits=[(s_mm, nv), (s_mm, vtr_done[t4 % 2]),
                                (s_st, arin_stores)], inc=s_v)
                bank_free[b] = (s_v, ev)
                for tc4 in range(4):
                    tc = t4 * 4 + tc4
                    tr(psT[:, 0:128], vsl[:, tc4 * 128:(tc4 + 1) * 128], identB[:, :],
                       waits=[(s_v, ev), bank_free[7]])
                    ntr = tmark()
                    nevt = vop("scalar", "activation", v_tm[:, tc, :], psT[:, 0:128],
                               AF.Identity, waits=[(s_mm, ntr)], inc=s_sc)
                    bank_free[7] = (s_sc, nevt)
                vtr_done[t4 % 2] = s_mm.n
                vev.append(s_sc.n)
            for tc in range(16):
                t4 = tc // 4
                tr(psT[:, 128:256], kfm[:, tc * 128:(tc + 1) * 128], identB[:, :],
                   waits=[(s_sc, ev_k[t4]), bank_free[7]])
                ntr = tmark()
                vop("scalar", "activation", a_tm[:, tc, 0:128], psT[:, 128:256], AF.Relu,
                    waits=[(s_mm, ntr)], inc=s_sc)
                nevt = vop("scalar", "activation", a_tm[:, tc, 128:256], psT[:, 128:256],
                           AF.Relu, inc=s_sc, kwargs={"scale": -1.0})
                bank_free[7] = (s_sc, nevt)
            n_atm = s_sc.n
            for jr in (1, 2, 3):
                base = (jr - 1) * 256
                vop("vector", "tensor_tensor", U_tm[:, :, base + jr:base + 256],
                    a_tm[:, :, jr:256], a_tm[:, :, 0:256 - jr], ALU.mult,
                    waits=[(s_sc, n_atm)], inc=s_v)
                vop("vector", "tensor_tensor", U_tm[:, :, base:base + jr],
                    a_tm[:, :, 0:jr], a_tm[:, :, 256 - jr:256], ALU.mult, inc=s_v)
            n_utm = s_v.n
            for tc in range(16):
                vop("scalar", "activation", vb(24576, 768), U_tm[:, tc, :],
                    AF.Square,
                    waits=[(s_v, n_utm), (s_mm, prev_cols_user)] if tc == 0 else (),
                    inc=s_sc, kwargs={"accum_out": col(kcols, KC_SS + tc)})
            n_ss = s_sc.n
            ufm_done = []
            roll_free = [0, 0]
            pend = {}

            def issue_kroll(uc):
                jr, half = uc // 2 + 1, uc % 2
                slot = roll if uc % 2 == 0 else rollB
                sm = s_qrA if uc % 2 == 0 else s_qrB
                w = [(s_sc, ev_k[3]), (s_v, roll_free[uc % 2]), (s_mm, s_mm.n)]
                pend[uc] = (sm, roll_dmas(a_fm, slot, jr, half, sm, w,
                                          queue="gpsimd"))

            issue_kroll(0)
            issue_kroll(1)
            prev_tt = s_v.n
            for uchunk in range(6):
                slot = roll if uchunk % 2 == 0 else rollB
                sm, nr2 = pend[uchunk]
                prev_tt = vop("vector", "tensor_tensor", U_fm[:, uchunk, :],
                              a_fm[:, uchunk % 2, :], slot, ALU.mult,
                              waits=[(sm, nr2)], inc=s_v)
                roll_free[uchunk % 2] = prev_tt
                ufm_done.append(prev_tt)
                if uchunk + 2 < 6:
                    issue_kroll(uchunk + 2)
            for tc in range(16):
                for c6 in range(6):
                    w0 = [(s_v, ufm_done[c6])] if tc == 0 else []
                    if tc == 0 and c6 == 0:
                        w0.append(bank_free[6])
                        w0.append((s_nb, n_colsb_ld))
                    mm(psN[:, tc * 4:(tc + 1) * 4],
                       U_fm[:, c6, tc * 128:(tc + 1) * 128],
                       colsb[:, c6 * 16 + 4 * j:c6 * 16 + 4 * j + 4],
                       start=(c6 == 0), stop=(c6 == 5),
                       waits=w0)
            n_dn = tmark()
            ndn = vop("vector", "tensor_copy",
                      kcols[:, KC_DN:KC_DN + 64].rearrange("p (a b) -> p a b", a=4),
                      psN[:, 0:64].rearrange("p (b a) -> p a b", b=16),
                      waits=[(s_mm, n_dn)], inc=s_v)
            bank_free[6] = (s_v, ndn)
            vop("vector", "tensor_scalar_max", kcols[:, KC_T1:KC_T1 + 16],
                kcols[:, KC_SS:KC_SS + 16], 1e-24, waits=[(s_sc, n_ss)], inc=s_v)
            nv1 = s_v.n
            vop("scalar", "activation", kcols[:, KC_T1:KC_T1 + 16],
                kcols[:, KC_T1:KC_T1 + 16], AF.Ln, waits=[(s_v, nv1)], inc=s_sc)
            nsc1 = vop("scalar", "activation", kcols[:, KC_T1:KC_T1 + 16],
                       kcols[:, KC_T1:KC_T1 + 16], AF.Exp, inc=s_sc,
                       kwargs={"scale": -0.5})
            vop("vector", "tensor_scalar_min", kcols[:, KC_RINV:KC_RINV + 16],
                kcols[:, KC_T1:KC_T1 + 16], 1e12, waits=[(s_sc, nsc1)], inc=s_v)
            for i in range(4):
                vop("vector", "tensor_tensor",
                    kcols[:, KC_DEN + i * 16:KC_DEN + (i + 1) * 16],
                    kcols[:, KC_DN + i * 16:KC_DN + (i + 1) * 16],
                    kcols[:, KC_RINV:KC_RINV + 16], ALU.mult, inc=s_v)
            vop("vector", "tensor_scalar_add", kcols[:, KC_DEN:KC_DEN + 64],
                kcols[:, KC_DEN:KC_DEN + 64], EPS, inc=s_v)
            vop("vector", "tensor_tensor", kcols[:, KC_T2:KC_T2 + 16],
                kcols[:, KC_RINV:KC_RINV + 16], kcols[:, KC_RINV:KC_RINV + 16],
                ALU.mult, inc=s_v)
            vop("vector", "tensor_tensor", kcols[:, KC_T2:KC_T2 + 16],
                kcols[:, KC_T2:KC_T2 + 16], kcols[:, KC_SS:KC_SS + 16], ALU.mult,
                inc=s_v)
            vop("vector", "tensor_scalar_max", kcols[:, KC_T2:KC_T2 + 16],
                kcols[:, KC_T2:KC_T2 + 16], 1e-12, inc=s_v)
            vop("vector", "reciprocal", kcols[:, KC_T2:KC_T2 + 16],
                kcols[:, KC_T2:KC_T2 + 16], inc=s_v)
            kcb_v = kcolsb[:, :].rearrange("p (a b) -> p a b", a=16)
            for i in range(4):
                vop("vector", "tensor_tensor", kcols[:, KC_T1:KC_T1 + 16],
                    kcols[:, KC_DEN + i * 16:KC_DEN + (i + 1) * 16],
                    kcols[:, KC_T2:KC_T2 + 16], ALU.mult, inc=s_v)
                vop("vector", "tensor_scalar", kcols[:, KC_T1:KC_T1 + 16],
                    kcols[:, KC_T1:KC_T1 + 16], -1.0, 1.0, ALU.mult, ALU.add, inc=s_v)
                vop("vector", "tensor_scalar_max", kcols[:, KC_T1:KC_T1 + 16],
                    kcols[:, KC_T1:KC_T1 + 16], 0.0, inc=s_v)
                vop("vector", "tensor_scalar_min", kcols[:, KC_T1:KC_T1 + 16],
                    kcols[:, KC_T1:KC_T1 + 16], 1.0, inc=s_v)
                vop("vector", "tensor_tensor", kcols[:, KC_T1:KC_T1 + 16],
                    kcols[:, KC_T1:KC_T1 + 16], kcols[:, KC_RINV:KC_RINV + 16],
                    ALU.mult, inc=s_v)
                vop("vector", "tensor_copy", kcb_v[:, :, i], kcols[:, KC_T1:KC_T1 + 16],
                    inc=s_v)
            vop("vector", "reciprocal", kcols[:, KC_NS1:KC_NS1 + 64],
                kcols[:, KC_DEN:KC_DEN + 64], inc=s_v)
            for i in range(4):
                vop("vector", "tensor_tensor",
                    kcols[:, KC_NS1 + i * 16:KC_NS1 + (i + 1) * 16],
                    kcols[:, KC_NS1 + i * 16:KC_NS1 + (i + 1) * 16],
                    kcols[:, KC_RINV:KC_RINV + 16], ALU.mult, inc=s_v)
            n_cm = vop("vector", "tensor_scalar_mul", kcols[:, KC_NS1:KC_NS1 + 64],
                       kcols[:, KC_NS1:KC_NS1 + 64], -1.0, inc=s_v)

            n_mbrdy = dma("gpsimd", wmbrhs[:, :, :],
                          d_wmb[:, :, j * 512:(j + 1) * 512], s_wmb,
                          waits=[(s_v, ufm_done[5]), (s_sc, n_ss)])
            n_brow = dma("gpsimd", browb[:, :], d_bmb[0:1, j * 512:(j + 1) * 512], s_br,
                         waits=[(s_mm, s_mm.n)])

            wmv_marks = [0] * 16
            mbt_free = [roll_free[1], roll_free[1]]
            for tc in range(16):
                tsl128 = slice(tc * 128, (tc + 1) * 128)
                bmb_ = tc % 2
                bnm = 2 + tc % 2
                for fc in range(NC16):
                    mm(ps[bmb_][:, :], h_b[:, fc, tsl128], wmbrhs[:, fc, :],
                       start=(fc == 0), stop=False,
                       waits=[(s_wmb, n_mbrdy), bank_free[bmb_]] if fc == 0 else ())
                mm(ps[bmb_][:, :], ones_row[:, :], browb[0:1, :],
                   start=False, stop=True,
                   waits=[(s_br, n_brow)] if tc == 0 else ())
                nmb = tmark()
                nmbe = vop("scalar", "activation", mbt[:, tc % 2, :], ps[bmb_][:, :],
                           AF.Sigmoid,
                           waits=[(s_mm, nmb), (s_v, mbt_free[tc % 2])], inc=s_sc)
                bank_free[bmb_] = (s_sc, nmbe)
                for c6 in range(6):
                    mm(ps[bnm][:, :], U_fm[:, c6, tsl128], memb[:, j * 6 + c6, :],
                       start=(c6 == 0), stop=(c6 == 5),
                       waits=[bank_free[bnm], (s_mb, n_memcast)] if c6 == 0 else ())
                nnum = tmark()
                for i in range(4):
                    vop("vector", "scalar_tensor_tensor",
                        t1s[:, i * 128:(i + 1) * 128], ps[bnm][:, i * 128:(i + 1) * 128],
                        col(kcols, KC_NS1 + i * 16 + tc), v_tm[:, tc, :],
                        ALU.mult, ALU.add,
                        waits=[(s_mm, nnum), (s_v, n_cm), (s_sc, vev[-1])] if i == 0 else (),
                        inc=s_v)
                nt1 = s_v.n
                bank_free[bnm] = (s_v, nt1)
                for i in range(4):
                    vop("vector", "scalar_tensor_tensor",
                        wmv[:, tc, i * 128:(i + 1) * 128], t1s[:, i * 128:(i + 1) * 128],
                        col(kcols, KC_RINV + tc), mbt[:, tc % 2, i * 128:(i + 1) * 128],
                        ALU.mult, ALU.mult,
                        waits=[(s_sc, nmbe)] if i == 0 else (), inc=s_v)
                mbt_free[tc % 2] = s_v.n
                wmv_marks[tc] = s_v.n
            n_wmv = s_v.n
            prev_mbt_users = [(s_v, n_wmv), (s_sc, s_sc.n)]

            for c6 in range(6):
                for tc in range(16):
                    w0 = []
                    if c6 == 0:
                        w0.append((s_v, wmv_marks[tc]))
                    if tc == 0:
                        w0.append(bank_free[c6])
                    mm(ps[c6][:, :], U_tm[:, tc, c6 * 128:(c6 + 1) * 128],
                       wmv[:, tc, :], start=(tc == 0), stop=(tc == 15), waits=w0)
                    mm(psN[:, c6 * 4:(c6 + 1) * 4],
                       U_tm[:, tc, c6 * 128:(c6 + 1) * 128],
                       kcb_v[:, tc, :], start=(tc == 0), stop=(tc == 15),
                       waits=[bank_free[6]] if (tc == 0 and c6 == 0) else ())
            n_delta = tmark()
            prev_cols_user = n_delta
            for c6 in range(6):
                sl = stg_ev[c6 % 2]
                nev = vop("vector", "tensor_copy", sl, ps[c6][:, :],
                          waits=[(s_mm, n_delta), (s_st, arin_stores)], inc=s_v)
                bank_free[c6] = (s_v, nev)
                arin_stores = dma("sync",
                                  d_arin[(j * 6 + c6) * 128:(j * 6 + c6 + 1) * 128, :],
                                  sl, s_st, waits=[(s_v, nev)])
            ndne = vop("vector", "tensor_copy",
                       dnacc[:, :].rearrange("p (a b) -> p a b", a=6)[:, :, 4 * j:4 * j + 4],
                       psN[:, 0:24].rearrange("p (a b) -> p a b", a=6),
                       waits=[(s_mm, n_delta)], inc=s_v)
            bank_free[6] = (s_v, ndne)

            if "kdbg" in taps and j == 0:
                v = vop("vector", "tensor_copy", fsc[:, 0:256], kcols[:, :], inc=s_v)
                dma("gpsimd", d_tap["kdbg"][:, 0:256], fsc[:, 0:256], s_stg,
                    waits=[(s_v, v)])

            def f_arj(e, w=arin_stores, jj=j):
                e.wait_ge(s_st.h, w)
                e.collective_compute(
                    "AllReduce", ALU.add,
                    replica_groups=[list(range(8))],
                    ins=[d_arin[jj * 768:(jj + 1) * 768, :]],
                    outs=[d_arout[jj * 768:(jj + 1) * 768, :]],
                ).then_inc(s_cc.h, 1)
            emit("gpsimd", f_arj)
            s_cc.n += 1
            n_ar_j[j] = s_cc.n

        dma("sync", d_arin[3072:3096, :], dnacc[:, :], s_std, waits=[(s_v, s_v.n)])

        def f_ar(e, w=s_std.n):
            e.wait_ge(s_std.h, w)
            e.collective_compute(
                "AllReduce", ALU.add,
                replica_groups=[list(range(8))],
                ins=[d_arin[3072:3096, :]],
                outs=[d_arout[3072:3096, :]],
            ).then_inc(s_cc.h, 1)
        emit("gpsimd", f_ar)
        s_cc.n += 1
        n_ar = s_cc.n

        prevup = s_v.n
        for idx in range(24):
            sl = 1024 * (idx % 2)
            na = dma("sync", wstg[:, sl:sl + 512], d_arout[idx * 128:(idx + 1) * 128, :],
                     s_x, waits=[(s_cc, n_ar_j[idx // 6]), (s_v, prevup)])
            nb = dma("sync", wstg[:, sl + 512:sl + 1024],
                     d_mem[idx // 6, idx % 6, :, :], s_x)
            prevup = vop("vector", "tensor_tensor", memb[:, idx, :],
                         wstg[:, sl:sl + 512], wstg[:, sl + 512:sl + 1024], ALU.add,
                         waits=[(s_x, nb), (s_mm, s_mm.n)], inc=s_v)

        n_na = dma("sync", fsc[:, 0:96], d_arout[3072:3096, :], s_x,
                   waits=[(s_cc, n_ar), (s_v, s_v.n)])
        vop("vector", "tensor_tensor", nrmf[:, :], nrmf[:, :], fsc[:, 0:96], ALU.add,
            waits=[(s_x, n_na)], inc=s_v)
        n_newmem = vop("vector", "tensor_copy", colsb[:, 96:192], nrmf[:, :], inc=s_v)

        if "newmem" in taps:
            for idx in range(24):
                v = vop("vector", "tensor_copy", fsc[:, 512:1024], memb[:, idx, :],
                        waits=[(s_v, n_newmem), (s_stg, s_stg.n)], inc=s_v)
                dma("gpsimd", d_tap["newmem"][idx, :, :], fsc[:, 512:1024], s_stg,
                    waits=[(s_v, v)])
            v = vop("vector", "tensor_copy", fsc[:, 0:96], nrmf[:, :], inc=s_v)
            dma("gpsimd", d_tap["newnrm"][:, :], fsc[:, 0:96], s_stg, waits=[(s_v, v)])

        U_q = vb(0, 6, 2048)
        qa_fm = vb(12288, 2, 2, 2048)
        qroll = vb(20480, 2048)
        qrollB = vb(22528, 2048)
        qdust = fsc[:, 0:512].bitcast(BF16).rearrange("p (a b) -> p a b", a=2)
        wqb = vb(24576, 16, 128)
        wgb = vb(26624, 16, 128)
        gate_b = vb(28672, 2, 2048)
        xqB = vf(16384, 2048)
        bcD = vf(18432, 2048)
        xqA = vf(20480, 2048)
        tmpq = vf(22528, 512)
        ssT = vf(23040, 16)
        deT = vf(23056, 16)
        fT = vf(23072, 16)

        prev_projq = 0
        prev_projg = 0
        qa_free = [0, 0]
        gate_free = [0, 0]
        stA = {}

        qw_pend = {}

        def issue_qw(h):
            nwq = dma("gpsimd", wqb[:, :, :], d_wq[:, :, h * 128:(h + 1) * 128],
                      s_wq, waits=[(s_mm, stA.get(h - 1, (0,) * 9)[0])])
            nwg = dma("gpsimd", wgb[:, :, :], d_wg[:, :, h * 128:(h + 1) * 128],
                      s_wg, waits=[(s_mm, stA.get(h - 1, (0,) * 9)[1])])
            qw_pend[h] = (nwq, nwg)

        def q_stage_a(h):
            sl = h % 2
            nwq, nwg = qw_pend[h]
            nqa = 0
            ngev = 0
            lastq = 0
            lastg = 0
            for t4 in range(NT):
                tsl = slice(t4 * TT, (t4 + 1) * TT)
                b = t4 % 2
                for fc in range(NC16):
                    mm(ps[b][:, :], wqb[:, fc, :], h_b[:, fc, tsl],
                       start=(fc == 0), stop=(fc == NC16 - 1),
                       waits=[(s_wq, nwq), bank_free[b],
                              (s_v, qa_free[sl])] if fc == 0 else ())
                nq = tmark()
                lastq = nq
                vop("scalar", "activation", qa_fm[:, sl, 0, tsl], ps[b][:, :], AF.Relu,
                    waits=[(s_mm, nq)], inc=s_sc, kwargs={"bias": col(cols, CO_BQ + h)})
                nqa = vop("scalar", "activation", qa_fm[:, sl, 1, tsl], ps[b][:, :],
                          AF.Relu, inc=s_sc,
                          kwargs={"scale": -1.0, "bias": col(cols, CO_NBQ + h)})
                bank_free[b] = (s_sc, nqa)
                bg_ = 2 + t4 % 2
                for fc in range(NC16):
                    mm(ps[bg_][:, :], wgb[:, fc, :], h_b[:, fc, tsl],
                       start=(fc == 0), stop=(fc == NC16 - 1),
                       waits=[(s_wg, nwg), bank_free[bg_],
                              (s_v, gate_free[sl])] if fc == 0 else ())
                ng = tmark()
                lastg = ng
                ngev = vop("scalar", "activation", gate_b[:, sl, tsl], ps[bg_][:, :],
                           AF.Sigmoid, waits=[(s_mm, ng)], inc=s_sc,
                           kwargs={"bias": col(cols, CO_BG + h)})
                bank_free[bg_] = (s_sc, ngev)
            stA[h] = (lastq, lastg, nqa, ngev)

        qroll_free = [0, 0]
        xq_slot_free = [0, 0]
        xq_store_mark = [0, 0]
        dustA_mark = [0]
        dustB_mark = [0]
        qpend_g = {}

        def issue_qroll(h, uc):
            sl = h % 2
            nqa = stA[h][2]
            jr, half = uc // 2 + 1, uc % 2
            slot = qroll if uc % 2 == 0 else qrollB
            sm = s_qrA if uc % 2 == 0 else s_qrB
            w = [(s_sc, nqa), (s_v, qroll_free[uc % 2])]
            qpend_g[(h, uc)] = (sm, roll_dmas(qa_fm[:, sl], slot, jr, half, sm, w,
                                              queue="gpsimd"))

        def q_stage_b(h):
            sl = h % 2
            j, i = h // 4, h % 4
            lastq, lastg, nqa, ngev = stA[h]
            if (h, 0) not in qpend_g:
                issue_qroll(h, 0)
                issue_qroll(h, 1)
            uq_done = []
            for uchunk in range(6):
                slot = qroll if uchunk % 2 == 0 else qrollB
                sm2, nr2 = qpend_g[(h, uchunk)]
                ptt = vop("vector", "tensor_tensor", U_q[:, uchunk, :],
                          qa_fm[:, sl, uchunk % 2, :], slot, ALU.mult,
                          waits=[(sm2, nr2)], inc=s_v)
                qroll_free[uchunk % 2] = ptt
                uq_done.append(ptt)
                if uchunk + 2 < 6:
                    issue_qroll(h, uchunk + 2)
            qa_free[sl] = uq_done[-1]
            if h + 2 < 16:
                issue_qw(h + 2)
            xqT = xqA if sl == 0 else xqB
            s_xqs = s_xq2[sl]
            s_stq = s_stx if sl == 0 else s_sty
            nxq = dma("sync", xqT[:, :], d_x[:, h, :], s_xqs,
                      waits=[(s_v, xq_slot_free[sl]), (s_stq, xq_store_mark[sl])])
            dustA = fsc[:, 0:1024].bitcast(BF16)
            dustB = wstg[:, 0:1024].bitcast(BF16)
            nsq = [None] * 6

            def issue_square(c6):
                d = dustA if c6 % 2 == 0 else dustB
                w = [(s_v, uq_done[c6])]
                if c6 % 2 == 0:
                    w.append((s_mm, dustA_mark[0]))
                    nsq[c6] = (s_sc, vop("scalar", "activation", d, U_q[:, c6, :],
                                         AF.Square, waits=w, inc=s_sc))
                else:
                    w.append((s_mm, dustB_mark[0]))
                    nsq[c6] = (s_v, vop("vector", "tensor_tensor", d, U_q[:, c6, :],
                                        U_q[:, c6, :], ALU.mult, waits=w, inc=s_v))
            for t4 in range(NT):
                tsl = slice(t4 * TT, (t4 + 1) * TT)
                bd = 4 + t4 % 2
                for c6 in range(6):
                    mm(ps[bd][0:1, :],
                       colsb[:, 96 + c6 * 16 + h:96 + c6 * 16 + h + 1],
                       U_q[:, c6, tsl], start=(c6 == 0), stop=(c6 == 5),
                       waits=([(s_v, n_newmem), bank_free[bd], (s_v, uq_done[0])]
                              if c6 == 0 else
                              [(s_v, uq_done[c6])] if t4 == 0 else ()))
                nsd = tmark()
                nev = vop("scalar", "activation", bcD[0:1, tsl], ps[bd][0:1, :],
                          AF.Identity, waits=[(s_mm, nsd)], inc=s_sc,
                          kwargs={"bias": 0.0})
                bank_free[bd] = (s_sc, nev)
            nscat_d = dma("gpsimd", deT[:, :], bcD[0:1, :], s_qd,
                          waits=[(s_sc, s_sc.n)])
            issue_square(0)
            issue_square(1)
            nss_t4 = [0] * NT
            for c6 in range(6):
                d = dustA if c6 % 2 == 0 else dustB
                for t4 in range(NT):
                    tsl = slice(t4 * TT, (t4 + 1) * TT)
                    nss_t4[t4] = mm(ps[t4][0:1, :], ones_col[:, :], d[:, tsl],
                                    start=(c6 == 0), stop=(c6 == 5),
                                    waits=([nsq[c6]] +
                                           ([bank_free[t4]] if c6 == 0 else [])),
                                    inc=(s_mm if (t4 == NT - 1 or c6 == 5)
                                         else None))
                if c6 % 2 == 0:
                    dustA_mark[0] = s_mm.n
                else:
                    dustB_mark[0] = s_mm.n
                if c6 + 2 < 6:
                    issue_square(c6 + 2)
            for t4 in range(NT):
                tsl = slice(t4 * TT, (t4 + 1) * TT)
                nev = vop("scalar", "activation", bcD[0:1, tsl], ps[t4][0:1, :],
                          AF.Identity,
                          waits=[(s_mm, nss_t4[t4]), (s_qd, nscat_d)], inc=s_sc,
                          kwargs={"bias": 0.0})
                bank_free[t4] = (s_sc, nev)
            nscat_s = dma("gpsimd", ssT[:, :], bcD[0:1, :], s_qss,
                          waits=[(s_sc, s_sc.n)])
            vop("vector", "tensor_scalar_max", ssT[:, :], ssT[:, :], 1e-24,
                waits=[(s_qss, nscat_s)], inc=s_v)
            nv1 = s_v.n
            vop("scalar", "activation", ssT[:, :], ssT[:, :], AF.Ln,
                waits=[(s_v, nv1)], inc=s_sc)
            nsc2 = vop("scalar", "activation", ssT[:, :], ssT[:, :], AF.Exp,
                       inc=s_sc, kwargs={"scale": -0.5})
            vop("vector", "tensor_scalar_min", ssT[:, :], ssT[:, :], 1e12,
                waits=[(s_sc, nsc2)], inc=s_v)
            vop("vector", "tensor_tensor", deT[:, :], deT[:, :], ssT[:, :],
                ALU.mult, inc=s_v)
            vop("vector", "tensor_scalar_add", deT[:, :], deT[:, :], EPS, inc=s_v)
            vop("vector", "reciprocal", deT[:, :], deT[:, :], inc=s_v)
            nf = vop("vector", "tensor_tensor", fT[:, :], ssT[:, :], deT[:, :],
                     ALU.mult, inc=s_v)
            ngat = dma("gpsimd", bcD[0:1, :], fT[:, :], s_qg, waits=[(s_v, nf)])
            nna4 = [0] * NT
            for t4 in range(NT):
                tsl = slice(t4 * TT, (t4 + 1) * TT)
                for c6 in range(6):
                    mm(ps[t4][:, :], memb[:, j * 6 + c6, i * 128:(i + 1) * 128],
                       U_q[:, c6, tsl], start=(c6 == 0), stop=(c6 == 5),
                       waits=[bank_free[t4], (s_v, n_newmem)] if c6 == 0 else ())
                nna4[t4] = tmark()
            nbcf = [0] * NT
            for t4 in range(NT):
                tsl = slice(t4 * TT, (t4 + 1) * TT)
                nbcf[t4] = bcast(bcD[0:1, tsl], bcD[:, tsl], 4 + t4 % 2,
                                 waits=[(s_qg, ngat)])
            ngm = 0
            for t4 in range(NT):
                tsl = slice(t4 * TT, (t4 + 1) * TT)
                nna = nna4[t4]
                vop("vector", "tensor_tensor", tmpq[:, :], ps[t4][:, :], bcD[:, tsl],
                    ALU.mult, waits=[(s_mm, nna), (s_v, nbcf[t4])], inc=s_v)
                nc1 = s_v.n
                bank_free[t4] = (s_v, nc1)
                ngm = vop("vector", "tensor_tensor", tmpq[:, :], tmpq[:, :],
                          gate_b[:, sl, tsl], ALU.mult, inc=s_v)
                vop("vector", "tensor_tensor", tmpq[:, :], tmpq[:, :], h_b[:, h, tsl],
                    ALU.add, inc=s_v)
                nxn = vop("vector", "scalar_tensor_tensor", xqT[:, tsl], tmpq[:, :],
                          col(cols, CO_GA + h), xqT[:, tsl], ALU.mult, ALU.add,
                          waits=[(s_xqs, nxq)], inc=s_v)
                dma("scalar", d_xnew[:, h, tsl], xqT[:, tsl], s_stq,
                    waits=[(s_v, nxn)])
            gate_free[sl] = ngm
            xq_slot_free[sl] = s_v.n
            xq_store_mark[sl] = s_stq.n

        issue_qw(0)
        q_stage_a(0)
        issue_qw(1)
        q_stage_a(1)
        issue_qroll(0, 0)
        issue_qroll(0, 1)
        for h in range(16):
            q_stage_b(h)
            if h + 1 < 16:
                issue_qroll(h + 1, 0)
                issue_qroll(h + 1, 1)
            if h + 2 < 16:
                q_stage_a(h + 2)
        n_xnew_st = s_stx.n
        n_xnew_st2 = s_sty.n

        if "xnew" in taps:
            dma("gpsimd", d_tap["xnew"][:, :, :], d_xnew[:, :, :], s_stg,
                waits=[(s_stx, n_xnew_st), (s_sty, n_xnew_st2)])

        barrier()

        ln_phase(d_xnew, CO_ONECB, CO_SB, "h2",
                 first_waits=[(s_stx, n_xnew_st), (s_sty, n_xnew_st2)])
        barrier()

        act = vb(0, 15, 2048)
        wgateb = vb(30720, 2, 16, 128)
        wupb = vb(34816, 2, 16, 128)
        wdownb = vb(38912, 2, 16, 128)
        silu_t = vb(43008, 512)
        xnin = vf(21760, 2, 512)
        ystg = fsc[:, 0:512]
        yin = fsc[:, 512:1536].rearrange("p (a b) -> p a b", a=2)

        n_wga = [0, 0, 0]
        silu_free = 0
        gu_slot_mark = [[0, 0], [0, 0]]
        gu_loads = {}

        def issue_gu_load(o):
            sl = (o % 2)
            ng = dma("gpsimd", wgateb[:, sl, :, :],
                     d_wgate[:, :, o * 128:(o + 1) * 128], s_wga2[sl],
                     waits=[(s_mm, gu_slot_mark[0][sl])])
            nu = dma("gpsimd", wupb[:, sl, :, :],
                     d_wup[:, :, o * 128:(o + 1) * 128], s_wup2[sl],
                     waits=[(s_mm, gu_slot_mark[1][sl])])
            gu_loads[o] = (ng, nu)

        for gi, (gbase, gn) in enumerate(FFN_GROUPS):
            for ol in range(gn):
                o = gbase + ol
                if o not in gu_loads:
                    issue_gu_load(o)
                if o + 1 < INTER // 128 and (o + 1) not in gu_loads:
                    issue_gu_load(o + 1)
                nwg_, nwu_ = gu_loads[o]
                sl2 = o % 2
                for t4 in range(NT):
                    tsl = slice(t4 * TT, (t4 + 1) * TT)
                    bgt = t4 % 2
                    bup = 2 + t4 % 2
                    for fc in range(NC16):
                        mm(ps[bgt][:, :], wgateb[:, sl2, fc, :], h_b[:, fc, tsl],
                           start=(fc == 0), stop=(fc == NC16 - 1),
                           waits=[(s_wga2[sl2], nwg_), bank_free[bgt]] if fc == 0 else ())
                    ngm = tmark()
                    if t4 == NT - 1:
                        n_wga[0] = ngm
                        gu_slot_mark[0][sl2] = ngm
                    for fc in range(NC16):
                        mm(ps[bup][:, :], wupb[:, sl2, fc, :], h_b[:, fc, tsl],
                           start=(fc == 0), stop=(fc == NC16 - 1),
                           waits=[(s_wup2[sl2], nwu_), bank_free[bup]] if fc == 0 else ())
                    num_ = tmark()
                    if t4 == NT - 1:
                        n_wga[1] = num_
                        gu_slot_mark[1][sl2] = num_
                        if o + 2 < INTER // 128 and (o + 2) not in gu_loads:
                            issue_gu_load(o + 2)
                    nsl = vop("scalar", "activation", silu_t, ps[bgt][:, :], AF.Silu,
                              waits=[(s_mm, ngm), (s_v, silu_free)], inc=s_sc,
                              kwargs={"scale": col(cols, CO_GS + o)})
                    bank_free[bgt] = (s_sc, nsl)
                    nac = vop("vector", "scalar_tensor_tensor", act[:, ol, tsl],
                              ps[bup][:, :], col(cols, CO_US + o), silu_t,
                              ALU.mult, ALU.mult,
                              waits=[(s_mm, num_), (s_sc, nsl)], inc=s_v)
                    silu_free = nac
                    bank_free[bup] = (s_v, nac)
            n_act = s_v.n
            dn_slot_mark = [0, 0]
            dn_loads = {}

            def issue_dn_load(mc2):
                sl3 = mc2 % 2
                dn_loads[mc2] = dma(
                    "gpsimd", wdownb[:, sl3, 0:gn, :],
                    d_wdown[:, gbase:gbase + gn, mc2 * 128:(mc2 + 1) * 128],
                    s_wdn2[sl3], waits=[(s_mm, dn_slot_mark[sl3])])

            yin_mark = [0, 0]
            xnin_mark = [0, 0]
            yin_loads = {}

            def issue_yin(ti):
                mc2, t42 = ti // 4, ti % 4
                sl3 = ti % 2
                yin_loads[ti] = dma(
                    "scalar", yin[:, sl3, :],
                    d_y[:, mc2, t42 * TT:(t42 + 1) * TT], s_yi2[sl3],
                    waits=[(s_stg, n_ydone), (s_v, yin_mark[sl3])])

            n_ydone = s_stg.n if gi == 2 else 0
            for mc in range(NC16):
                if mc not in dn_loads:
                    issue_dn_load(mc)
                if mc + 1 < NC16 and (mc + 1) not in dn_loads:
                    issue_dn_load(mc + 1)
                nwd = dn_loads[mc]
                if gi == 2 and mc == 0:
                    issue_yin(0)
                    issue_yin(1)
                for t4 in range(NT):
                    tsl = slice(t4 * TT, (t4 + 1) * TT)
                    b = 4 + t4 % 2
                    for il in range(gn):
                        mm(ps[b][:, :], wdownb[:, mc % 2, il, :], act[:, il, tsl],
                           start=(il == 0), stop=(il == gn - 1),
                           waits=[(s_wdn2[mc % 2], nwd), (s_v, n_act), bank_free[b]] if il == 0 else ())
                    ndm = tmark()
                    if t4 == NT - 1:
                        n_wga[2] = ndm
                        dn_slot_mark[mc % 2] = ndm
                        if mc + 2 < NC16 and (mc + 2) not in dn_loads:
                            issue_dn_load(mc + 2)
                    if gi == 0:
                        nev = vop("vector", "tensor_copy", ystg, ps[b][:, :],
                                  waits=[(s_mm, ndm), (s_stg, s_stg.n)], inc=s_v)
                        bank_free[b] = (s_v, nev)
                        dma("gpsimd", d_y[:, mc, tsl], ystg, s_stg, waits=[(s_v, nev)])
                    elif gi == 1:
                        nev = vop("vector", "tensor_copy", ystg, ps[b][:, :],
                                  waits=[(s_mm, ndm), (s_stg, s_stg.n)], inc=s_v)
                        bank_free[b] = (s_v, nev)
                        dma("gpsimd", d_y[:, mc, tsl], ystg, s_stg,
                            waits=[(s_v, nev)], accum=True)
                    else:
                        nxn_ = dma("scalar", xnin[:, t4 % 2, :],
                                   d_xnew[:, mc, tsl], s_xi2[t4 % 2],
                                   waits=[(s_v, xnin_mark[t4 % 2]),
                                          (s_stx, n_xnew_st),
                                          (s_sty, n_xnew_st2)])
                        ti = mc * 4 + t4
                        nyi = yin_loads[ti]
                        nev = vop("vector", "tensor_tensor", ystg, ps[b][:, :],
                                  yin[:, ti % 2, :], ALU.add,
                                  waits=[(s_mm, ndm), (s_yi2[ti % 2], nyi),
                                         (s_stg, s_stg.n)], inc=s_v)
                        yin_mark[ti % 2] = nev
                        if ti + 2 < NC16 * 4:
                            issue_yin(ti + 2)
                        bank_free[b] = (s_v, nev)
                        nout = vop("vector", "scalar_tensor_tensor", ystg,
                                   ystg, col(cols, CO_GBDS + mc), xnin[:, t4 % 2, :],
                                   ALU.mult, ALU.add,
                                   waits=[(s_xi2[t4 % 2], nxn_)], inc=s_v)
                        xnin_mark[t4 % 2] = nout
                        dma("gpsimd", d_out[:, mc, tsl], ystg, s_stg,
                            waits=[(s_v, nout)])

        fin = s_stg.n
        emit("gpsimd", lambda e, t=fin: e.wait_ge(s_stg.h, t))
        barrier()

        with nc.Block() as block:
            @block.sync
            def _(e):
                for f in prog["sync"]:
                    f(e)

            @block.tensor
            def _(e):
                for f in prog["tensor"]:
                    f(e)

            @block.vector
            def _(e):
                for f in prog["vector"]:
                    f(e)

            @block.scalar
            def _(e):
                for f in prog["scalar"]:
                    f(e)

            @block.gpsimd
            def _(e):
                for f in prog["gpsimd"]:
                    f(e)

    return nc


def _layout_inputs(inputs):
    g = lambda k: np.asarray(inputs[k])
    x = g("L0_attn_hidden_states")[0]
    per_core = []
    shared = {}

    def chunk16(w):
        return np.ascontiguousarray(w.reshape(NC16, P, w.shape[1]).transpose(1, 0, 2))

    shared["wqt_l"] = chunk16(np.ascontiguousarray(g("L0_attn_Wq").T))
    shared["wkt_l"] = chunk16(np.ascontiguousarray(g("L0_attn_Wk").T))
    shared["wvt_l"] = chunk16(np.ascontiguousarray(g("L0_attn_Wv").T))
    shared["wgt_l"] = chunk16(np.ascontiguousarray(g("L0_attn_Wg").T))
    shared["wmbt_l"] = chunk16(np.ascontiguousarray(g("L0_attn_Wmb").T))
    shared["wgatet_l"] = chunk16(np.ascontiguousarray(g("L0_ffn_gate_w").T))
    shared["wupt_l"] = chunk16(np.ascontiguousarray(g("L0_ffn_up_w").T))
    wd = np.ascontiguousarray(g("L0_ffn_down_w").T)
    shared["wdownt_l"] = np.ascontiguousarray(wd.reshape(NI44, P, 2048).transpose(1, 0, 2))
    shared["bq_c"] = np.ascontiguousarray(g("L0_attn_bq").reshape(16, P).T)
    shared["bk_c"] = np.ascontiguousarray(g("L0_attn_bk").reshape(4, P).T)
    shared["bv_c"] = np.ascontiguousarray(g("L0_attn_bv").reshape(4, P).T)
    shared["bg_c"] = np.ascontiguousarray(g("L0_attn_bg").reshape(16, P).T)
    shared["bmb_r"] = np.ascontiguousarray(g("L0_attn_bmb").reshape(1, 2048))
    shared["mem_l"] = np.ascontiguousarray(
        g("L0_attn_memory").reshape(4, 4, 6, P, D).transpose(0, 2, 3, 1, 4).reshape(4, 6, P, 512))
    shared["nrm_l"] = np.ascontiguousarray(
        g("L0_attn_norm").reshape(16, 6, P).transpose(2, 1, 0).reshape(P, 96))
    shared["gs_c"] = np.ascontiguousarray(g("L0_ffn_gate_s").reshape(NI44, P).T)
    shared["us_c"] = np.ascontiguousarray(g("L0_ffn_up_s").reshape(NI44, P).T)
    shared["ds_c"] = np.ascontiguousarray(g("L0_ffn_down_s").reshape(16, P).T)
    shared["c_c"] = np.ascontiguousarray(g("L0_c").reshape(16, P).T)
    shared["ident_b"] = np.eye(P, dtype=ml_dtypes.bfloat16)
    shared["ident_f"] = np.eye(P, dtype=np.float32)
    shared["ones_col_b"] = np.ones((P, 1), dtype=ml_dtypes.bfloat16)
    shared["ones_row_b"] = np.ones((1, P), dtype=ml_dtypes.bfloat16)
    shared["ones_row_f"] = np.ones((1, P), dtype=np.float32)

    wmod = g("L0_wmod")
    for core in range(8):
        xs = np.ascontiguousarray(x[core * T:(core + 1) * T, :].T)
        m = dict(shared)
        m["x_l"] = np.ascontiguousarray(xs.reshape(NC16, P, T).transpose(1, 0, 2))
        m["wmod_sh_l"] = chunk16(
            np.ascontiguousarray(wmod[:, core * 1536:(core + 1) * 1536]))
        per_core.append(m)
    return per_core


def kernel(**inputs):
    in_maps = _layout_inputs(inputs)
    nc = build(taps=TAPS)
    res = run_bass_kernel_spmd(nc, in_maps, core_ids=list(range(8)))
    outs = []
    for r in res.results:
        o = r["out_l"]
        outs.append(o.transpose(1, 0, 2).reshape(2048, T).T)
    full = np.concatenate(outs, axis=0)[None]
    if TAPS:
        kernel.last_results = res
    return full.astype(np.float32)
